# version 14
# baseline (speedup 1.0000x reference)
import numpy as np
from contextlib import ExitStack

import ml_dtypes
import concourse.bass as bass
import concourse.mybir as mybir
from concourse.bass_utils import run_bass_kernel_spmd

F32 = mybir.dt.float32
BF16 = mybir.dt.bfloat16
AF = mybir.ActivationFunctionType
ALU = mybir.AluOpType
AX = mybir.AxisListType

NCORES = 8
S = 16384
D = 1024
T = S // NCORES
E = 8
FF = 4096
DC = D // 128
TT = T // 128
NCH = S // 1024
EPS = 1e-5
SCALE = 1.0 / 32.0
NEG_LN32 = float(-np.log(32.0))


class Ctr:
    def __init__(self, es, nc, name):
        self.h = es.enter_context(nc.semaphore(name))
        self.n = 0

    def inc(self, ins, k=1):
        ins.then_inc(self.h, k)
        self.n += k
        return self.n


def _ln_group(blk, srcs, xst, sq_scr, stat, xn32, xn_bf, want32,
              slots, ld, sv, ss, eps_ap, wait_loads=(), wait_xn=()):
    n = len(srcs)

    @blk.sync
    def _(sync):
        for (h, v) in wait_loads:
            if v > 0:
                sync.wait_ge(h, v)
        for i in range(n):
            sync.dma_start(out=xst[:, slots[i] * D:(slots[i] + 1) * D],
                           in_=srcs[i]).then_inc(ld.h, 16)
            ld.n += 16
    ld_grp = ld.n

    @blk.vector
    def _(v):
        v.wait_ge(ld.h, ld_grp)
        for i in range(n):
            c = slots[i] * 16
            v.bn_stats(stat[:, c:c + 6], xst[:, slots[i] * D:slots[i] * D + 512])
            v.bn_stats(stat[:, c + 6:c + 12], xst[:, slots[i] * D + 512:(slots[i] + 1) * D])
        for i in range(n):
            c = slots[i] * 16
            v.bn_aggr(stat[:, c + 12:c + 14], stat[:, c:c + 12])
        ins = None
        for i in range(n):
            c = slots[i] * 16
            ins = v.tensor_scalar(stat[:, c + 15:c + 16], stat[:, c + 12:c + 13], -1.0, None, ALU.mult)
        sv.inc(ins)
    sv_stats = sv.n

    @blk.scalar
    def _(sc):
        sc.wait_ge(sv.h, sv_stats)
        for i in range(n):
            c = slots[i] * 16
            sc.activation(stat[:, c:c + 1], stat[:, c + 13:c + 14], AF.Ln, bias=eps_ap, scale=1.0)
        ins = None
        for i in range(n):
            c = slots[i] * 16
            ins = sc.activation(stat[:, c + 14:c + 15], stat[:, c:c + 1], AF.Exp, scale=-0.5)
        ss.inc(ins)
    ss_inv = ss.n

    xn_done = {}

    @blk.vector
    def _(v):
        v.wait_ge(ss.h, ss_inv)
        for (h, val) in wait_xn:
            if val > 0:
                v.wait_ge(h, val)
        for i in range(n):
            c = slots[i] * 16
            sl = slots[i]
            if want32:
                v.tensor_scalar(xn32[:, sl * D:(sl + 1) * D],
                                xst[:, sl * D:(sl + 1) * D],
                                stat[:, c + 15:c + 16], stat[:, c + 14:c + 15],
                                ALU.add, ALU.mult)
                ins = v.tensor_copy(xn_bf[:, sl * D:(sl + 1) * D],
                                    xn32[:, sl * D:(sl + 1) * D])
            else:
                ins = v.tensor_scalar(xn_bf[:, sl * D:(sl + 1) * D],
                                      xst[:, sl * D:(sl + 1) * D],
                                      stat[:, c + 15:c + 16], stat[:, c + 14:c + 15],
                                      ALU.add, ALU.mult)
            sv.inc(ins)
            xn_done[i] = sv.n
    return xn_done


def build_nc():
    nc = bass.Bass("TRN2", target_bir_lowering=False, enable_partition_id=True)
    _pid = nc.scalar.partition_id()
    kv_off = nc.scalar.scalar_reg_alu(ALU.mult, nc.scalar.scalar_reg_alu(ALU.mod, _pid, 2), S // 2)
    _ET = mybir.EngineType
    _par_regs = nc.alloc_registers("parreg", engines=[_ET.Activation])
    nc.regs_mov(_par_regs, _pid)
    nc.scalar.scalar_reg_alu(ALU.mod, _par_regs.handles[0], 2)
    _par = _par_regs.handles[0]

    x_own = nc.dram_tensor("x_own", [T, D], F32, kind="ExternalInput")
    x_kv = nc.dram_tensor("x_kv", [S // 2, D], F32, kind="ExternalInput")
    wq = nc.dram_tensor("wq", [D, D], F32, kind="ExternalInput")
    wk = nc.dram_tensor("wk", [D, D], F32, kind="ExternalInput")
    wv = nc.dram_tensor("wv", [D, D], F32, kind="ExternalInput")
    gate_w = nc.dram_tensor("gate_w", [D, E], F32, kind="ExternalInput")
    w1 = nc.dram_tensor("w1", [E, D, FF], F32, kind="ExternalInput")
    b1 = nc.dram_tensor("b1", [E, FF], F32, kind="ExternalInput")
    w2 = nc.dram_tensor("w2", [E, FF, D], F32, kind="ExternalInput")
    b2 = nc.dram_tensor("b2", [E, D], F32, kind="ExternalInput")
    out = nc.dram_tensor("out", [T, D], F32, kind="ExternalOutput")

    kT_sh = nc.dram_tensor("kT_sh", [DC, 128, S], BF16, addr_space="Shared")
    v_sh = nc.dram_tensor("v_sh", [S, D], BF16, addr_space="Shared")
    km_in = nc.dram_tensor("km_in", [1, 8], F32)
    km_out = nc.dram_tensor("km_out", [1, 8], F32)
    xnT_scr = nc.dram_tensor("xnT_scr", [DC, 128, T], BF16)
    xn_scr = nc.dram_tensor("xn_scr", [TT, 128, D], BF16)

    id_bf_c = nc.inline_tensor(np.eye(128, dtype=ml_dtypes.bfloat16), name="id_bf_c")
    id_32_c = nc.inline_tensor(np.eye(128, dtype=np.float32), name="id_32_c")

    es = ExitStack()
    with es:
        O_sb = es.enter_context(nc.sbuf_tensor("O_sb", [128, TT * 1026], F32))
        comb = es.enter_context(nc.sbuf_tensor("comb", [128, TT * E], F32))
        sel_full = es.enter_context(nc.sbuf_tensor("sel_full", [128, TT * E], F32))
        negB = es.enter_context(nc.sbuf_tensor("negB", [128, 1], F32))
        id_bf = es.enter_context(nc.sbuf_tensor("id_bf", [128, 128], BF16))
        id_32 = es.enter_context(nc.sbuf_tensor("id_32", [128, 128], F32))
        ones2_bf = es.enter_context(nc.sbuf_tensor("ones2_bf", [128, 2], BF16))
        onesk_bf = es.enter_context(nc.sbuf_tensor("onesk_bf", [128, 1], BF16))
        ones1_bf = es.enter_context(nc.sbuf_tensor("ones1_bf", [1, 128], BF16))
        kmax2 = es.enter_context(nc.sbuf_tensor("kmax2", [1, 2], F32))
        qmax2 = es.enter_context(nc.sbuf_tensor("qmax2", [1, 2], F32))
        bstat = es.enter_context(nc.sbuf_tensor("bstat", [1, 8], F32))
        nb_bf = es.enter_context(nc.sbuf_tensor("nb_bf", [1, 1], BF16))
        eps_sb = es.enter_context(nc.sbuf_tensor("eps_sb", [128, 1], F32))
        nl32_sb = es.enter_context(nc.sbuf_tensor("nl32_sb", [1, 1], F32))
        qt_guard = nc.sbuf_tensor("QT", [128, DC * T], BF16)
        QT = qt_guard.__enter__()

        ld = Ctr(es, nc, "ld")
        ldg = Ctr(es, nc, "ldg")
        wr = Ctr(es, nc, "wr")
        sv = Ctr(es, nc, "sv")
        ss = Ctr(es, nc, "ss")
        st = Ctr(es, nc, "st")
        so = Ctr(es, nc, "so")
        sg = Ctr(es, nc, "sg")
        cc = Ctr(es, nc, "cc")

        g1 = ExitStack()
        with g1:
            xst = g1.enter_context(nc.sbuf_tensor("xst", [128, 4 * D], F32))
            sq_scr = g1.enter_context(nc.sbuf_tensor("sq_scr", [128, D], F32))
            stat = g1.enter_context(nc.sbuf_tensor("stat", [128, 64], F32))
            xn32 = g1.enter_context(nc.sbuf_tensor("xn32", [128, 4 * D], F32))
            xn_bf = g1.enter_context(nc.sbuf_tensor("xn_bf", [128, 4 * D], BF16))
            wq_bf = g1.enter_context(nc.sbuf_tensor("wq_bf", [128, DC * D], BF16))
            gate32 = g1.enter_context(nc.sbuf_tensor("gate32", [128, DC * E], F32))
            xnT_own = g1.enter_context(nc.sbuf_tensor("xnT_own", [128, DC * T], BF16))
            xnT32_t = g1.enter_context(nc.sbuf_tensor("xnT32_t", [128, D], F32))
            glog = g1.enter_context(nc.sbuf_tensor("glog", [128, 4 * E], F32))
            mx2 = g1.enter_context(nc.sbuf_tensor("mx2", [128, 4 * E], F32))
            cstat = g1.enter_context(nc.sbuf_tensor("cstat", [128, 16], F32))
            expl = g1.enter_context(nc.sbuf_tensor("expl", [128, 4 * E], F32))
            maskt = g1.enter_context(nc.sbuf_tensor("maskt", [128, 4 * E], F32))
            qsq_bf = g1.enter_context(nc.sbuf_tensor("qsq_bf", [128, T], BF16))

            p1 = ExitStack()
            with p1:
                trbf_ps = [p1.enter_context(nc.psum_tensor(f"trbf{i}", [128, D], BF16)) for i in range(2)]
                tr32_ps = p1.enter_context(nc.psum_tensor("tr32", [128, D], F32))
                gate_ps = p1.enter_context(nc.psum_tensor("gate_ps", [128, E], F32))
                qt_ps = [p1.enter_context(nc.psum_tensor(f"qt{i}", [128, 512], F32)) for i in range(2)]

                blk = nc.Block("gen_own"); blk.__enter__()

                @blk.sync
                def _(sync):
                    sync.dma_start(out=id_bf[:], in_=id_bf_c[:]).then_inc(ld.h, 16)
                    sync.dma_start(out=id_32[:], in_=id_32_c[:]).then_inc(ld.h, 16)
                    sync.dma_start(
                        out=gate32[:].rearrange("p (j e) -> p j e", j=DC),
                        in_=gate_w[:].rearrange("(j p) e -> p j e", p=128),
                    ).then_inc(ld.h, 16)
                    ld.n += 48
                ld_const = ld.n

                @blk.vector
                def _(v):
                    v.memset(ones2_bf[:], 1.0)
                    v.memset(onesk_bf[:], 1.0)
                    v.memset(ones1_bf[:], 1.0)
                    v.memset(kmax2[:], 0.0)
                    v.memset(eps_sb[:], EPS)
                    v.memset(nl32_sb[:], NEG_LN32)
                    ins = None
                    for gt in range(TT):
                        ins = v.memset(O_sb[:, gt * 1026:(gt + 1) * 1026], 0.0)
                    sv.inc(ins)

                @blk.gpsimd
                def _(g):
                    g.dma_start(
                        out=wq_bf[:].rearrange("p (j d) -> p j d", j=DC),
                        in_=wq[:].rearrange("(j p) d -> p j d", p=128),
                    ).then_inc(ldg.h, 16)
                    ldg.n += 16
                ldg_wq = ldg.n

                grp_tr_done = {}
                grp_xn_done = {}
                grp_xnwr_done = {}
                evac_done = {}
                gate_done_d = {}
                glog_done = {}

                for grp in range(4):
                    tiles = [grp * 4 + i for i in range(4)]
                    xn_done = _ln_group(
                        blk,
                        [x_own[g * 128:(g + 1) * 128, :] for g in tiles],
                        xst, sq_scr, stat, xn32, xn_bf, True,
                        [i for i in range(4)],
                        ld, sv, ss, eps_sb[:],
                        wait_loads=[(sv.h, grp_xn_done.get(grp - 1, 0))],
                        wait_xn=[(st.h, grp_tr_done.get(grp - 1, 0)),
                                 (wr.h, grp_xnwr_done.get(grp - 1, 0))],
                    )
                    grp_xn_done[grp] = sv.n

                    @blk.scalar
                    def _(sc, tiles=tiles, xn_done=xn_done):
                        for i, g in enumerate(tiles):
                            sc.wait_ge(sv.h, xn_done[i])
                            ins = sc.dma_start(out=xn_scr[g, :, :],
                                               in_=xn_bf[:, (g % 4) * D:(g % 4 + 1) * D])
                            wr.inc(ins, 16)
                    grp_xnwr_done[grp] = wr.n

                    for i, g in enumerate(tiles):
                        @blk.tensor
                        def _(te, g=g, i=i, xn_done=xn_done):
                            te.wait_ge(sv.h, xn_done[i])
                            te.wait_ge(sv.h, evac_done.get(g - 2, 0))
                            te.wait_ge(sv.h, evac_done.get(g - 1, 0))
                            for j in range(DC):
                                te.transpose(trbf_ps[g % 2][:, j * 128:(j + 1) * 128],
                                             xn_bf[:, (g % 4) * D + j * 128:(g % 4) * D + (j + 1) * 128],
                                             id_bf[:])
                            ins = None
                            for j in range(DC):
                                ins = te.transpose(tr32_ps[:, j * 128:(j + 1) * 128],
                                                   xn32[:, (g % 4) * D + j * 128:(g % 4) * D + (j + 1) * 128],
                                                   id_32[:])
                            st.inc(ins)
                        tr_done = st.n

                        @blk.vector
                        def _(v, g=g, tr_done=tr_done):
                            v.wait_ge(st.h, tr_done)
                            v.wait_ge(st.h, gate_done_d.get(g - 1, 0))
                            dst = xnT_own[:].rearrange("p (j t) -> p j t", j=DC)[:, :, g * 128:(g + 1) * 128]
                            src = trbf_ps[g % 2][:].rearrange("p (j c) -> p j c", j=DC)
                            v.tensor_copy(dst, src)
                            ins = v.tensor_copy(xnT32_t[:], tr32_ps[:])
                            sv.inc(ins)
                            evac_done[g] = sv.n

                        @blk.tensor
                        def _(te, g=g):
                            te.wait_ge(sv.h, evac_done[g])
                            te.wait_ge(ld.h, ld_const)
                            te.wait_ge(sv.h, glog_done.get(g - 1, 0))
                            ins = None
                            for j in range(DC):
                                ins = te.matmul(gate_ps[:], xnT32_t[:, j * 128:(j + 1) * 128],
                                                gate32[:, j * E:(j + 1) * E],
                                                start=(j == 0), stop=(j == DC - 1))
                            st.inc(ins)
                            gate_done_d[g] = st.n

                        @blk.vector
                        def _(v, g=g):
                            v.wait_ge(st.h, gate_done_d[g])
                            ins = v.tensor_copy(glog[:, (g % 4) * E:(g % 4 + 1) * E], gate_ps[:])
                            sv.inc(ins)
                            glog_done[g] = sv.n
                    grp_tr_done[grp] = st.n

                    @blk.vector
                    def _(v, tiles=tiles):
                        v.wait_ge(sv.h, glog_done[tiles[-1]])
                        for g in tiles:
                            s4 = g % 4
                            v.max(out=mx2[:, s4 * E:(s4 + 1) * E], in_=glog[:, s4 * E:(s4 + 1) * E])
                        ins = None
                        for g in tiles:
                            s4 = g % 4
                            ins = v.tensor_scalar(cstat[:, s4 * 4:s4 * 4 + 1], mx2[:, s4 * E:s4 * E + 1],
                                                  -1.0, None, ALU.mult)
                        sv.inc(ins)
                    sv_negm1 = sv.n

                    @blk.scalar
                    def _(sc, tiles=tiles, sv_negm1=sv_negm1):
                        sc.wait_ge(sv.h, sv_negm1)
                        for g in tiles:
                            s4 = g % 4
                            sc.activation(cstat[:, s4 * 4 + 1:s4 * 4 + 2], mx2[:, s4 * E + 1:s4 * E + 2],
                                          AF.Exp, bias=cstat[:, s4 * 4:s4 * 4 + 1], scale=1.0)
                        ins = None
                        for g in tiles:
                            s4 = g % 4
                            ins = sc.activation(expl[:, s4 * E:(s4 + 1) * E], glog[:, s4 * E:(s4 + 1) * E],
                                                AF.Exp, bias=cstat[:, s4 * 4:s4 * 4 + 1], scale=1.0)
                        ss.inc(ins)
                    ss_cexp = ss.n

                    @blk.vector
                    def _(v, tiles=tiles, ss_cexp=ss_cexp):
                        v.wait_ge(ss.h, ss_cexp)
                        for g in tiles:
                            s4 = g % 4
                            v.tensor_scalar(cstat[:, s4 * 4 + 2:s4 * 4 + 3], cstat[:, s4 * 4 + 1:s4 * 4 + 2],
                                            1.0, None, ALU.add)
                        for g in tiles:
                            s4 = g % 4
                            v.reciprocal(cstat[:, s4 * 4 + 3:s4 * 4 + 4], cstat[:, s4 * 4 + 2:s4 * 4 + 3])
                        for g in tiles:
                            s4 = g % 4
                            v.tensor_scalar(maskt[:, s4 * E:(s4 + 1) * E], glog[:, s4 * E:(s4 + 1) * E],
                                            mx2[:, s4 * E + 1:s4 * E + 2], None, ALU.is_ge)
                        for g in tiles:
                            s4 = g % 4
                            v.tensor_copy(sel_full[:, g * E:(g + 1) * E], maskt[:, s4 * E:(s4 + 1) * E])
                        for g in tiles:
                            s4 = g % 4
                            v.tensor_tensor(comb[:, g * E:(g + 1) * E], expl[:, s4 * E:(s4 + 1) * E],
                                            maskt[:, s4 * E:(s4 + 1) * E], ALU.mult)
                        ins = None
                        for g in tiles:
                            s4 = g % 4
                            ins = v.tensor_scalar(comb[:, g * E:(g + 1) * E], comb[:, g * E:(g + 1) * E],
                                                  cstat[:, s4 * 4 + 3:s4 * 4 + 4], None, ALU.mult)
                        sv.inc(ins)

                qt_mm_done = {}
                NQT = T // 512

                @blk.tensor
                def _(te):
                    te.wait_ge(sv.h, evac_done[TT - 1])
                    te.wait_ge(ldg.h, ldg_wq)
                    ss_base = ss.n
                    for j in range(DC):
                        for n in range(NQT):
                            idx = j * NQT + n
                            if idx >= 2:
                                te.wait_ge(ss.h, ss_base + idx - 1)
                            ins = None
                            for dd in range(DC):
                                ins = te.matmul(qt_ps[idx % 2][:],
                                                wq_bf[:, dd * D + j * 128:dd * D + (j + 1) * 128],
                                                xnT_own[:, dd * T + n * 512:dd * T + (n + 1) * 512],
                                                start=(dd == 0), stop=(dd == DC - 1))
                            st.inc(ins)
                            qt_mm_done[idx] = st.n

                @blk.scalar
                def _(sc):
                    for j in range(DC):
                        for n in range(NQT):
                            idx = j * NQT + n
                            sc.wait_ge(st.h, qt_mm_done[idx])
                            ins = sc.activation(QT[:, j * T + n * 512:j * T + (n + 1) * 512],
                                                qt_ps[idx % 2][:], AF.Copy)
                            ss.inc(ins)
                ss_qt_all = ss.n

                @blk.scalar
                def _(sc):
                    for j in range(DC):
                        ins = sc.dma_start(out=xnT_scr[j, :, :], in_=xnT_own[:, j * T:(j + 1) * T])
                        wr.inc(ins, 16)
                wr_xnT = wr.n

            p2 = ExitStack()
            with p2:
                qn2_ps = p2.enter_context(nc.psum_tensor("qn2", [1, T], F32))
                qsq_done = {}
                qn_mm_done = {}
                for j in range(DC):
                    @blk.scalar
                    def _(sc, j=j):
                        if j == 0:
                            sc.wait_ge(ss.h, ss_qt_all)
                        else:
                            sc.wait_ge(st.h, qn_mm_done.get(j - 1, 0))
                        ins = sc.activation(qsq_bf[:], QT[:, j * T:(j + 1) * T], AF.Square)
                        ss.inc(ins)
                        qsq_done[j] = ss.n

                    @blk.tensor
                    def _(te, j=j):
                        te.wait_ge(ss.h, qsq_done[j])
                        ins = None
                        for n in range(T // 512):
                            ins = te.matmul(qn2_ps[:, n * 512:(n + 1) * 512],
                                            onesk_bf[:], qsq_bf[:, n * 512:(n + 1) * 512],
                                            start=(j == 0), stop=(j == DC - 1))
                        st.inc(ins)
                        qn_mm_done[j] = st.n

                @blk.vector
                def _(v):
                    v.wait_ge(st.h, qn_mm_done[DC - 1])
                    ins = v.reduce_max(qmax2[:, 0:1], qn2_ps[:], axis=AX.X)
                    sv.inc(ins)
            blk.__exit__(None, None, None)
            sv_qmax = sv.n

        g2 = ExitStack()
        with g2:
            xst = g2.enter_context(nc.sbuf_tensor("xst2", [128, 4 * D], F32))
            sq_scr = g2.enter_context(nc.sbuf_tensor("sq_scr2", [128, D], F32))
            stat = g2.enter_context(nc.sbuf_tensor("stat2", [128, 64], F32))
            xn_bf = g2.enter_context(nc.sbuf_tensor("xn_bf2", [128, 4 * D], BF16))
            wk_bf = g2.enter_context(nc.sbuf_tensor("wk_bf", [128, DC * D], BF16))
            wv_bf = g2.enter_context(nc.sbuf_tensor("wv_bf", [128, DC * D], BF16))
            xnT_kv = g2.enter_context(nc.sbuf_tensor("xnT_kv", [128, 2 * DC * 1024], BF16))
            ktb = g2.enter_context(nc.sbuf_tensor("ktb", [128, 2 * 1024], BF16))
            vtb = g2.enter_context(nc.sbuf_tensor("vtb", [128, 2 * 1024], BF16))
            ksq_bf = g2.enter_context(nc.sbuf_tensor("ksq_bf", [128, 2 * 1024], BF16))
            km_sb8 = g2.enter_context(nc.sbuf_tensor("km_sb8", [1, 8], F32))

            p3 = ExitStack()
            with p3:
                trbf_ps = [p3.enter_context(nc.psum_tensor(f"trbf2_{i}", [128, D], BF16)) for i in range(2)]
                kt_ps = [p3.enter_context(nc.psum_tensor(f"ktp{i}", [128, 512], F32)) for i in range(2)]
                v_ps = [p3.enter_context(nc.psum_tensor(f"vp{i}", [128, 512], F32)) for i in range(2)]
                kn2_ps = p3.enter_context(nc.psum_tensor("kn2", [1, 1024], F32))

                blk = nc.Block("gen_full"); blk.__enter__()

                @blk.gpsimd
                def _(g):
                    g.dma_start(
                        out=wk_bf[:].rearrange("p (j d) -> p j d", j=DC),
                        in_=wk[:].rearrange("(j p) d -> p j d", p=128),
                    ).then_inc(ldg.h, 16)
                    g.dma_start(
                        out=wv_bf[:].rearrange("p (j d) -> p j d", j=DC),
                        in_=wv[:].rearrange("(j p) d -> p j d", p=128),
                    ).then_inc(ldg.h, 16)
                    ldg.n += 32
                ldg_wkv = ldg.n

                ch_mm_done = {}
                ch_kmax_done = {}
                trevac_done = {}
                prev_xn_sv = [0]
                prev_tr_st = [0]
                NCHH = NCH // 2

                for ch in range(NCHH):
                    base = (ch % 2) * DC * 1024
                    for grp in range(2):
                        tiles = [ch * 8 + grp * 4 + i for i in range(4)]
                        xn_done = _ln_group(
                            blk,
                            [x_kv[g * 128:(g + 1) * 128, :] for g in tiles],
                            xst, sq_scr, stat, None, xn_bf, False,
                            [i for i in range(4)],
                            ld, sv, ss, eps_sb[:],
                            wait_loads=[(sv.h, prev_xn_sv[0])],
                            wait_xn=[(st.h, prev_tr_st[0])],
                        )
                        prev_xn_sv[0] = sv.n
                        for i in range(4):
                            tl = grp * 4 + i
                            @blk.tensor
                            def _(te, tl=tl, i=i, ch=ch, xn_done=xn_done, base=base):
                                te.wait_ge(sv.h, xn_done[i])
                                if tl >= 2:
                                    te.wait_ge(sv.h, trevac_done.get((ch, tl - 2), 0))
                                else:
                                    te.wait_ge(sv.h, trevac_done.get((ch - 1, 6 + tl), 0))
                                ins = None
                                for j in range(DC):
                                    ins = te.transpose(trbf_ps[tl % 2][:, j * 128:(j + 1) * 128],
                                                       xn_bf[:, i * D + j * 128:i * D + (j + 1) * 128],
                                                       id_bf[:])
                                st.inc(ins)
                            tr_done = st.n

                            @blk.vector
                            def _(v, tl=tl, tr_done=tr_done, ch=ch, base=base):
                                v.wait_ge(st.h, tr_done)
                                v.wait_ge(st.h, ch_mm_done.get(ch - 2, 0))
                                dst = xnT_kv[:, base:base + DC * 1024].rearrange(
                                    "p (j t) -> p j t", j=DC)[:, :, tl * 128:(tl + 1) * 128]
                                src = trbf_ps[tl % 2][:].rearrange("p (j c) -> p j c", j=DC)
                                ins = v.tensor_copy(dst, src)
                                sv.inc(ins)
                                trevac_done[(ch, tl)] = sv.n
                        prev_tr_st[0] = st.n
                    sv_evac_ch = sv.n

                    kt_mm_done = {}
                    kt_evac_actual = {}
                    kt_wr_done = {}
                    ksq_done = {}

                    @blk.tensor
                    def _(te, ch=ch, base=base, sv_evac_ch=sv_evac_ch):
                        te.wait_ge(sv.h, sv_evac_ch)
                        if ch == 0:
                            te.wait_ge(ldg.h, ldg_wkv)
                        ss_base = ss.n
                        for j in range(DC):
                            for n in range(2):
                                idx = j * 2 + n
                                if idx >= 2:
                                    te.wait_ge(ss.h, ss_base + idx - 1)
                                ins = None
                                for dd in range(DC):
                                    ins = te.matmul(kt_ps[idx % 2][:],
                                                    wk_bf[:, dd * D + j * 128:dd * D + (j + 1) * 128],
                                                    xnT_kv[:, base + dd * 1024 + n * 512:base + dd * 1024 + (n + 1) * 512],
                                                    start=(dd == 0), stop=(dd == DC - 1))
                                st.inc(ins)
                                kt_mm_done[idx] = st.n

                    @blk.scalar
                    def _(sc, ch=ch):
                        for j in range(DC):
                            for n in range(2):
                                idx = j * 2 + n
                                sc.wait_ge(st.h, kt_mm_done[idx])
                                if j >= 2:
                                    sc.wait_ge(wr.h, kt_wr_done.get(j - 2, 0))
                                    sc.wait_ge(sv.h, ksq_done.get(j - 2, 0))
                                ins = sc.activation(
                                    ktb[:, (j % 2) * 1024 + n * 512:(j % 2) * 1024 + (n + 1) * 512],
                                    kt_ps[idx % 2][:], AF.Copy)
                                ss.inc(ins)
                            kt_evac_actual[j] = ss.n
                            for p in range(2):
                                with sc.If_eq(_par, p):
                                    sc.dma_start(
                                        out=kT_sh[j, :, p * (S // 2) + ch * 1024:p * (S // 2) + (ch + 1) * 1024],
                                        in_=ktb[:, (j % 2) * 1024:(j % 2 + 1) * 1024],
                                    ).then_inc(wr.h, 16)
                            wr.n += 16
                            kt_wr_done[j] = wr.n

                    @blk.vector
                    def _(v, ch=ch):
                        for j in range(DC):
                            v.wait_ge(ss.h, kt_evac_actual[j])
                            ins = v.tensor_tensor(ksq_bf[:, (j % 2) * 1024:(j % 2 + 1) * 1024],
                                                  ktb[:, (j % 2) * 1024:(j % 2 + 1) * 1024],
                                                  ktb[:, (j % 2) * 1024:(j % 2 + 1) * 1024], ALU.mult)
                            sv.inc(ins)
                            ksq_done[j] = sv.n

                    @blk.tensor
                    def _(te, ch=ch):
                        te.wait_ge(sv.h, ch_kmax_done.get(ch - 1, 0))
                        ins = None
                        for j in range(DC):
                            te.wait_ge(sv.h, ksq_done[j])
                            for n in range(2):
                                ins = te.matmul(kn2_ps[:, n * 512:(n + 1) * 512],
                                                onesk_bf[:],
                                                ksq_bf[:, (j % 2) * 1024 + n * 512:(j % 2) * 1024 + (n + 1) * 512],
                                                start=(j == 0), stop=(j == DC - 1))
                        st.inc(ins)
                    kn_done = st.n

                    @blk.vector
                    def _(v, ch=ch, kn_done=kn_done):
                        v.wait_ge(st.h, kn_done)
                        ins = v.reduce_max(kmax2[:, 1:2], kn2_ps[:], axis=AX.X)
                        sv.inc(ins)
                        v.wait_ge(sv.h, sv.n)
                        ins = v.tensor_tensor(kmax2[:, 0:1], kmax2[:, 0:1], kmax2[:, 1:2], ALU.max)
                        sv.inc(ins)
                        ch_kmax_done[ch] = sv.n

                    v_mm_done = {}
                    v_evac_actual = {}
                    v_wr_done = {}

                    @blk.tensor
                    def _(te, ch=ch, base=base):
                        ss_base = ss.n
                        for tl in range(8):
                            for n in range(2):
                                idx = tl * 2 + n
                                if idx >= 2:
                                    te.wait_ge(ss.h, ss_base + idx - 1)
                                ins = None
                                for dd in range(DC):
                                    ins = te.matmul(v_ps[idx % 2][:],
                                                    xnT_kv[:, base + dd * 1024 + tl * 128:base + dd * 1024 + tl * 128 + 128],
                                                    wv_bf[:, dd * D + n * 512:dd * D + (n + 1) * 512],
                                                    start=(dd == 0), stop=(dd == DC - 1))
                                st.inc(ins)
                                v_mm_done[idx] = st.n

                    @blk.scalar
                    def _(sc, ch=ch):
                        for tl in range(8):
                            for n in range(2):
                                idx = tl * 2 + n
                                sc.wait_ge(st.h, v_mm_done[idx])
                                if tl >= 2:
                                    sc.wait_ge(wr.h, v_wr_done.get(tl - 2, 0))
                                ins = sc.activation(
                                    vtb[:, (tl % 2) * 1024 + n * 512:(tl % 2) * 1024 + (n + 1) * 512],
                                    v_ps[idx % 2][:], AF.Copy)
                                ss.inc(ins)
                            row = (ch * 8 + tl) * 128
                            for p in range(2):
                                with sc.If_eq(_par, p):
                                    sc.dma_start(
                                        out=v_sh[p * (S // 2) + row:p * (S // 2) + row + 128, :],
                                        in_=vtb[:, (tl % 2) * 1024:(tl % 2 + 1) * 1024],
                                    ).then_inc(wr.h, 16)
                            wr.n += 16
                            v_wr_done[tl] = wr.n
                    ch_mm_done[ch] = st.n
                wr_scr_all = wr.n

                @blk.vector
                def _(v):
                    v.wait_ge(sv.h, ch_kmax_done[NCHH - 1])
                    ins = v.memset(km_sb8[:], 0.0)
                    sv.inc(ins)
                    v.wait_ge(sv.h, sv.n)
                    ins = v.tensor_copy(km_sb8[:, 0:1], kmax2[:, 0:1])
                    sv.inc(ins)
                sv_km8 = sv.n

                @blk.scalar
                def _(sc):
                    sc.wait_ge(sv.h, sv_km8)
                    ins = sc.dma_start(out=km_in[:], in_=km_sb8[:])
                    wr.inc(ins, 16)
                wr_all_kv = wr.n

                @blk.gpsimd
                def _(g):
                    g.wait_ge(wr.h, wr_all_kv)
                    g.collective_compute(
                        "AllReduce", ALU.max,
                        replica_groups=[list(range(NCORES))],
                        ins=[km_in.ap().opt()], outs=[km_out.ap().opt()],
                    ).then_inc(cc.h, 1)
                    cc.n += 1
                cc_bar = cc.n

                blk.__exit__(None, None, None)

        a1 = ExitStack()
        with a1:
            kt_tile = a1.enter_context(nc.sbuf_tensor("kt_tile", [128, 2 * DC * 1024], BF16))
            v_tile = a1.enter_context(nc.sbuf_tensor("v_tile", [128, 2 * 8 * 1024], BF16))
            expS = a1.enter_context(nc.sbuf_tensor("expS", [128, 2 * 8 * 512], BF16))
            x_res = a1.enter_context(nc.sbuf_tensor("x_res", [128, 2 * D], F32))
            rzc = a1.enter_context(nc.sbuf_tensor("rzc", [128, TT], F32))
            kmg8 = a1.enter_context(nc.sbuf_tensor("kmg8", [1, 8], F32))

            p5 = ExitStack()
            with p5:
                s_ps = [p5.enter_context(nc.psum_tensor(f"sps{i}", [128, 512], F32)) for i in range(2)]
                o_ps = [p5.enter_context(nc.psum_tensor(f"ops{i}", [128, 1026], F32)) for i in range(2)]

                blk = nc.Block("attn"); blk.__enter__()

                @blk.sync
                def _(sync):
                    sync.wait_ge(cc.h, cc_bar)
                    sync.dma_start(out=kmg8[:], in_=km_out[:]).then_inc(ld.h, 16)
                    ld.n += 16
                ld_kmg = ld.n

                @blk.scalar
                def _(sc):
                    sc.wait_ge(ld.h, ld_kmg)
                    sc.wait_ge(sv.h, sv_qmax)
                    ins = sc.activation(bstat[:, 0:1], qmax2[:, 0:1], AF.Ln)
                    ss.inc(ins)
                    sc.wait_ge(ss.h, ss.n)
                    ins = sc.activation(bstat[:, 1:2], kmg8[:, 0:1], AF.Ln)
                    ss.inc(ins)
                ss_lnk = ss.n

                @blk.vector
                def _(v):
                    v.wait_ge(ss.h, ss_lnk)
                    ins = v.tensor_tensor(bstat[:, 2:3], bstat[:, 0:1], bstat[:, 1:2], ALU.add)
                    sv.inc(ins)
                sv_lnsum = sv.n

                @blk.scalar
                def _(sc):
                    sc.wait_ge(sv.h, sv_lnsum)
                    ins = sc.activation(bstat[:, 3:4], bstat[:, 2:3], AF.Exp, bias=nl32_sb[:], scale=0.5)
                    ss.inc(ins)
                ss_B = ss.n

                @blk.vector
                def _(v):
                    v.wait_ge(ss.h, ss_B)
                    ins = v.tensor_scalar(bstat[:, 4:5], bstat[:, 3:4], -1.0, None, ALU.mult)
                    sv.inc(ins)
                    v.wait_ge(sv.h, sv.n)
                    ins = v.tensor_copy(nb_bf[:], bstat[0:1, 4:5])
                    sv.inc(ins)
                sv_nbbf = sv.n

                @blk.tensor
                def _(te):
                    te.wait_ge(sv.h, sv_nbbf)
                    ins = te.matmul(s_ps[0][:, 0:1], ones1_bf[:], nb_bf[:], start=True, stop=True)
                    st.inc(ins)
                st_nb = st.n

                @blk.vector
                def _(v):
                    v.wait_ge(st.h, st_nb)
                    ins = v.tensor_copy(negB[:], s_ps[0][:, 0:1])
                    sv.inc(ins)
                sv_negB = sv.n

                kv_loaded = {}
                kv_mm_done = {}
                exp_done = {}
                pv_done = {}
                flush_done = {}

                for kvt in range(NCH):
                    @blk.sync
                    def _(sync, kvt=kvt):
                        sync.wait_ge(st.h, kv_mm_done.get(kvt - 2, 0))
                        half = kvt % 2
                        sync.dma_start(
                            out=kt_tile[:, half * DC * 1024:(half + 1) * DC * 1024].rearrange("p (j k) -> p j k", j=DC),
                            in_=kT_sh[:, :, kvt * 1024:(kvt + 1) * 1024].rearrange("j p k -> p j k"),
                        ).then_inc(ld.h, 16)
                        sync.dma_start(
                            out=v_tile[:, half * 8 * 1024:(half + 1) * 8 * 1024].rearrange("p (g d) -> p g d", g=8),
                            in_=v_sh[kvt * 1024:(kvt + 1) * 1024, :].rearrange("(g p) d -> p g d", p=128),
                        ).then_inc(ld.h, 16)
                        ld.n += 32
                    kv_loaded[kvt] = ld.n

                    for qs in range(4):
                        ebase = (qs % 2) * 8 * 512
                        smm_done = {}

                        @blk.tensor
                        def _(te, kvt=kvt, qs=qs, smm_done=smm_done):
                            te.wait_ge(ld.h, kv_loaded[kvt])
                            if kvt == 0 and qs == 0:
                                te.wait_ge(sv.h, sv_negB)
                            half = (kvt % 2) * DC * 1024
                            ss_base = ss.n
                            for kg in range(8):
                                if kg >= 2:
                                    te.wait_ge(ss.h, ss_base + kg - 1)
                                else:
                                    key = (kvt, qs - 1, 6 + kg) if qs >= 1 else (kvt - 1, 3, 6 + kg)
                                    if key in exp_done:
                                        te.wait_ge(ss.h, exp_done[key])
                                ins = None
                                for dd in range(DC):
                                    ins = te.matmul(s_ps[kg % 2][:],
                                                    kt_tile[:, half + dd * 1024 + kg * 128:half + dd * 1024 + (kg + 1) * 128],
                                                    QT[:, dd * T + qs * 512:dd * T + (qs + 1) * 512],
                                                    start=(dd == 0), stop=(dd == DC - 1))
                                st.inc(ins)
                                smm_done[kg] = st.n

                        @blk.scalar
                        def _(sc, kvt=kvt, qs=qs, ebase=ebase, smm_done=smm_done):
                            for kg in range(8):
                                sc.wait_ge(st.h, smm_done[kg])
                                ins = sc.activation(expS[:, ebase + kg * 512:ebase + (kg + 1) * 512],
                                                    s_ps[kg % 2][:], AF.Exp,
                                                    bias=negB[:], scale=SCALE)
                                ss.inc(ins)
                                exp_done[(kvt, qs, kg)] = ss.n

                        @blk.tensor
                        def _(te, kvt=kvt, qs=qs, ebase=ebase):
                            te.wait_ge(ss.h, exp_done[(kvt, qs, 7)])
                            half = (kvt % 2) * 8 * 1024
                            sv_base = sv.n
                            for qt in range(4):
                                if qt >= 2:
                                    te.wait_ge(sv.h, sv_base + qt - 1)
                                else:
                                    key = (kvt, qs - 1, 2 + qt) if qs >= 1 else (kvt - 1, 3, 2 + qt)
                                    if key in flush_done:
                                        te.wait_ge(sv.h, flush_done[key])
                                ins = None
                                for kg in range(8):
                                    lhs = expS[:, ebase + kg * 512 + qt * 128:ebase + kg * 512 + (qt + 1) * 128]
                                    te.matmul(o_ps[qt % 2][:, 0:512], lhs,
                                              v_tile[:, half + kg * 1024:half + kg * 1024 + 512],
                                              start=(kg == 0), stop=(kg == 7))
                                    te.matmul(o_ps[qt % 2][:, 512:1024], lhs,
                                              v_tile[:, half + kg * 1024 + 512:half + kg * 1024 + 1024],
                                              start=(kg == 0), stop=(kg == 7))
                                    ins = te.matmul(o_ps[qt % 2][:, 1024:1026], lhs,
                                                    ones2_bf[:],
                                                    start=(kg == 0), stop=(kg == 7))
                                st.inc(ins)
                                pv_done[(kvt, qs, qt)] = st.n

                        @blk.vector
                        def _(v, kvt=kvt, qs=qs):
                            for qt in range(4):
                                v.wait_ge(st.h, pv_done[(kvt, qs, qt)])
                                gt = qs * 4 + qt
                                ins = v.tensor_tensor(O_sb[:, gt * 1026:(gt + 1) * 1026],
                                                      O_sb[:, gt * 1026:(gt + 1) * 1026],
                                                      o_ps[qt % 2][:], ALU.add)
                                sv.inc(ins)
                                flush_done[(kvt, qs, qt)] = sv.n
                    kv_mm_done[kvt] = st.n

                last_flush = sv.n

                @blk.vector
                def _(v):
                    v.wait_ge(sv.h, last_flush)
                    ins = None
                    for gt in range(TT):
                        ins = v.reciprocal(rzc[:, gt:gt + 1], O_sb[:, gt * 1026 + 1024:gt * 1026 + 1025])
                    sv.inc(ins)
                sv_rz = sv.n

                xres_done = {}
                fin_done = {}
                for gt in range(TT):
                    @blk.sync
                    def _(sync, gt=gt):
                        sync.wait_ge(sv.h, fin_done.get(gt - 2, 0))
                        sync.dma_start(out=x_res[:, (gt % 2) * D:(gt % 2 + 1) * D],
                                       in_=x_own[gt * 128:(gt + 1) * 128, :]).then_inc(ld.h, 16)
                        ld.n += 16
                    xres_done[gt] = ld.n

                    @blk.vector
                    def _(v, gt=gt):
                        v.wait_ge(ld.h, xres_done[gt])
                        v.wait_ge(sv.h, sv_rz)
                        ins = v.scalar_tensor_tensor(O_sb[:, gt * 1026:gt * 1026 + 1024],
                                                     O_sb[:, gt * 1026:gt * 1026 + 1024],
                                                     rzc[:, gt:gt + 1],
                                                     x_res[:, (gt % 2) * D:(gt % 2 + 1) * D],
                                                     ALU.mult, ALU.add)
                        sv.inc(ins)
                        fin_done[gt] = sv.n
                blk.__exit__(None, None, None)

        qt_guard.__exit__(None, None, None)

        C = 640
        SC5 = C // 128
        FQ = 4
        f1 = ExitStack()
        with f1:
            iota_f = f1.enter_context(nc.sbuf_tensor("iota_f", [128, C], F32))
            comb_bf = f1.enter_context(nc.sbuf_tensor("comb_bf", [128, TT * E], BF16))
            posb = f1.enter_context(nc.sbuf_tensor("posb", [128, TT * E], F32))
            comb_sel = f1.enter_context(nc.sbuf_tensor("comb_sel", [128, 8], F32))
            b1_sb = f1.enter_context(nc.sbuf_tensor("b1_sb", [128, E * (FF // 128)], F32))
            b2e = f1.enter_context(nc.sbuf_tensor("b2e", [1, D], BF16))
            b1_rm = f1.enter_context(nc.sbuf_tensor("b1_rm", [128, 256], F32))

            def E_view(tt):
                return u20[:, tt * C:(tt + 1) * C]

            def hq_view(fc):
                return u20[:, fc * C:(fc + 1) * C]

            def ybf_view(st5):
                return u20[:, 8 * C + st5 * 1024:8 * C + (st5 + 1) * 1024]

            blk = nc.Block("ffn")
            blk.__enter__()

            ppre = ExitStack()
            with ppre:
                b1t_ps = ppre.enter_context(nc.psum_tensor("b1t_ps", [128, 256], F32))
                trp8 = ppre.enter_context(nc.psum_tensor("trp8", [8, 128], F32))
                trpb = ppre.enter_context(nc.psum_tensor("trpb", [128, 8], F32))
                selT = ppre.enter_context(nc.sbuf_tensor("selT", [8, T], F32))
                ones8 = ppre.enter_context(nc.sbuf_tensor("ones8", [8, T], F32))
                mask8 = ppre.enter_context(nc.sbuf_tensor("mask8", [8, T], F32))
                posT8 = ppre.enter_context(nc.sbuf_tensor("posT8", [8, T], F32))

                @blk.sync
                def _(sync):
                    sync.dma_start(
                        out=b1_rm[:].rearrange("q (m p) -> q m p", m=2),
                        in_=b1[:].rearrange("e f -> (e f)").rearrange("(m q p) -> q m p", m=2, p=128),
                    ).then_inc(ld.h, 16)
                    ld.n += 16
                ld_ffn_pre = ld.n

                @blk.gpsimd
                def _(g):
                    ins = g.iota(iota_f[:], [[1, C]], channel_multiplier=0,
                                 allow_small_or_imprecise_dtypes=True)
                    sg.inc(ins)
                sg_iota = sg.n

                @blk.tensor
                def _(te):
                    te.wait_ge(ld.h, ld_ffn_pre)
                    te.transpose(b1t_ps[:, 0:128], b1_rm[:, 0:128], id_32[:])
                    ins = te.transpose(b1t_ps[:, 128:256], b1_rm[:, 128:256], id_32[:])
                    st.inc(ins)
                st_b1t = st.n

                @blk.vector
                def _(v):
                    v.wait_ge(st.h, st_b1t)
                    v.tensor_copy(b1_sb[:], b1t_ps[:])
                    v.memset(ones8[:], 1.0)
                    ins = v.tensor_copy(comb_bf[:], comb[:])
                    sv.inc(ins)
                sv_sel = sv.n

                seltr_done = {}
                for tt in range(TT):
                    @blk.tensor
                    def _(te, tt=tt):
                        te.wait_ge(sv.h, sv_sel)
                        te.wait_ge(sv.h, seltr_done.get(tt - 1, 0))
                        ins = te.transpose(trp8[:], sel_full[:, tt * E:(tt + 1) * E], id_32[:])
                        st.inc(ins)
                    st_tr = st.n

                    @blk.vector
                    def _(v, tt=tt, st_tr=st_tr):
                        v.wait_ge(st.h, st_tr)
                        ins = v.tensor_copy(selT[:, tt * 128:(tt + 1) * 128], trp8[:])
                        sv.inc(ins)
                        seltr_done[tt] = sv.n

                @blk.vector
                def _(v):
                    v.wait_ge(sv.h, seltr_done[TT - 1])
                    v.tensor_tensor_scan(posT8[:], ones8[:], selT[:], 0.0, ALU.mult, ALU.add).then_inc(sv.h, 1)
                    sv.n += 1
                    v.wait_ge(sv.h, sv.n)
                    v.tensor_tensor(posT8[:], posT8[:], selT[:], ALU.subtract).then_inc(sv.h, 1)
                    sv.n += 1
                    v.wait_ge(sv.h, sv.n)
                    v.tensor_scalar(mask8[:], selT[:], -1.0e9, 1.0e9, ALU.mult, ALU.add).then_inc(sv.h, 1)
                    sv.n += 1
                    v.wait_ge(sv.h, sv.n)
                    ins = v.tensor_tensor(posT8[:], posT8[:], mask8[:], ALU.add)
                    sv.inc(ins)
                sv_posT = sv.n

                postr_done = {}
                for tt in range(TT):
                    @blk.tensor
                    def _(te, tt=tt):
                        te.wait_ge(sv.h, sv_posT)
                        te.wait_ge(sv.h, postr_done.get(tt - 1, 0))
                        ins = te.transpose(trpb[:], posT8[:, tt * 128:(tt + 1) * 128], id_32[0:8, 0:8])
                        st.inc(ins)
                    st_tr = st.n

                    @blk.vector
                    def _(v, tt=tt, st_tr=st_tr):
                        v.wait_ge(st.h, st_tr)
                        ins = v.tensor_copy(posb[:, tt * E:(tt + 1) * E], trpb[:])
                        sv.inc(ins)
                        postr_done[tt] = sv.n
            sv_ffn_pre = sv.n

            xn_nat = f1.enter_context(nc.sbuf_tensor("xn_nat", [128, TT * D], BF16))
            u20 = f1.enter_context(nc.sbuf_tensor("u20", [128, TT * C], BF16))
            ET_sb = f1.enter_context(nc.sbuf_tensor("ET_sb", [128, SC5 * T], BF16))
            xn_selT = f1.enter_context(nc.sbuf_tensor("xn_selT", [128, DC * C], BF16))
            y_f32 = f1.enter_context(nc.sbuf_tensor("y_f32", [128, SC5 * 1024], F32))
            w1c = f1.enter_context(nc.sbuf_tensor("w1c", [128, 4 * DC * 128], BF16))
            w2q = f1.enter_context(nc.sbuf_tensor("w2q", [128, 8 * 1024], BF16))

            @blk.sync
            def _(sync):
                sync.wait_ge(sv.h, sv_ffn_pre)
                sync.wait_ge(wr.h, wr_xnT)
                sync.dma_start(
                    out=xn_nat[:].rearrange("p (g d) -> p g d", g=TT),
                    in_=xn_scr[:].rearrange("g p d -> p g d"),
                ).then_inc(ld.h, 16)
                ld.n += 16
            ld_xn_nat = ld.n

            p6 = ExitStack()
            with p6:
                hy_ps = [p6.enter_context(nc.psum_tensor(f"hy{i}", [128, 1024], F32)) for i in range(2)]
                trE_ps = [p6.enter_context(nc.psum_tensor(f"trE_ps{i}", [128, C], BF16)) for i in range(2)]
                csel_ps = p6.enter_context(nc.psum_tensor("csel_ps", [128, 8], F32))

                scatter_done = {0: 0}
                oflush_all = {0: 0}
                last_y_done = [0]
                prev_h_done = [0]
                prev_scmm = [None]

                for e in range(E):
                    if prev_scmm[0] is not None:
                        @blk.vector
                        def _(v, e=e, scmm_prev=prev_scmm[0]):
                            for tt in range(TT):
                                v.wait_ge(st.h, scmm_prev[tt])
                                ins = v.tensor_tensor(O_sb[:, tt * 1026:tt * 1026 + 1024],
                                                      O_sb[:, tt * 1026:tt * 1026 + 1024],
                                                      hy_ps[tt % 2][:], ALU.add)
                                sv.inc(ins)
                    oflush_all[e] = sv.n

                    @blk.vector
                    def _(v, e=e):
                        if e == 0:
                            v.wait_ge(sv.h, sv_ffn_pre)
                            v.wait_ge(sg.h, sg_iota)
                            v.wait_ge(ld.h, ld_ffn_pre)
                        for tt in range(8):
                            v.tensor_scalar(E_view(tt), iota_f[:],
                                            posb[:, tt * E + e:tt * E + e + 1],
                                            None, ALU.is_equal)

                    @blk.vector
                    def _(v, e=e):
                        v.wait_ge(st.h, scatter_done[e])
                        ins = None
                        for tt in range(8, TT):
                            ins = v.tensor_scalar(E_view(tt), iota_f[:],
                                                  posb[:, tt * E + e:tt * E + e + 1],
                                                  None, ALU.is_equal)
                        sv.inc(ins)
                    E_done = sv.n

                    gth_mm = {}

                    @blk.tensor
                    def _(te, e=e, E_done=E_done, gth_mm=gth_mm):
                        te.wait_ge(sv.h, E_done)
                        te.wait_ge(sv.h, oflush_all[e])
                        if e == 0:
                            te.wait_ge(ld.h, ld_xn_nat)
                        ss_base = ss.n
                        for j in range(DC):
                            if j >= 2:
                                te.wait_ge(ss.h, ss_base + j - 1)
                            g_ps = hy_ps[j % 2]
                            ins = None
                            for tt in range(TT):
                                te.matmul(g_ps[:, 0:512],
                                          xn_nat[:, tt * D + j * 128:tt * D + (j + 1) * 128],
                                          u20[:, tt * C:tt * C + 512],
                                          start=(tt == 0), stop=(tt == TT - 1))
                                ins = te.matmul(g_ps[:, 512:C],
                                                xn_nat[:, tt * D + j * 128:tt * D + (j + 1) * 128],
                                                u20[:, tt * C + 512:(tt + 1) * C],
                                                start=(tt == 0), stop=(tt == TT - 1))
                            st.inc(ins)
                            gth_mm[j] = st.n
                        for st5 in range(SC5):
                            ins = None
                            for tt in range(TT):
                                ins = te.matmul(csel_ps[:, st5:st5 + 1],
                                                u20[:, tt * C + st5 * 128:tt * C + (st5 + 1) * 128],
                                                comb_bf[:, tt * E + e:tt * E + e + 1],
                                                start=(tt == 0), stop=(tt == TT - 1))
                            st.inc(ins)
                        gth_mm["csel"] = st.n
                        svb = sv.n
                        for tt in range(TT):
                            if tt >= 2:
                                te.wait_ge(sv.h, svb + tt)
                            ins = None
                            for sc5 in range(SC5):
                                ins = te.transpose(trE_ps[tt % 2][:, sc5 * 128:(sc5 + 1) * 128],
                                                   u20[:, tt * C + sc5 * 128:tt * C + (sc5 + 1) * 128],
                                                   id_bf[:])
                            st.inc(ins)
                            gth_mm[("et", tt)] = st.n

                    @blk.scalar
                    def _(sc, e=e, gth_mm=gth_mm):
                        if e == 0:
                            sc.wait_ge(sv.h, sv_sel)
                        for j in range(DC):
                            sc.wait_ge(st.h, gth_mm[j])
                            ins = sc.activation(xn_selT[:, j * C:(j + 1) * C],
                                                hy_ps[j % 2][:, 0:C], AF.Copy)
                            ss.inc(ins)
                    gth_evac = ss.n

                    @blk.vector
                    def _(v, e=e, gth_mm=gth_mm):
                        v.wait_ge(st.h, gth_mm["csel"])
                        ins = v.tensor_copy(comb_sel[:, 0:SC5], csel_ps[:, 0:SC5])
                        sv.inc(ins)
                        for tt in range(TT):
                            v.wait_ge(st.h, gth_mm[("et", tt)])
                            dst = ET_sb[:].rearrange("p (s t) -> p s t", s=SC5)[:, :, tt * 128:(tt + 1) * 128]
                            src = trE_ps[tt % 2][:].rearrange("p (s c) -> p s c", c=128)
                            ins = v.tensor_copy(dst, src)
                            sv.inc(ins)
                    et_evac = sv.n
                    csel_evac_sv = et_evac - TT

                    yflush = {}
                    for fb in range(FQ):
                        hmm_d = {}
                        w1_ld = {}

                        @blk.gpsimd
                        def _(g, e=e, fb=fb, w1_ld=w1_ld):
                            if e == 0 and fb == 0:
                                g.wait_ge(sv.h, sv_ffn_pre)
                            g.wait_ge(st.h, prev_h_done[0])
                            st_base_g = st.n
                            for fc in range(4):
                                ins = g.dma_start(
                                    out=w1c[:, (fc % 4) * DC * 128:((fc % 4) + 1) * DC * 128].rearrange("p (j f) -> p j f", j=DC),
                                    in_=w1[e, :, fb * 1024 + fc * 128:fb * 1024 + (fc + 1) * 128].rearrange("(j p) f -> p j f", p=128),
                                )
                                ldg.inc(ins, 16)
                                w1_ld[fc] = ldg.n
                            if fb == 0:
                                ins = g.dma_start(out=b2e[:], in_=b2[e][None, :])
                                ldg.inc(ins, 16)
                            g.wait_ge(st.h, last_y_done[0])
                            ins = g.dma_start(
                                out=w2q[:].rearrange("p (c d) -> p c d", c=8),
                                in_=w2[e, fb * 1024:(fb + 1) * 1024, :].rearrange("(c p) d -> p c d", p=128),
                            )
                            ldg.inc(ins, 16)
                            for fc in range(4, 8):
                                g.wait_ge(st.h, st_base_g + fc - 3)
                                ins = g.dma_start(
                                    out=w1c[:, (fc % 4) * DC * 128:((fc % 4) + 1) * DC * 128].rearrange("p (j f) -> p j f", j=DC),
                                    in_=w1[e, :, fb * 1024 + fc * 128:fb * 1024 + (fc + 1) * 128].rearrange("(j p) f -> p j f", p=128),
                                )
                                ldg.inc(ins, 16)
                                w1_ld[fc] = ldg.n
                        w2_ld = ldg.n

                        @blk.tensor
                        def _(te, e=e, fb=fb, hmm_d=hmm_d, w1_ld=w1_ld,
                              gth_evac=gth_evac, et_evac=et_evac, yflush=yflush):
                            ss_base = ss.n
                            for fc in range(8):
                                te.wait_ge(ldg.h, w1_ld[fc])
                                if fb == 0 and fc == 0:
                                    te.wait_ge(ss.h, gth_evac)
                                    te.wait_ge(sv.h, et_evac)
                                if fb > 0 and fc < 2:
                                    te.wait_ge(sv.h, yflush[(fb - 1, 4 - fc)])
                                if fc >= 2:
                                    te.wait_ge(ss.h, ss_base + fc - 1)
                                h_ps = hy_ps[fc % 2]
                                ins = None
                                for n2 in range(2):
                                    lo = n2 * 512
                                    hi = min(C, lo + 512)
                                    for dd in range(DC):
                                        ins = te.matmul(h_ps[:, lo:hi],
                                                        w1c[:, (fc % 4) * DC * 128 + dd * 128:(fc % 4) * DC * 128 + (dd + 1) * 128],
                                                        xn_selT[:, dd * C + lo:dd * C + hi],
                                                        start=(dd == 0), stop=(dd == DC - 1))
                                st.inc(ins)
                                hmm_d[fc] = st.n
                        prev_h_done[0] = st.n

                        @blk.scalar
                        def _(sc, e=e, fb=fb, hmm_d=hmm_d):
                            for fc in range(8):
                                sc.wait_ge(st.h, hmm_d[fc])
                                bcol = e * (FF // 128) + fb * 8 + fc
                                ins = sc.activation(hq_view(fc), hy_ps[fc % 2][:, 0:C],
                                                    AF.Gelu, bias=b1_sb[:, bcol:bcol + 1], scale=1.0)
                                ss.inc(ins)
                        gelu_all = ss.n

                        ymm_d = {}

                        @blk.tensor
                        def _(te, e=e, fb=fb, gelu_all=gelu_all, ymm_d=ymm_d,
                              w2_ld=w2_ld, yflush=yflush):
                            te.wait_ge(ss.h, gelu_all)
                            te.wait_ge(ldg.h, w2_ld)
                            sv_base = sv.n
                            for st5 in range(SC5):
                                if st5 >= 2:
                                    te.wait_ge(sv.h, sv_base + st5 - 1)
                                elif fb > 0:
                                    te.wait_ge(sv.h, yflush.get((fb - 1, 4 - st5), 0))
                                y_ps = hy_ps[st5 % 2]
                                first0 = first1 = True
                                if fb == 0:
                                    te.matmul(y_ps[:, 0:512], ones1_bf[:], b2e[:, 0:512],
                                              start=True, stop=False)
                                    te.matmul(y_ps[:, 512:1024], ones1_bf[:], b2e[:, 512:1024],
                                              start=True, stop=False)
                                    first0 = first1 = False
                                ins = None
                                for fc in range(8):
                                    lhs = u20[:, fc * C + st5 * 128:fc * C + (st5 + 1) * 128]
                                    te.matmul(y_ps[:, 0:512], lhs, w2q[:, fc * 1024:fc * 1024 + 512],
                                              start=first0, stop=(fc == 7))
                                    ins = te.matmul(y_ps[:, 512:1024], lhs,
                                                    w2q[:, fc * 1024 + 512:fc * 1024 + 1024],
                                                    start=first1, stop=(fc == 7))
                                    first0 = first1 = False
                                st.inc(ins)
                                ymm_d[st5] = st.n
                        last_y_done[0] = st.n

                        @blk.vector
                        def _(v, e=e, fb=fb, ymm_d=ymm_d, yflush=yflush):
                            for st5 in range(SC5):
                                v.wait_ge(st.h, ymm_d[st5])
                                if fb == 0:
                                    ins = v.tensor_copy(y_f32[:, st5 * 1024:(st5 + 1) * 1024],
                                                        hy_ps[st5 % 2][:])
                                else:
                                    ins = v.tensor_tensor(y_f32[:, st5 * 1024:(st5 + 1) * 1024],
                                                          y_f32[:, st5 * 1024:(st5 + 1) * 1024],
                                                          hy_ps[st5 % 2][:], ALU.add)
                                sv.inc(ins)
                                yflush[(fb, st5)] = sv.n

                    @blk.vector
                    def _(v, e=e, csel_evac_sv=csel_evac_sv):
                        v.wait_ge(sv.h, csel_evac_sv)
                        ins = None
                        for st5 in range(SC5):
                            ins = v.tensor_scalar(ybf_view(st5), y_f32[:, st5 * 1024:(st5 + 1) * 1024],
                                                  comb_sel[:, st5:st5 + 1], None, ALU.mult)
                        sv.inc(ins)
                    ybf_done = sv.n

                    scmm_d = {}

                    @blk.tensor
                    def _(te, e=e, ybf_done=ybf_done, scmm_d=scmm_d):
                        te.wait_ge(sv.h, ybf_done)
                        sv_base = sv.n
                        for tt in range(TT):
                            if tt >= 2:
                                te.wait_ge(sv.h, sv_base + tt - 1)
                            s_ps2 = hy_ps[tt % 2]
                            ins = None
                            for sc5 in range(SC5):
                                lhs = ET_sb[:, sc5 * T + tt * 128:sc5 * T + (tt + 1) * 128]
                                te.matmul(s_ps2[:, 0:512], lhs,
                                          u20[:, 8 * C + sc5 * 1024:8 * C + sc5 * 1024 + 512],
                                          start=(sc5 == 0), stop=(sc5 == SC5 - 1))
                                ins = te.matmul(s_ps2[:, 512:1024], lhs,
                                                u20[:, 8 * C + sc5 * 1024 + 512:8 * C + (sc5 + 1) * 1024],
                                                start=(sc5 == 0), stop=(sc5 == SC5 - 1))
                            st.inc(ins)
                            scmm_d[tt] = st.n
                        scatter_done[e + 1] = st.n

                    prev_scmm[0] = scmm_d

                @blk.vector
                def _(v, scmm_prev=prev_scmm[0]):
                    for tt in range(TT):
                        v.wait_ge(st.h, scmm_prev[tt])
                        ins = v.tensor_tensor(O_sb[:, tt * 1026:tt * 1026 + 1024],
                                              O_sb[:, tt * 1026:tt * 1026 + 1024],
                                              hy_ps[tt % 2][:], ALU.add)
                        sv.inc(ins)
                ffn_all = sv.n

                @blk.sync
                def _(sync):
                    sync.wait_ge(sv.h, ffn_all)
                    sync.dma_start(
                        out=out[:].rearrange("(t p) d -> p t d", p=128),
                        in_=O_sb[:].rearrange("p (t c) -> p t c", c=1026)[:, :, 0:1024],
                    ).then_inc(so.h, 16)
                    so.n += 16
                    sync.wait_ge(so.h, so.n)

                @blk.gpsimd
                def _(g):
                    g.wait_ge(so.h, so.n)

                blk.__exit__(None, None, None)

    return nc


_NC_CACHE = [None]


def kernel(L0_attn_x, L0_attn_W_q, L0_attn_W_k, L0_attn_W_v,
           L0_ffn_gate_w, L0_ffn_w1, L0_ffn_b1, L0_ffn_w2, L0_ffn_b2):
    x = np.ascontiguousarray(np.asarray(L0_attn_x, dtype=np.float32).reshape(S, D))
    wq_np = np.ascontiguousarray(np.asarray(L0_attn_W_q, dtype=np.float32))
    wk_np = np.ascontiguousarray(np.asarray(L0_attn_W_k, dtype=np.float32))
    wv_np = np.ascontiguousarray(np.asarray(L0_attn_W_v, dtype=np.float32))
    gw_np = np.ascontiguousarray(np.asarray(L0_ffn_gate_w, dtype=np.float32))
    w1_np = np.ascontiguousarray(np.asarray(L0_ffn_w1, dtype=np.float32))
    b1_np = np.ascontiguousarray(np.asarray(L0_ffn_b1, dtype=np.float32))
    w2_np = np.ascontiguousarray(np.asarray(L0_ffn_w2, dtype=np.float32))
    b2_np = np.ascontiguousarray(np.asarray(L0_ffn_b2, dtype=np.float32))

    if _NC_CACHE[0] is None:
        _NC_CACHE[0] = build_nc()
    nc = _NC_CACHE[0]

    in_maps = []
    for c in range(NCORES):
        in_maps.append({
            "x_own": np.ascontiguousarray(x[c * T:(c + 1) * T]),
            "x_kv": np.ascontiguousarray(x[(c % 2) * (S // 2):((c % 2) + 1) * (S // 2)]),
            "wq": wq_np, "wk": wk_np, "wv": wv_np,
            "gate_w": gw_np,
            "w1": w1_np, "b1": b1_np, "w2": w2_np, "b2": b2_np,
        })
    res = run_bass_kernel_spmd(nc, in_maps, core_ids=list(range(NCORES)))
    shards = [res.results[c]["out"] for c in range(NCORES)]
    return np.concatenate(shards, axis=0).reshape(1, S, D).astype(np.float32, copy=False)


# version 15
# speedup vs baseline: 1.0298x; 1.0298x over previous
import numpy as np
from contextlib import ExitStack

import ml_dtypes
import concourse.bass as bass
import concourse.mybir as mybir
from concourse.bass_utils import run_bass_kernel_spmd

F32 = mybir.dt.float32
BF16 = mybir.dt.bfloat16
AF = mybir.ActivationFunctionType
ALU = mybir.AluOpType
AX = mybir.AxisListType

NCORES = 8
S = 16384
D = 1024
T = S // NCORES
E = 8
FF = 4096
DC = D // 128
TT = T // 128
NCH = S // 1024
EPS = 1e-5
SCALE = 1.0 / 32.0
NEG_LN32 = float(-np.log(32.0))


class Ctr:
    def __init__(self, es, nc, name):
        self.h = es.enter_context(nc.semaphore(name))
        self.n = 0

    def inc(self, ins, k=1):
        ins.then_inc(self.h, k)
        self.n += k
        return self.n


def _ln_group(blk, srcs, xst, sq_scr, stat, xn32, xn_bf, want32,
              slots, ld, sv, ss, eps_ap, wait_loads=(), wait_xn=()):
    n = len(srcs)

    @blk.sync
    def _(sync):
        for (h, v) in wait_loads:
            if v > 0:
                sync.wait_ge(h, v)
        for i in range(n):
            sync.dma_start(out=xst[:, slots[i] * D:(slots[i] + 1) * D],
                           in_=srcs[i]).then_inc(ld.h, 16)
            ld.n += 16
    ld_grp = ld.n

    @blk.vector
    def _(v):
        v.wait_ge(ld.h, ld_grp)
        for i in range(n):
            c = slots[i] * 16
            v.bn_stats(stat[:, c:c + 6], xst[:, slots[i] * D:slots[i] * D + 512])
            v.bn_stats(stat[:, c + 6:c + 12], xst[:, slots[i] * D + 512:(slots[i] + 1) * D])
        for i in range(n):
            c = slots[i] * 16
            v.bn_aggr(stat[:, c + 12:c + 14], stat[:, c:c + 12])
        ins = None
        for i in range(n):
            c = slots[i] * 16
            ins = v.tensor_scalar(stat[:, c + 15:c + 16], stat[:, c + 12:c + 13], -1.0, None, ALU.mult)
        sv.inc(ins)
    sv_stats = sv.n

    @blk.scalar
    def _(sc):
        sc.wait_ge(sv.h, sv_stats)
        for i in range(n):
            c = slots[i] * 16
            sc.activation(stat[:, c:c + 1], stat[:, c + 13:c + 14], AF.Ln, bias=eps_ap, scale=1.0)
        ins = None
        for i in range(n):
            c = slots[i] * 16
            ins = sc.activation(stat[:, c + 14:c + 15], stat[:, c:c + 1], AF.Exp, scale=-0.5)
        ss.inc(ins)
    ss_inv = ss.n

    xn_done = {}

    @blk.vector
    def _(v):
        v.wait_ge(ss.h, ss_inv)
        for (h, val) in wait_xn:
            if val > 0:
                v.wait_ge(h, val)
        for i in range(n):
            c = slots[i] * 16
            sl = slots[i]
            if want32:
                v.tensor_scalar(xn32[:, sl * D:(sl + 1) * D],
                                xst[:, sl * D:(sl + 1) * D],
                                stat[:, c + 15:c + 16], stat[:, c + 14:c + 15],
                                ALU.add, ALU.mult)
                ins = v.tensor_copy(xn_bf[:, sl * D:(sl + 1) * D],
                                    xn32[:, sl * D:(sl + 1) * D])
            else:
                ins = v.tensor_scalar(xn_bf[:, sl * D:(sl + 1) * D],
                                      xst[:, sl * D:(sl + 1) * D],
                                      stat[:, c + 15:c + 16], stat[:, c + 14:c + 15],
                                      ALU.add, ALU.mult)
            sv.inc(ins)
            xn_done[i] = sv.n
    return xn_done


def build_nc():
    nc = bass.Bass("TRN2", target_bir_lowering=False, enable_partition_id=True)
    _pid = nc.scalar.partition_id()
    kv_off = nc.scalar.scalar_reg_alu(ALU.mult, nc.scalar.scalar_reg_alu(ALU.mod, _pid, 2), S // 2)
    _ET = mybir.EngineType
    _par_regs = nc.alloc_registers("parreg", engines=[_ET.Activation])
    nc.regs_mov(_par_regs, _pid)
    nc.scalar.scalar_reg_alu(ALU.mod, _par_regs.handles[0], 2)
    _par = _par_regs.handles[0]
    _pid_sp = nc.sync.partition_id()
    _parsp_regs = nc.alloc_registers("parsp", engines=[_ET.SP])
    nc.regs_mov(_parsp_regs, _pid_sp)
    nc.sync.scalar_reg_alu(ALU.mod, _parsp_regs.handles[0], 2)
    _parsp = _parsp_regs.handles[0]

    x_own = nc.dram_tensor("x_own", [T, D], F32, kind="ExternalInput")
    x_kv = nc.dram_tensor("x_kv", [S // 2, D], F32, kind="ExternalInput")
    wq = nc.dram_tensor("wq", [D, D], F32, kind="ExternalInput")
    wk = nc.dram_tensor("wk", [D, D], F32, kind="ExternalInput")
    wv = nc.dram_tensor("wv", [D, D], F32, kind="ExternalInput")
    gate_w = nc.dram_tensor("gate_w", [D, E], F32, kind="ExternalInput")
    w1 = nc.dram_tensor("w1", [E, D, FF], F32, kind="ExternalInput")
    b1 = nc.dram_tensor("b1", [E, FF], F32, kind="ExternalInput")
    w2 = nc.dram_tensor("w2", [E, FF, D], F32, kind="ExternalInput")
    b2 = nc.dram_tensor("b2", [E, D], F32, kind="ExternalInput")
    out = nc.dram_tensor("out", [T, D], F32, kind="ExternalOutput")

    kT_sh = nc.dram_tensor("kT_sh", [DC, 128, S], BF16, addr_space="Shared")
    v_sh = nc.dram_tensor("v_sh", [S, D], BF16, addr_space="Shared")
    km_in = nc.dram_tensor("km_in", [1, 8], F32)
    km_out = nc.dram_tensor("km_out", [1, 8], F32)
    xnT_scr = nc.dram_tensor("xnT_scr", [DC, 128, T], BF16)
    xn_scr = nc.dram_tensor("xn_scr", [TT, 128, D], BF16)

    id_bf_c = nc.inline_tensor(np.eye(128, dtype=ml_dtypes.bfloat16), name="id_bf_c")
    id_32_c = nc.inline_tensor(np.eye(128, dtype=np.float32), name="id_32_c")

    es = ExitStack()
    with es:
        O_sb = es.enter_context(nc.sbuf_tensor("O_sb", [128, TT * 1026], F32))
        comb = es.enter_context(nc.sbuf_tensor("comb", [128, TT * E], F32))
        sel_full = es.enter_context(nc.sbuf_tensor("sel_full", [128, TT * E], F32))
        negB = es.enter_context(nc.sbuf_tensor("negB", [128, 1], F32))
        id_bf = es.enter_context(nc.sbuf_tensor("id_bf", [128, 128], BF16))
        id_32 = es.enter_context(nc.sbuf_tensor("id_32", [128, 128], F32))
        ones2_bf = es.enter_context(nc.sbuf_tensor("ones2_bf", [128, 2], BF16))
        onesk_bf = es.enter_context(nc.sbuf_tensor("onesk_bf", [128, 1], BF16))
        ones1_bf = es.enter_context(nc.sbuf_tensor("ones1_bf", [1, 128], BF16))
        kmax2 = es.enter_context(nc.sbuf_tensor("kmax2", [1, 2], F32))
        qmax2 = es.enter_context(nc.sbuf_tensor("qmax2", [1, 2], F32))
        bstat = es.enter_context(nc.sbuf_tensor("bstat", [1, 8], F32))
        nb_bf = es.enter_context(nc.sbuf_tensor("nb_bf", [1, 1], BF16))
        eps_sb = es.enter_context(nc.sbuf_tensor("eps_sb", [128, 1], F32))
        nl32_sb = es.enter_context(nc.sbuf_tensor("nl32_sb", [1, 1], F32))
        qt_guard = nc.sbuf_tensor("QT", [128, DC * T], BF16)
        QT = qt_guard.__enter__()

        ld = Ctr(es, nc, "ld")
        ldg = Ctr(es, nc, "ldg")
        wr = Ctr(es, nc, "wr")
        sv = Ctr(es, nc, "sv")
        ss = Ctr(es, nc, "ss")
        st = Ctr(es, nc, "st")
        so = Ctr(es, nc, "so")
        sg = Ctr(es, nc, "sg")
        cc = Ctr(es, nc, "cc")

        g1 = ExitStack()
        with g1:
            xst = g1.enter_context(nc.sbuf_tensor("xst", [128, 4 * D], F32))
            sq_scr = g1.enter_context(nc.sbuf_tensor("sq_scr", [128, D], F32))
            stat = g1.enter_context(nc.sbuf_tensor("stat", [128, 64], F32))
            xn32 = g1.enter_context(nc.sbuf_tensor("xn32", [128, 4 * D], F32))
            xn_bf = g1.enter_context(nc.sbuf_tensor("xn_bf", [128, 4 * D], BF16))
            wq_bf = g1.enter_context(nc.sbuf_tensor("wq_bf", [128, DC * D], BF16))
            gate32 = g1.enter_context(nc.sbuf_tensor("gate32", [128, DC * E], F32))
            xnT_own = g1.enter_context(nc.sbuf_tensor("xnT_own", [128, DC * T], BF16))
            xnT32_t = g1.enter_context(nc.sbuf_tensor("xnT32_t", [128, D], F32))
            glog = g1.enter_context(nc.sbuf_tensor("glog", [128, 4 * E], F32))
            mx2 = g1.enter_context(nc.sbuf_tensor("mx2", [128, 4 * E], F32))
            cstat = g1.enter_context(nc.sbuf_tensor("cstat", [128, 16], F32))
            expl = g1.enter_context(nc.sbuf_tensor("expl", [128, 4 * E], F32))
            maskt = g1.enter_context(nc.sbuf_tensor("maskt", [128, 4 * E], F32))
            qsq_bf = g1.enter_context(nc.sbuf_tensor("qsq_bf", [128, T], BF16))

            p1 = ExitStack()
            with p1:
                trbf_ps = [p1.enter_context(nc.psum_tensor(f"trbf{i}", [128, D], BF16)) for i in range(2)]
                tr32_ps = p1.enter_context(nc.psum_tensor("tr32", [128, D], F32))
                gate_ps = p1.enter_context(nc.psum_tensor("gate_ps", [128, E], F32))
                qt_ps = [p1.enter_context(nc.psum_tensor(f"qt{i}", [128, 512], F32)) for i in range(2)]

                blk = nc.Block("gen_own"); blk.__enter__()

                @blk.sync
                def _(sync):
                    sync.dma_start(out=id_bf[:], in_=id_bf_c[:]).then_inc(ld.h, 16)
                    sync.dma_start(out=id_32[:], in_=id_32_c[:]).then_inc(ld.h, 16)
                    sync.dma_start(
                        out=gate32[:].rearrange("p (j e) -> p j e", j=DC),
                        in_=gate_w[:].rearrange("(j p) e -> p j e", p=128),
                    ).then_inc(ld.h, 16)
                    ld.n += 48
                ld_const = ld.n

                @blk.vector
                def _(v):
                    v.memset(ones2_bf[:], 1.0)
                    v.memset(onesk_bf[:], 1.0)
                    v.memset(ones1_bf[:], 1.0)
                    v.memset(kmax2[:], 0.0)
                    v.memset(eps_sb[:], EPS)
                    v.memset(nl32_sb[:], NEG_LN32)
                    ins = None
                    for gt in range(TT):
                        ins = v.memset(O_sb[:, gt * 1026:(gt + 1) * 1026], 0.0)
                    sv.inc(ins)

                @blk.gpsimd
                def _(g):
                    g.dma_start(
                        out=wq_bf[:].rearrange("p (j d) -> p j d", j=DC),
                        in_=wq[:].rearrange("(j p) d -> p j d", p=128),
                    ).then_inc(ldg.h, 16)
                    ldg.n += 16
                ldg_wq = ldg.n

                grp_tr_done = {}
                grp_xn_done = {}
                grp_xnwr_done = {}
                evac_done = {}
                gate_done_d = {}
                glog_done = {}

                for grp in range(4):
                    tiles = [grp * 4 + i for i in range(4)]
                    xn_done = _ln_group(
                        blk,
                        [x_own[g * 128:(g + 1) * 128, :] for g in tiles],
                        xst, sq_scr, stat, xn32, xn_bf, True,
                        [i for i in range(4)],
                        ld, sv, ss, eps_sb[:],
                        wait_loads=[(sv.h, grp_xn_done.get(grp - 1, 0))],
                        wait_xn=[(st.h, grp_tr_done.get(grp - 1, 0)),
                                 (wr.h, grp_xnwr_done.get(grp - 1, 0))],
                    )
                    grp_xn_done[grp] = sv.n

                    @blk.scalar
                    def _(sc, tiles=tiles, xn_done=xn_done):
                        for i, g in enumerate(tiles):
                            sc.wait_ge(sv.h, xn_done[i])
                            ins = sc.dma_start(out=xn_scr[g, :, :],
                                               in_=xn_bf[:, (g % 4) * D:(g % 4 + 1) * D])
                            wr.inc(ins, 16)
                    grp_xnwr_done[grp] = wr.n

                    for i, g in enumerate(tiles):
                        @blk.tensor
                        def _(te, g=g, i=i, xn_done=xn_done):
                            te.wait_ge(sv.h, xn_done[i])
                            te.wait_ge(sv.h, evac_done.get(g - 2, 0))
                            te.wait_ge(sv.h, evac_done.get(g - 1, 0))
                            for j in range(DC):
                                te.transpose(trbf_ps[g % 2][:, j * 128:(j + 1) * 128],
                                             xn_bf[:, (g % 4) * D + j * 128:(g % 4) * D + (j + 1) * 128],
                                             id_bf[:])
                            ins = None
                            for j in range(DC):
                                ins = te.transpose(tr32_ps[:, j * 128:(j + 1) * 128],
                                                   xn32[:, (g % 4) * D + j * 128:(g % 4) * D + (j + 1) * 128],
                                                   id_32[:])
                            st.inc(ins)
                        tr_done = st.n

                        @blk.vector
                        def _(v, g=g, tr_done=tr_done):
                            v.wait_ge(st.h, tr_done)
                            v.wait_ge(st.h, gate_done_d.get(g - 1, 0))
                            dst = xnT_own[:].rearrange("p (j t) -> p j t", j=DC)[:, :, g * 128:(g + 1) * 128]
                            src = trbf_ps[g % 2][:].rearrange("p (j c) -> p j c", j=DC)
                            v.tensor_copy(dst, src)
                            ins = v.tensor_copy(xnT32_t[:], tr32_ps[:])
                            sv.inc(ins)
                            evac_done[g] = sv.n

                        @blk.tensor
                        def _(te, g=g):
                            te.wait_ge(sv.h, evac_done[g])
                            te.wait_ge(ld.h, ld_const)
                            te.wait_ge(sv.h, glog_done.get(g - 1, 0))
                            ins = None
                            for j in range(DC):
                                ins = te.matmul(gate_ps[:], xnT32_t[:, j * 128:(j + 1) * 128],
                                                gate32[:, j * E:(j + 1) * E],
                                                start=(j == 0), stop=(j == DC - 1))
                            st.inc(ins)
                            gate_done_d[g] = st.n

                        @blk.vector
                        def _(v, g=g):
                            v.wait_ge(st.h, gate_done_d[g])
                            ins = v.tensor_copy(glog[:, (g % 4) * E:(g % 4 + 1) * E], gate_ps[:])
                            sv.inc(ins)
                            glog_done[g] = sv.n
                    grp_tr_done[grp] = st.n

                    @blk.vector
                    def _(v, tiles=tiles):
                        v.wait_ge(sv.h, glog_done[tiles[-1]])
                        for g in tiles:
                            s4 = g % 4
                            v.max(out=mx2[:, s4 * E:(s4 + 1) * E], in_=glog[:, s4 * E:(s4 + 1) * E])
                        ins = None
                        for g in tiles:
                            s4 = g % 4
                            ins = v.tensor_scalar(cstat[:, s4 * 4:s4 * 4 + 1], mx2[:, s4 * E:s4 * E + 1],
                                                  -1.0, None, ALU.mult)
                        sv.inc(ins)
                    sv_negm1 = sv.n

                    @blk.scalar
                    def _(sc, tiles=tiles, sv_negm1=sv_negm1):
                        sc.wait_ge(sv.h, sv_negm1)
                        for g in tiles:
                            s4 = g % 4
                            sc.activation(cstat[:, s4 * 4 + 1:s4 * 4 + 2], mx2[:, s4 * E + 1:s4 * E + 2],
                                          AF.Exp, bias=cstat[:, s4 * 4:s4 * 4 + 1], scale=1.0)
                        ins = None
                        for g in tiles:
                            s4 = g % 4
                            ins = sc.activation(expl[:, s4 * E:(s4 + 1) * E], glog[:, s4 * E:(s4 + 1) * E],
                                                AF.Exp, bias=cstat[:, s4 * 4:s4 * 4 + 1], scale=1.0)
                        ss.inc(ins)
                    ss_cexp = ss.n

                    @blk.vector
                    def _(v, tiles=tiles, ss_cexp=ss_cexp):
                        v.wait_ge(ss.h, ss_cexp)
                        for g in tiles:
                            s4 = g % 4
                            v.tensor_scalar(cstat[:, s4 * 4 + 2:s4 * 4 + 3], cstat[:, s4 * 4 + 1:s4 * 4 + 2],
                                            1.0, None, ALU.add)
                        for g in tiles:
                            s4 = g % 4
                            v.reciprocal(cstat[:, s4 * 4 + 3:s4 * 4 + 4], cstat[:, s4 * 4 + 2:s4 * 4 + 3])
                        for g in tiles:
                            s4 = g % 4
                            v.tensor_scalar(maskt[:, s4 * E:(s4 + 1) * E], glog[:, s4 * E:(s4 + 1) * E],
                                            mx2[:, s4 * E + 1:s4 * E + 2], None, ALU.is_ge)
                        for g in tiles:
                            s4 = g % 4
                            v.tensor_copy(sel_full[:, g * E:(g + 1) * E], maskt[:, s4 * E:(s4 + 1) * E])
                        for g in tiles:
                            s4 = g % 4
                            v.tensor_tensor(comb[:, g * E:(g + 1) * E], expl[:, s4 * E:(s4 + 1) * E],
                                            maskt[:, s4 * E:(s4 + 1) * E], ALU.mult)
                        ins = None
                        for g in tiles:
                            s4 = g % 4
                            ins = v.tensor_scalar(comb[:, g * E:(g + 1) * E], comb[:, g * E:(g + 1) * E],
                                                  cstat[:, s4 * 4 + 3:s4 * 4 + 4], None, ALU.mult)
                        sv.inc(ins)

                qt_mm_done = {}
                NQT = T // 512

                @blk.tensor
                def _(te):
                    te.wait_ge(sv.h, evac_done[TT - 1])
                    te.wait_ge(ldg.h, ldg_wq)
                    ss_base = ss.n
                    for j in range(DC):
                        for n in range(NQT):
                            idx = j * NQT + n
                            if idx >= 2:
                                te.wait_ge(ss.h, ss_base + idx - 1)
                            ins = None
                            for dd in range(DC):
                                ins = te.matmul(qt_ps[idx % 2][:],
                                                wq_bf[:, dd * D + j * 128:dd * D + (j + 1) * 128],
                                                xnT_own[:, dd * T + n * 512:dd * T + (n + 1) * 512],
                                                start=(dd == 0), stop=(dd == DC - 1))
                            st.inc(ins)
                            qt_mm_done[idx] = st.n

                @blk.scalar
                def _(sc):
                    for j in range(DC):
                        for n in range(NQT):
                            idx = j * NQT + n
                            sc.wait_ge(st.h, qt_mm_done[idx])
                            ins = sc.activation(QT[:, j * T + n * 512:j * T + (n + 1) * 512],
                                                qt_ps[idx % 2][:], AF.Copy)
                            ss.inc(ins)
                ss_qt_all = ss.n

                @blk.scalar
                def _(sc):
                    for j in range(DC):
                        ins = sc.dma_start(out=xnT_scr[j, :, :], in_=xnT_own[:, j * T:(j + 1) * T])
                        wr.inc(ins, 16)
                wr_xnT = wr.n

            p2 = ExitStack()
            with p2:
                qn2_ps = p2.enter_context(nc.psum_tensor("qn2", [1, T], F32))
                qsq_done = {}
                qn_mm_done = {}
                for j in range(DC):
                    @blk.scalar
                    def _(sc, j=j):
                        if j == 0:
                            sc.wait_ge(ss.h, ss_qt_all)
                        else:
                            sc.wait_ge(st.h, qn_mm_done.get(j - 1, 0))
                        ins = sc.activation(qsq_bf[:], QT[:, j * T:(j + 1) * T], AF.Square)
                        ss.inc(ins)
                        qsq_done[j] = ss.n

                    @blk.tensor
                    def _(te, j=j):
                        te.wait_ge(ss.h, qsq_done[j])
                        ins = None
                        for n in range(T // 512):
                            ins = te.matmul(qn2_ps[:, n * 512:(n + 1) * 512],
                                            onesk_bf[:], qsq_bf[:, n * 512:(n + 1) * 512],
                                            start=(j == 0), stop=(j == DC - 1))
                        st.inc(ins)
                        qn_mm_done[j] = st.n

                @blk.vector
                def _(v):
                    v.wait_ge(st.h, qn_mm_done[DC - 1])
                    ins = v.reduce_max(qmax2[:, 0:1], qn2_ps[:], axis=AX.X)
                    sv.inc(ins)
            blk.__exit__(None, None, None)
            sv_qmax = sv.n

        g2 = ExitStack()
        with g2:
            xst = g2.enter_context(nc.sbuf_tensor("xst2", [128, 4 * D], F32))
            sq_scr = g2.enter_context(nc.sbuf_tensor("sq_scr2", [128, D], F32))
            stat = g2.enter_context(nc.sbuf_tensor("stat2", [128, 64], F32))
            xn_bf = g2.enter_context(nc.sbuf_tensor("xn_bf2", [128, 4 * D], BF16))
            wk_bf = g2.enter_context(nc.sbuf_tensor("wk_bf", [128, DC * D], BF16))
            wv_bf = g2.enter_context(nc.sbuf_tensor("wv_bf", [128, DC * D], BF16))
            xnT_kv = g2.enter_context(nc.sbuf_tensor("xnT_kv", [128, 2 * DC * 1024], BF16))
            ktb = g2.enter_context(nc.sbuf_tensor("ktb", [128, 2 * 1024], BF16))
            vtb = g2.enter_context(nc.sbuf_tensor("vtb", [128, 2 * 1024], BF16))
            ksq_bf = g2.enter_context(nc.sbuf_tensor("ksq_bf", [128, 2 * 1024], BF16))
            km_sb8 = g2.enter_context(nc.sbuf_tensor("km_sb8", [1, 8], F32))

            p3 = ExitStack()
            with p3:
                trbf_ps = [p3.enter_context(nc.psum_tensor(f"trbf2_{i}", [128, D], BF16)) for i in range(2)]
                kt_ps = [p3.enter_context(nc.psum_tensor(f"ktp{i}", [128, 512], F32)) for i in range(2)]
                v_ps = [p3.enter_context(nc.psum_tensor(f"vp{i}", [128, 512], F32)) for i in range(2)]
                kn2_ps = p3.enter_context(nc.psum_tensor("kn2", [1, 1024], F32))

                blk = nc.Block("gen_full"); blk.__enter__()

                @blk.gpsimd
                def _(g):
                    g.dma_start(
                        out=wk_bf[:].rearrange("p (j d) -> p j d", j=DC),
                        in_=wk[:].rearrange("(j p) d -> p j d", p=128),
                    ).then_inc(ldg.h, 16)
                    g.dma_start(
                        out=wv_bf[:].rearrange("p (j d) -> p j d", j=DC),
                        in_=wv[:].rearrange("(j p) d -> p j d", p=128),
                    ).then_inc(ldg.h, 16)
                    ldg.n += 32
                ldg_wkv = ldg.n

                ch_mm_done = {}
                ch_kmax_done = {}
                trevac_done = {}
                prev_xn_sv = [0]
                prev_tr_st = [0]
                NCHH = NCH // 2

                for ch in range(NCHH):
                    base = (ch % 2) * DC * 1024
                    for grp in range(2):
                        tiles = [ch * 8 + grp * 4 + i for i in range(4)]
                        xn_done = _ln_group(
                            blk,
                            [x_kv[g * 128:(g + 1) * 128, :] for g in tiles],
                            xst, sq_scr, stat, None, xn_bf, False,
                            [i for i in range(4)],
                            ld, sv, ss, eps_sb[:],
                            wait_loads=[(sv.h, prev_xn_sv[0])],
                            wait_xn=[(st.h, prev_tr_st[0])],
                        )
                        prev_xn_sv[0] = sv.n
                        for i in range(4):
                            tl = grp * 4 + i
                            @blk.tensor
                            def _(te, tl=tl, i=i, ch=ch, xn_done=xn_done, base=base):
                                te.wait_ge(sv.h, xn_done[i])
                                if tl >= 2:
                                    te.wait_ge(sv.h, trevac_done.get((ch, tl - 2), 0))
                                else:
                                    te.wait_ge(sv.h, trevac_done.get((ch - 1, 6 + tl), 0))
                                ins = None
                                for j in range(DC):
                                    ins = te.transpose(trbf_ps[tl % 2][:, j * 128:(j + 1) * 128],
                                                       xn_bf[:, i * D + j * 128:i * D + (j + 1) * 128],
                                                       id_bf[:])
                                st.inc(ins)
                            tr_done = st.n

                            @blk.vector
                            def _(v, tl=tl, tr_done=tr_done, ch=ch, base=base):
                                v.wait_ge(st.h, tr_done)
                                v.wait_ge(st.h, ch_mm_done.get(ch - 2, 0))
                                dst = xnT_kv[:, base:base + DC * 1024].rearrange(
                                    "p (j t) -> p j t", j=DC)[:, :, tl * 128:(tl + 1) * 128]
                                src = trbf_ps[tl % 2][:].rearrange("p (j c) -> p j c", j=DC)
                                ins = v.tensor_copy(dst, src)
                                sv.inc(ins)
                                trevac_done[(ch, tl)] = sv.n
                        prev_tr_st[0] = st.n
                    sv_evac_ch = sv.n

                    kt_mm_done = {}
                    kt_evac_actual = {}
                    kt_wr_done = {}
                    ksq_done = {}

                    @blk.tensor
                    def _(te, ch=ch, base=base, sv_evac_ch=sv_evac_ch):
                        te.wait_ge(sv.h, sv_evac_ch)
                        if ch == 0:
                            te.wait_ge(ldg.h, ldg_wkv)
                        ss_base = ss.n
                        for j in range(DC):
                            for n in range(2):
                                idx = j * 2 + n
                                if idx >= 2:
                                    te.wait_ge(ss.h, ss_base + idx - 1)
                                ins = None
                                for dd in range(DC):
                                    ins = te.matmul(kt_ps[idx % 2][:],
                                                    wk_bf[:, dd * D + j * 128:dd * D + (j + 1) * 128],
                                                    xnT_kv[:, base + dd * 1024 + n * 512:base + dd * 1024 + (n + 1) * 512],
                                                    start=(dd == 0), stop=(dd == DC - 1))
                                st.inc(ins)
                                kt_mm_done[idx] = st.n

                    @blk.scalar
                    def _(sc, ch=ch):
                        for j in range(DC):
                            for n in range(2):
                                idx = j * 2 + n
                                sc.wait_ge(st.h, kt_mm_done[idx])
                                if j >= 2:
                                    sc.wait_ge(wr.h, kt_wr_done.get(j - 2, 0))
                                    sc.wait_ge(sv.h, ksq_done.get(j - 2, 0))
                                ins = sc.activation(
                                    ktb[:, (j % 2) * 1024 + n * 512:(j % 2) * 1024 + (n + 1) * 512],
                                    kt_ps[idx % 2][:], AF.Copy)
                                ss.inc(ins)
                            kt_evac_actual[j] = ss.n
                            for p in range(2):
                                with sc.If_eq(_par, p):
                                    sc.dma_start(
                                        out=kT_sh[j, :, p * (S // 2) + ch * 1024:p * (S // 2) + (ch + 1) * 1024],
                                        in_=ktb[:, (j % 2) * 1024:(j % 2 + 1) * 1024],
                                    ).then_inc(wr.h, 16)
                            wr.n += 16
                            kt_wr_done[j] = wr.n

                    @blk.vector
                    def _(v, ch=ch):
                        for j in range(DC):
                            v.wait_ge(ss.h, kt_evac_actual[j])
                            ins = v.tensor_tensor(ksq_bf[:, (j % 2) * 1024:(j % 2 + 1) * 1024],
                                                  ktb[:, (j % 2) * 1024:(j % 2 + 1) * 1024],
                                                  ktb[:, (j % 2) * 1024:(j % 2 + 1) * 1024], ALU.mult)
                            sv.inc(ins)
                            ksq_done[j] = sv.n

                    @blk.tensor
                    def _(te, ch=ch):
                        te.wait_ge(sv.h, ch_kmax_done.get(ch - 1, 0))
                        ins = None
                        for j in range(DC):
                            te.wait_ge(sv.h, ksq_done[j])
                            for n in range(2):
                                ins = te.matmul(kn2_ps[:, n * 512:(n + 1) * 512],
                                                onesk_bf[:],
                                                ksq_bf[:, (j % 2) * 1024 + n * 512:(j % 2) * 1024 + (n + 1) * 512],
                                                start=(j == 0), stop=(j == DC - 1))
                        st.inc(ins)
                    kn_done = st.n

                    @blk.vector
                    def _(v, ch=ch, kn_done=kn_done):
                        v.wait_ge(st.h, kn_done)
                        ins = v.reduce_max(kmax2[:, 1:2], kn2_ps[:], axis=AX.X)
                        sv.inc(ins)
                        v.wait_ge(sv.h, sv.n)
                        ins = v.tensor_tensor(kmax2[:, 0:1], kmax2[:, 0:1], kmax2[:, 1:2], ALU.max)
                        sv.inc(ins)
                        ch_kmax_done[ch] = sv.n

                    v_mm_done = {}
                    v_evac_actual = {}
                    v_wr_done = {}

                    @blk.tensor
                    def _(te, ch=ch, base=base):
                        ss_base = ss.n
                        for tl in range(8):
                            for n in range(2):
                                idx = tl * 2 + n
                                if idx >= 2:
                                    te.wait_ge(ss.h, ss_base + idx - 1)
                                ins = None
                                for dd in range(DC):
                                    ins = te.matmul(v_ps[idx % 2][:],
                                                    xnT_kv[:, base + dd * 1024 + tl * 128:base + dd * 1024 + tl * 128 + 128],
                                                    wv_bf[:, dd * D + n * 512:dd * D + (n + 1) * 512],
                                                    start=(dd == 0), stop=(dd == DC - 1))
                                st.inc(ins)
                                v_mm_done[idx] = st.n

                    @blk.scalar
                    def _(sc, ch=ch):
                        for tl in range(8):
                            for n in range(2):
                                idx = tl * 2 + n
                                sc.wait_ge(st.h, v_mm_done[idx])
                                if tl >= 2:
                                    sc.wait_ge(wr.h, v_wr_done.get(tl - 2, 0))
                                ins = sc.activation(
                                    vtb[:, (tl % 2) * 1024 + n * 512:(tl % 2) * 1024 + (n + 1) * 512],
                                    v_ps[idx % 2][:], AF.Copy)
                                ss.inc(ins)
                            row = (ch * 8 + tl) * 128
                            for p in range(2):
                                with sc.If_eq(_par, p):
                                    sc.dma_start(
                                        out=v_sh[p * (S // 2) + row:p * (S // 2) + row + 128, :],
                                        in_=vtb[:, (tl % 2) * 1024:(tl % 2 + 1) * 1024],
                                    ).then_inc(wr.h, 16)
                            wr.n += 16
                            v_wr_done[tl] = wr.n
                    ch_mm_done[ch] = st.n
                wr_scr_all = wr.n

                @blk.vector
                def _(v):
                    v.wait_ge(sv.h, ch_kmax_done[NCHH - 1])
                    ins = v.memset(km_sb8[:], 0.0)
                    sv.inc(ins)
                    v.wait_ge(sv.h, sv.n)
                    ins = v.tensor_copy(km_sb8[:, 0:1], kmax2[:, 0:1])
                    sv.inc(ins)
                sv_km8 = sv.n

                @blk.scalar
                def _(sc):
                    sc.wait_ge(sv.h, sv_km8)
                    ins = sc.dma_start(out=km_in[:], in_=km_sb8[:])
                    wr.inc(ins, 16)
                wr_all_kv = wr.n

                @blk.gpsimd
                def _(g):
                    g.wait_ge(wr.h, wr_all_kv)
                    g.collective_compute(
                        "AllReduce", ALU.max,
                        replica_groups=[[2 * i, 2 * i + 1] for i in range(NCORES // 2)],
                        ins=[km_in.ap().opt()], outs=[km_out.ap().opt()],
                    ).then_inc(cc.h, 1)
                    cc.n += 1
                cc_bar = cc.n

                blk.__exit__(None, None, None)

        a1 = ExitStack()
        with a1:
            kt_tile = a1.enter_context(nc.sbuf_tensor("kt_tile", [128, 2 * DC * 1024], BF16))
            v_tile = a1.enter_context(nc.sbuf_tensor("v_tile", [128, 2 * 8 * 1024], BF16))
            expS = a1.enter_context(nc.sbuf_tensor("expS", [128, 2 * 8 * 512], BF16))
            x_res = a1.enter_context(nc.sbuf_tensor("x_res", [128, 2 * D], F32))
            rzc = a1.enter_context(nc.sbuf_tensor("rzc", [128, TT], F32))
            kmg8 = a1.enter_context(nc.sbuf_tensor("kmg8", [1, 8], F32))

            p5 = ExitStack()
            with p5:
                s_ps = [p5.enter_context(nc.psum_tensor(f"sps{i}", [128, 512], F32)) for i in range(2)]
                o_ps = [p5.enter_context(nc.psum_tensor(f"ops{i}", [128, 1026], F32)) for i in range(2)]

                blk = nc.Block("attn"); blk.__enter__()

                kv_loaded = {}
                kv_mm_done = {}

                def _kv_load_body(kvt):
                    @blk.sync
                    def _(sync, kvt=kvt):
                        sync.wait_ge(st.h, kv_mm_done.get(kvt - 2, 0))
                        if kvt == 0:
                            sync.wait_ge(wr.h, wr_all_kv)
                        half = kvt % 2
                        for p in range(2):
                            phys = (kvt + 8 * p) % NCH
                            with sync.If_eq(_parsp, p):
                                sync.dma_start(
                                    out=kt_tile[:, half * DC * 1024:(half + 1) * DC * 1024].rearrange("p (j k) -> p j k", j=DC),
                                    in_=kT_sh[:, :, phys * 1024:(phys + 1) * 1024].rearrange("j p k -> p j k"),
                                ).then_inc(ld.h, 16)
                                sync.dma_start(
                                    out=v_tile[:, half * 8 * 1024:(half + 1) * 8 * 1024].rearrange("p (g d) -> p g d", g=8),
                                    in_=v_sh[phys * 1024:(phys + 1) * 1024, :].rearrange("(g p) d -> p g d", p=128),
                                ).then_inc(ld.h, 16)
                        ld.n += 32
                    kv_loaded[kvt] = ld.n

                _kv_load_body(0)
                _kv_load_body(1)

                @blk.sync
                def _(sync):
                    sync.wait_ge(cc.h, cc_bar)
                    sync.dma_start(out=kmg8[:], in_=km_out[:]).then_inc(ld.h, 16)
                    ld.n += 16
                ld_kmg = ld.n

                @blk.scalar
                def _(sc):
                    sc.wait_ge(ld.h, ld_kmg)
                    sc.wait_ge(sv.h, sv_qmax)
                    ins = sc.activation(bstat[:, 0:1], qmax2[:, 0:1], AF.Ln)
                    ss.inc(ins)
                    sc.wait_ge(ss.h, ss.n)
                    ins = sc.activation(bstat[:, 1:2], kmg8[:, 0:1], AF.Ln)
                    ss.inc(ins)
                ss_lnk = ss.n

                @blk.vector
                def _(v):
                    v.wait_ge(ss.h, ss_lnk)
                    ins = v.tensor_tensor(bstat[:, 2:3], bstat[:, 0:1], bstat[:, 1:2], ALU.add)
                    sv.inc(ins)
                sv_lnsum = sv.n

                @blk.scalar
                def _(sc):
                    sc.wait_ge(sv.h, sv_lnsum)
                    ins = sc.activation(bstat[:, 3:4], bstat[:, 2:3], AF.Exp, bias=nl32_sb[:], scale=0.5)
                    ss.inc(ins)
                ss_B = ss.n

                @blk.vector
                def _(v):
                    v.wait_ge(ss.h, ss_B)
                    ins = v.tensor_scalar(bstat[:, 4:5], bstat[:, 3:4], -1.0, None, ALU.mult)
                    sv.inc(ins)
                    v.wait_ge(sv.h, sv.n)
                    ins = v.tensor_copy(nb_bf[:], bstat[0:1, 4:5])
                    sv.inc(ins)
                sv_nbbf = sv.n

                @blk.tensor
                def _(te):
                    te.wait_ge(sv.h, sv_nbbf)
                    ins = te.matmul(s_ps[0][:, 0:1], ones1_bf[:], nb_bf[:], start=True, stop=True)
                    st.inc(ins)
                st_nb = st.n

                @blk.vector
                def _(v):
                    v.wait_ge(st.h, st_nb)
                    ins = v.tensor_copy(negB[:], s_ps[0][:, 0:1])
                    sv.inc(ins)
                sv_negB = sv.n

                exp_done = {}
                pv_done = {}
                flush_done = {}

                for kvt in range(NCH):
                    if kvt >= 2:
                        _kv_load_body(kvt)

                    for qs in range(4):
                        ebase = (qs % 2) * 8 * 512
                        smm_done = {}

                        @blk.tensor
                        def _(te, kvt=kvt, qs=qs, smm_done=smm_done):
                            te.wait_ge(ld.h, kv_loaded[kvt])
                            half = (kvt % 2) * DC * 1024
                            ss_base = ss.n
                            for kg in range(8):
                                if kg >= 2:
                                    te.wait_ge(ss.h, ss_base + kg - 1)
                                else:
                                    key = (kvt, qs - 1, 6 + kg) if qs >= 1 else (kvt - 1, 3, 6 + kg)
                                    if key in exp_done:
                                        te.wait_ge(ss.h, exp_done[key])
                                ins = None
                                for dd in range(DC):
                                    ins = te.matmul(s_ps[kg % 2][:],
                                                    kt_tile[:, half + dd * 1024 + kg * 128:half + dd * 1024 + (kg + 1) * 128],
                                                    QT[:, dd * T + qs * 512:dd * T + (qs + 1) * 512],
                                                    start=(dd == 0), stop=(dd == DC - 1))
                                st.inc(ins)
                                smm_done[kg] = st.n

                        @blk.scalar
                        def _(sc, kvt=kvt, qs=qs, ebase=ebase, smm_done=smm_done):
                            if kvt == 0 and qs == 0:
                                sc.wait_ge(sv.h, sv_negB)
                            for kg in range(8):
                                sc.wait_ge(st.h, smm_done[kg])
                                ins = sc.activation(expS[:, ebase + kg * 512:ebase + (kg + 1) * 512],
                                                    s_ps[kg % 2][:], AF.Exp,
                                                    bias=negB[:], scale=SCALE)
                                ss.inc(ins)
                                exp_done[(kvt, qs, kg)] = ss.n

                        @blk.tensor
                        def _(te, kvt=kvt, qs=qs, ebase=ebase):
                            te.wait_ge(ss.h, exp_done[(kvt, qs, 7)])
                            half = (kvt % 2) * 8 * 1024
                            sv_base = sv.n
                            for qt in range(4):
                                if qt >= 2:
                                    te.wait_ge(sv.h, sv_base + qt - 1)
                                else:
                                    key = (kvt, qs - 1, 2 + qt) if qs >= 1 else (kvt - 1, 3, 2 + qt)
                                    if key in flush_done:
                                        te.wait_ge(sv.h, flush_done[key])
                                ins = None
                                for kg in range(8):
                                    lhs = expS[:, ebase + kg * 512 + qt * 128:ebase + kg * 512 + (qt + 1) * 128]
                                    te.matmul(o_ps[qt % 2][:, 0:512], lhs,
                                              v_tile[:, half + kg * 1024:half + kg * 1024 + 512],
                                              start=(kg == 0), stop=(kg == 7))
                                    te.matmul(o_ps[qt % 2][:, 512:1024], lhs,
                                              v_tile[:, half + kg * 1024 + 512:half + kg * 1024 + 1024],
                                              start=(kg == 0), stop=(kg == 7))
                                    ins = te.matmul(o_ps[qt % 2][:, 1024:1026], lhs,
                                                    ones2_bf[:],
                                                    start=(kg == 0), stop=(kg == 7))
                                st.inc(ins)
                                pv_done[(kvt, qs, qt)] = st.n

                        @blk.vector
                        def _(v, kvt=kvt, qs=qs):
                            for qt in range(4):
                                v.wait_ge(st.h, pv_done[(kvt, qs, qt)])
                                gt = qs * 4 + qt
                                ins = v.tensor_tensor(O_sb[:, gt * 1026:(gt + 1) * 1026],
                                                      O_sb[:, gt * 1026:(gt + 1) * 1026],
                                                      o_ps[qt % 2][:], ALU.add)
                                sv.inc(ins)
                                flush_done[(kvt, qs, qt)] = sv.n
                    kv_mm_done[kvt] = st.n

                last_flush = sv.n

                @blk.vector
                def _(v):
                    v.wait_ge(sv.h, last_flush)
                    ins = None
                    for gt in range(TT):
                        ins = v.reciprocal(rzc[:, gt:gt + 1], O_sb[:, gt * 1026 + 1024:gt * 1026 + 1025])
                    sv.inc(ins)
                sv_rz = sv.n

                xres_done = {}
                fin_done = {}
                for gt in range(TT):
                    @blk.sync
                    def _(sync, gt=gt):
                        sync.wait_ge(sv.h, fin_done.get(gt - 2, 0))
                        sync.dma_start(out=x_res[:, (gt % 2) * D:(gt % 2 + 1) * D],
                                       in_=x_own[gt * 128:(gt + 1) * 128, :]).then_inc(ld.h, 16)
                        ld.n += 16
                    xres_done[gt] = ld.n

                    @blk.vector
                    def _(v, gt=gt):
                        v.wait_ge(ld.h, xres_done[gt])
                        v.wait_ge(sv.h, sv_rz)
                        ins = v.scalar_tensor_tensor(O_sb[:, gt * 1026:gt * 1026 + 1024],
                                                     O_sb[:, gt * 1026:gt * 1026 + 1024],
                                                     rzc[:, gt:gt + 1],
                                                     x_res[:, (gt % 2) * D:(gt % 2 + 1) * D],
                                                     ALU.mult, ALU.add)
                        sv.inc(ins)
                        fin_done[gt] = sv.n
                blk.__exit__(None, None, None)

        qt_guard.__exit__(None, None, None)

        C = 640
        SC5 = C // 128
        FQ = 4
        f1 = ExitStack()
        with f1:
            iota_f = f1.enter_context(nc.sbuf_tensor("iota_f", [128, C], F32))
            comb_bf = f1.enter_context(nc.sbuf_tensor("comb_bf", [128, TT * E], BF16))
            posb = f1.enter_context(nc.sbuf_tensor("posb", [128, TT * E], F32))
            comb_sel = f1.enter_context(nc.sbuf_tensor("comb_sel", [128, 8], F32))
            b1_sb = f1.enter_context(nc.sbuf_tensor("b1_sb", [128, E * (FF // 128)], F32))
            b2e = f1.enter_context(nc.sbuf_tensor("b2e", [1, D], BF16))
            b1_rm = f1.enter_context(nc.sbuf_tensor("b1_rm", [128, 256], F32))

            def E_view(tt):
                return u20[:, tt * C:(tt + 1) * C]

            def hq_view(fc):
                return u20[:, fc * C:(fc + 1) * C]

            def ybf_view(st5):
                return u20[:, 8 * C + st5 * 1024:8 * C + (st5 + 1) * 1024]

            blk = nc.Block("ffn")
            blk.__enter__()

            ppre = ExitStack()
            with ppre:
                b1t_ps = ppre.enter_context(nc.psum_tensor("b1t_ps", [128, 256], F32))
                trp8 = ppre.enter_context(nc.psum_tensor("trp8", [8, 128], F32))
                trpb = ppre.enter_context(nc.psum_tensor("trpb", [128, 8], F32))
                selT = ppre.enter_context(nc.sbuf_tensor("selT", [8, T], F32))
                ones8 = ppre.enter_context(nc.sbuf_tensor("ones8", [8, T], F32))
                mask8 = ppre.enter_context(nc.sbuf_tensor("mask8", [8, T], F32))
                posT8 = ppre.enter_context(nc.sbuf_tensor("posT8", [8, T], F32))

                @blk.sync
                def _(sync):
                    sync.dma_start(
                        out=b1_rm[:].rearrange("q (m p) -> q m p", m=2),
                        in_=b1[:].rearrange("e f -> (e f)").rearrange("(m q p) -> q m p", m=2, p=128),
                    ).then_inc(ld.h, 16)
                    ld.n += 16
                ld_ffn_pre = ld.n

                @blk.gpsimd
                def _(g):
                    ins = g.iota(iota_f[:], [[1, C]], channel_multiplier=0,
                                 allow_small_or_imprecise_dtypes=True)
                    sg.inc(ins)
                sg_iota = sg.n

                @blk.tensor
                def _(te):
                    te.wait_ge(ld.h, ld_ffn_pre)
                    te.transpose(b1t_ps[:, 0:128], b1_rm[:, 0:128], id_32[:])
                    ins = te.transpose(b1t_ps[:, 128:256], b1_rm[:, 128:256], id_32[:])
                    st.inc(ins)
                st_b1t = st.n

                @blk.vector
                def _(v):
                    v.wait_ge(st.h, st_b1t)
                    v.tensor_copy(b1_sb[:], b1t_ps[:])
                    v.memset(ones8[:], 1.0)
                    ins = v.tensor_copy(comb_bf[:], comb[:])
                    sv.inc(ins)
                sv_sel = sv.n

                seltr_done = {}
                for tt in range(TT):
                    @blk.tensor
                    def _(te, tt=tt):
                        te.wait_ge(sv.h, sv_sel)
                        te.wait_ge(sv.h, seltr_done.get(tt - 1, 0))
                        ins = te.transpose(trp8[:], sel_full[:, tt * E:(tt + 1) * E], id_32[:])
                        st.inc(ins)
                    st_tr = st.n

                    @blk.vector
                    def _(v, tt=tt, st_tr=st_tr):
                        v.wait_ge(st.h, st_tr)
                        ins = v.tensor_copy(selT[:, tt * 128:(tt + 1) * 128], trp8[:])
                        sv.inc(ins)
                        seltr_done[tt] = sv.n

                @blk.vector
                def _(v):
                    v.wait_ge(sv.h, seltr_done[TT - 1])
                    v.tensor_tensor_scan(posT8[:], ones8[:], selT[:], 0.0, ALU.mult, ALU.add).then_inc(sv.h, 1)
                    sv.n += 1
                    v.wait_ge(sv.h, sv.n)
                    v.tensor_tensor(posT8[:], posT8[:], selT[:], ALU.subtract).then_inc(sv.h, 1)
                    sv.n += 1
                    v.wait_ge(sv.h, sv.n)
                    v.tensor_scalar(mask8[:], selT[:], -1.0e9, 1.0e9, ALU.mult, ALU.add).then_inc(sv.h, 1)
                    sv.n += 1
                    v.wait_ge(sv.h, sv.n)
                    ins = v.tensor_tensor(posT8[:], posT8[:], mask8[:], ALU.add)
                    sv.inc(ins)
                sv_posT = sv.n

                postr_done = {}
                for tt in range(TT):
                    @blk.tensor
                    def _(te, tt=tt):
                        te.wait_ge(sv.h, sv_posT)
                        te.wait_ge(sv.h, postr_done.get(tt - 1, 0))
                        ins = te.transpose(trpb[:], posT8[:, tt * 128:(tt + 1) * 128], id_32[0:8, 0:8])
                        st.inc(ins)
                    st_tr = st.n

                    @blk.vector
                    def _(v, tt=tt, st_tr=st_tr):
                        v.wait_ge(st.h, st_tr)
                        ins = v.tensor_copy(posb[:, tt * E:(tt + 1) * E], trpb[:])
                        sv.inc(ins)
                        postr_done[tt] = sv.n
            sv_ffn_pre = sv.n

            xn_nat = f1.enter_context(nc.sbuf_tensor("xn_nat", [128, TT * D], BF16))
            u20 = f1.enter_context(nc.sbuf_tensor("u20", [128, TT * C], BF16))
            ET_sb = f1.enter_context(nc.sbuf_tensor("ET_sb", [128, SC5 * T], BF16))
            xn_selT = f1.enter_context(nc.sbuf_tensor("xn_selT", [128, DC * C], BF16))
            y_f32 = f1.enter_context(nc.sbuf_tensor("y_f32", [128, SC5 * 1024], F32))
            w1c = f1.enter_context(nc.sbuf_tensor("w1c", [128, 4 * DC * 128], BF16))
            w2q = f1.enter_context(nc.sbuf_tensor("w2q", [128, 8 * 1024], BF16))

            @blk.sync
            def _(sync):
                sync.wait_ge(sv.h, sv_ffn_pre)
                sync.wait_ge(wr.h, wr_xnT)
                sync.dma_start(
                    out=xn_nat[:].rearrange("p (g d) -> p g d", g=TT),
                    in_=xn_scr[:].rearrange("g p d -> p g d"),
                ).then_inc(ld.h, 16)
                ld.n += 16
            ld_xn_nat = ld.n

            p6 = ExitStack()
            with p6:
                hy_ps = [p6.enter_context(nc.psum_tensor(f"hy{i}", [128, 1024], F32)) for i in range(2)]
                trE_ps = [p6.enter_context(nc.psum_tensor(f"trE_ps{i}", [128, C], BF16)) for i in range(2)]
                csel_ps = p6.enter_context(nc.psum_tensor("csel_ps", [128, 8], F32))

                scatter_done = {0: 0}
                oflush_all = {0: 0}
                last_y_done = [0]
                prev_h_done = [0]
                prev_scmm = [None]

                for e in range(E):
                    if prev_scmm[0] is not None:
                        @blk.vector
                        def _(v, e=e, scmm_prev=prev_scmm[0]):
                            for tt in range(TT):
                                v.wait_ge(st.h, scmm_prev[tt])
                                ins = v.tensor_tensor(O_sb[:, tt * 1026:tt * 1026 + 1024],
                                                      O_sb[:, tt * 1026:tt * 1026 + 1024],
                                                      hy_ps[tt % 2][:], ALU.add)
                                sv.inc(ins)
                    oflush_all[e] = sv.n

                    @blk.vector
                    def _(v, e=e):
                        if e == 0:
                            v.wait_ge(sv.h, sv_ffn_pre)
                            v.wait_ge(sg.h, sg_iota)
                            v.wait_ge(ld.h, ld_ffn_pre)
                        for tt in range(8):
                            v.tensor_scalar(E_view(tt), iota_f[:],
                                            posb[:, tt * E + e:tt * E + e + 1],
                                            None, ALU.is_equal)

                    @blk.vector
                    def _(v, e=e):
                        v.wait_ge(st.h, scatter_done[e])
                        ins = None
                        for tt in range(8, TT):
                            ins = v.tensor_scalar(E_view(tt), iota_f[:],
                                                  posb[:, tt * E + e:tt * E + e + 1],
                                                  None, ALU.is_equal)
                        sv.inc(ins)
                    E_done = sv.n

                    gth_mm = {}

                    @blk.tensor
                    def _(te, e=e, E_done=E_done, gth_mm=gth_mm):
                        te.wait_ge(sv.h, E_done)
                        te.wait_ge(sv.h, oflush_all[e])
                        if e == 0:
                            te.wait_ge(ld.h, ld_xn_nat)
                        ss_base = ss.n
                        for j in range(DC):
                            if j >= 2:
                                te.wait_ge(ss.h, ss_base + j - 1)
                            g_ps = hy_ps[j % 2]
                            ins = None
                            for tt in range(TT):
                                te.matmul(g_ps[:, 0:512],
                                          xn_nat[:, tt * D + j * 128:tt * D + (j + 1) * 128],
                                          u20[:, tt * C:tt * C + 512],
                                          start=(tt == 0), stop=(tt == TT - 1))
                                ins = te.matmul(g_ps[:, 512:C],
                                                xn_nat[:, tt * D + j * 128:tt * D + (j + 1) * 128],
                                                u20[:, tt * C + 512:(tt + 1) * C],
                                                start=(tt == 0), stop=(tt == TT - 1))
                            st.inc(ins)
                            gth_mm[j] = st.n
                        for st5 in range(SC5):
                            ins = None
                            for tt in range(TT):
                                ins = te.matmul(csel_ps[:, st5:st5 + 1],
                                                u20[:, tt * C + st5 * 128:tt * C + (st5 + 1) * 128],
                                                comb_bf[:, tt * E + e:tt * E + e + 1],
                                                start=(tt == 0), stop=(tt == TT - 1))
                            st.inc(ins)
                        gth_mm["csel"] = st.n
                        svb = sv.n
                        for tt in range(TT):
                            if tt >= 2:
                                te.wait_ge(sv.h, svb + tt)
                            ins = None
                            for sc5 in range(SC5):
                                ins = te.transpose(trE_ps[tt % 2][:, sc5 * 128:(sc5 + 1) * 128],
                                                   u20[:, tt * C + sc5 * 128:tt * C + (sc5 + 1) * 128],
                                                   id_bf[:])
                            st.inc(ins)
                            gth_mm[("et", tt)] = st.n

                    @blk.scalar
                    def _(sc, e=e, gth_mm=gth_mm):
                        if e == 0:
                            sc.wait_ge(sv.h, sv_sel)
                        for j in range(DC):
                            sc.wait_ge(st.h, gth_mm[j])
                            ins = sc.activation(xn_selT[:, j * C:(j + 1) * C],
                                                hy_ps[j % 2][:, 0:C], AF.Copy)
                            ss.inc(ins)
                    gth_evac = ss.n

                    @blk.vector
                    def _(v, e=e, gth_mm=gth_mm):
                        v.wait_ge(st.h, gth_mm["csel"])
                        ins = v.tensor_copy(comb_sel[:, 0:SC5], csel_ps[:, 0:SC5])
                        sv.inc(ins)
                        for tt in range(TT):
                            v.wait_ge(st.h, gth_mm[("et", tt)])
                            dst = ET_sb[:].rearrange("p (s t) -> p s t", s=SC5)[:, :, tt * 128:(tt + 1) * 128]
                            src = trE_ps[tt % 2][:].rearrange("p (s c) -> p s c", c=128)
                            ins = v.tensor_copy(dst, src)
                            sv.inc(ins)
                    et_evac = sv.n
                    csel_evac_sv = et_evac - TT

                    yflush = {}
                    for fb in range(FQ):
                        hmm_d = {}
                        w1_ld = {}

                        @blk.gpsimd
                        def _(g, e=e, fb=fb, w1_ld=w1_ld):
                            if e == 0 and fb == 0:
                                g.wait_ge(sv.h, sv_ffn_pre)
                            g.wait_ge(st.h, prev_h_done[0])
                            st_base_g = st.n
                            for fc in range(4):
                                ins = g.dma_start(
                                    out=w1c[:, (fc % 4) * DC * 128:((fc % 4) + 1) * DC * 128].rearrange("p (j f) -> p j f", j=DC),
                                    in_=w1[e, :, fb * 1024 + fc * 128:fb * 1024 + (fc + 1) * 128].rearrange("(j p) f -> p j f", p=128),
                                )
                                ldg.inc(ins, 16)
                                w1_ld[fc] = ldg.n
                            if fb == 0:
                                ins = g.dma_start(out=b2e[:], in_=b2[e][None, :])
                                ldg.inc(ins, 16)
                            g.wait_ge(st.h, last_y_done[0])
                            ins = g.dma_start(
                                out=w2q[:].rearrange("p (c d) -> p c d", c=8),
                                in_=w2[e, fb * 1024:(fb + 1) * 1024, :].rearrange("(c p) d -> p c d", p=128),
                            )
                            ldg.inc(ins, 16)
                            for fc in range(4, 8):
                                g.wait_ge(st.h, st_base_g + fc - 3)
                                ins = g.dma_start(
                                    out=w1c[:, (fc % 4) * DC * 128:((fc % 4) + 1) * DC * 128].rearrange("p (j f) -> p j f", j=DC),
                                    in_=w1[e, :, fb * 1024 + fc * 128:fb * 1024 + (fc + 1) * 128].rearrange("(j p) f -> p j f", p=128),
                                )
                                ldg.inc(ins, 16)
                                w1_ld[fc] = ldg.n
                        w2_ld = ldg.n

                        @blk.tensor
                        def _(te, e=e, fb=fb, hmm_d=hmm_d, w1_ld=w1_ld,
                              gth_evac=gth_evac, et_evac=et_evac, yflush=yflush):
                            ss_base = ss.n
                            for fc in range(8):
                                te.wait_ge(ldg.h, w1_ld[fc])
                                if fb == 0 and fc == 0:
                                    te.wait_ge(ss.h, gth_evac)
                                    te.wait_ge(sv.h, et_evac)
                                if fb > 0 and fc < 2:
                                    te.wait_ge(sv.h, yflush[(fb - 1, 4 - fc)])
                                if fc >= 2:
                                    te.wait_ge(ss.h, ss_base + fc - 1)
                                h_ps = hy_ps[fc % 2]
                                ins = None
                                for n2 in range(2):
                                    lo = n2 * 512
                                    hi = min(C, lo + 512)
                                    for dd in range(DC):
                                        ins = te.matmul(h_ps[:, lo:hi],
                                                        w1c[:, (fc % 4) * DC * 128 + dd * 128:(fc % 4) * DC * 128 + (dd + 1) * 128],
                                                        xn_selT[:, dd * C + lo:dd * C + hi],
                                                        start=(dd == 0), stop=(dd == DC - 1))
                                st.inc(ins)
                                hmm_d[fc] = st.n
                        prev_h_done[0] = st.n

                        @blk.scalar
                        def _(sc, e=e, fb=fb, hmm_d=hmm_d):
                            for fc in range(8):
                                sc.wait_ge(st.h, hmm_d[fc])
                                bcol = e * (FF // 128) + fb * 8 + fc
                                ins = sc.activation(hq_view(fc), hy_ps[fc % 2][:, 0:C],
                                                    AF.Gelu, bias=b1_sb[:, bcol:bcol + 1], scale=1.0)
                                ss.inc(ins)
                        gelu_all = ss.n

                        ymm_d = {}

                        @blk.tensor
                        def _(te, e=e, fb=fb, gelu_all=gelu_all, ymm_d=ymm_d,
                              w2_ld=w2_ld, yflush=yflush):
                            te.wait_ge(ss.h, gelu_all)
                            te.wait_ge(ldg.h, w2_ld)
                            sv_base = sv.n
                            for st5 in range(SC5):
                                if st5 >= 2:
                                    te.wait_ge(sv.h, sv_base + st5 - 1)
                                elif fb > 0:
                                    te.wait_ge(sv.h, yflush.get((fb - 1, 4 - st5), 0))
                                y_ps = hy_ps[st5 % 2]
                                first0 = first1 = True
                                if fb == 0:
                                    te.matmul(y_ps[:, 0:512], ones1_bf[:], b2e[:, 0:512],
                                              start=True, stop=False)
                                    te.matmul(y_ps[:, 512:1024], ones1_bf[:], b2e[:, 512:1024],
                                              start=True, stop=False)
                                    first0 = first1 = False
                                ins = None
                                for fc in range(8):
                                    lhs = u20[:, fc * C + st5 * 128:fc * C + (st5 + 1) * 128]
                                    te.matmul(y_ps[:, 0:512], lhs, w2q[:, fc * 1024:fc * 1024 + 512],
                                              start=first0, stop=(fc == 7))
                                    ins = te.matmul(y_ps[:, 512:1024], lhs,
                                                    w2q[:, fc * 1024 + 512:fc * 1024 + 1024],
                                                    start=first1, stop=(fc == 7))
                                    first0 = first1 = False
                                st.inc(ins)
                                ymm_d[st5] = st.n
                        last_y_done[0] = st.n

                        @blk.vector
                        def _(v, e=e, fb=fb, ymm_d=ymm_d, yflush=yflush):
                            for st5 in range(SC5):
                                v.wait_ge(st.h, ymm_d[st5])
                                if fb == 0:
                                    ins = v.tensor_copy(y_f32[:, st5 * 1024:(st5 + 1) * 1024],
                                                        hy_ps[st5 % 2][:])
                                else:
                                    ins = v.tensor_tensor(y_f32[:, st5 * 1024:(st5 + 1) * 1024],
                                                          y_f32[:, st5 * 1024:(st5 + 1) * 1024],
                                                          hy_ps[st5 % 2][:], ALU.add)
                                sv.inc(ins)
                                yflush[(fb, st5)] = sv.n

                    @blk.vector
                    def _(v, e=e, csel_evac_sv=csel_evac_sv):
                        v.wait_ge(sv.h, csel_evac_sv)
                        ins = None
                        for st5 in range(SC5):
                            ins = v.tensor_scalar(ybf_view(st5), y_f32[:, st5 * 1024:(st5 + 1) * 1024],
                                                  comb_sel[:, st5:st5 + 1], None, ALU.mult)
                        sv.inc(ins)
                    ybf_done = sv.n

                    scmm_d = {}

                    @blk.tensor
                    def _(te, e=e, ybf_done=ybf_done, scmm_d=scmm_d):
                        te.wait_ge(sv.h, ybf_done)
                        sv_base = sv.n
                        for tt in range(TT):
                            if tt >= 2:
                                te.wait_ge(sv.h, sv_base + tt - 1)
                            s_ps2 = hy_ps[tt % 2]
                            ins = None
                            for sc5 in range(SC5):
                                lhs = ET_sb[:, sc5 * T + tt * 128:sc5 * T + (tt + 1) * 128]
                                te.matmul(s_ps2[:, 0:512], lhs,
                                          u20[:, 8 * C + sc5 * 1024:8 * C + sc5 * 1024 + 512],
                                          start=(sc5 == 0), stop=(sc5 == SC5 - 1))
                                ins = te.matmul(s_ps2[:, 512:1024], lhs,
                                                u20[:, 8 * C + sc5 * 1024 + 512:8 * C + (sc5 + 1) * 1024],
                                                start=(sc5 == 0), stop=(sc5 == SC5 - 1))
                            st.inc(ins)
                            scmm_d[tt] = st.n
                        scatter_done[e + 1] = st.n

                    prev_scmm[0] = scmm_d

                @blk.vector
                def _(v, scmm_prev=prev_scmm[0]):
                    for tt in range(TT):
                        v.wait_ge(st.h, scmm_prev[tt])
                        ins = v.tensor_tensor(O_sb[:, tt * 1026:tt * 1026 + 1024],
                                              O_sb[:, tt * 1026:tt * 1026 + 1024],
                                              hy_ps[tt % 2][:], ALU.add)
                        sv.inc(ins)
                ffn_all = sv.n

                @blk.sync
                def _(sync):
                    sync.wait_ge(sv.h, ffn_all)
                    sync.dma_start(
                        out=out[:].rearrange("(t p) d -> p t d", p=128),
                        in_=O_sb[:].rearrange("p (t c) -> p t c", c=1026)[:, :, 0:1024],
                    ).then_inc(so.h, 16)
                    so.n += 16
                    sync.wait_ge(so.h, so.n)

                @blk.gpsimd
                def _(g):
                    g.wait_ge(so.h, so.n)

                blk.__exit__(None, None, None)

    return nc


_NC_CACHE = [None]


def kernel(L0_attn_x, L0_attn_W_q, L0_attn_W_k, L0_attn_W_v,
           L0_ffn_gate_w, L0_ffn_w1, L0_ffn_b1, L0_ffn_w2, L0_ffn_b2):
    x = np.ascontiguousarray(np.asarray(L0_attn_x, dtype=np.float32).reshape(S, D))
    wq_np = np.ascontiguousarray(np.asarray(L0_attn_W_q, dtype=np.float32))
    wk_np = np.ascontiguousarray(np.asarray(L0_attn_W_k, dtype=np.float32))
    wv_np = np.ascontiguousarray(np.asarray(L0_attn_W_v, dtype=np.float32))
    gw_np = np.ascontiguousarray(np.asarray(L0_ffn_gate_w, dtype=np.float32))
    w1_np = np.ascontiguousarray(np.asarray(L0_ffn_w1, dtype=np.float32))
    b1_np = np.ascontiguousarray(np.asarray(L0_ffn_b1, dtype=np.float32))
    w2_np = np.ascontiguousarray(np.asarray(L0_ffn_w2, dtype=np.float32))
    b2_np = np.ascontiguousarray(np.asarray(L0_ffn_b2, dtype=np.float32))

    if _NC_CACHE[0] is None:
        _NC_CACHE[0] = build_nc()
    nc = _NC_CACHE[0]

    in_maps = []
    for c in range(NCORES):
        in_maps.append({
            "x_own": np.ascontiguousarray(x[c * T:(c + 1) * T]),
            "x_kv": np.ascontiguousarray(x[(c % 2) * (S // 2):((c % 2) + 1) * (S // 2)]),
            "wq": wq_np, "wk": wk_np, "wv": wv_np,
            "gate_w": gw_np,
            "w1": w1_np, "b1": b1_np, "w2": w2_np, "b2": b2_np,
        })
    res = run_bass_kernel_spmd(nc, in_maps, core_ids=list(range(NCORES)))
    shards = [res.results[c]["out"] for c in range(NCORES)]
    return np.concatenate(shards, axis=0).reshape(1, S, D).astype(np.float32, copy=False)


# version 16 (final)
# speedup vs baseline: 1.1201x; 1.0877x over previous
import numpy as np
from contextlib import ExitStack

import ml_dtypes
import concourse.bass as bass
import concourse.mybir as mybir
from concourse.bass_utils import run_bass_kernel_spmd

F32 = mybir.dt.float32
BF16 = mybir.dt.bfloat16
AF = mybir.ActivationFunctionType
ALU = mybir.AluOpType
AX = mybir.AxisListType

NCORES = 8
S = 16384
D = 1024
T = S // NCORES
E = 8
FF = 4096
DC = D // 128
TT = T // 128
NCH = S // 1024
EPS = 1e-5
SCALE = 1.0 / 32.0
NEG_LN32 = float(-np.log(32.0))


class Ctr:
    def __init__(self, es, nc, name):
        self.h = es.enter_context(nc.semaphore(name))
        self.n = 0

    def inc(self, ins, k=1):
        ins.then_inc(self.h, k)
        self.n += k
        return self.n


def _ln_group(blk, srcs, xst, sq_scr, stat, xn32, xn_bf, want32,
              slots, ld, sv, ss, eps_ap, wait_loads=(), wait_xn=()):
    n = len(srcs)

    @blk.sync
    def _(sync):
        for (h, v) in wait_loads:
            if v > 0:
                sync.wait_ge(h, v)
        for i in range(n):
            sync.dma_start(out=xst[:, slots[i] * D:(slots[i] + 1) * D],
                           in_=srcs[i]).then_inc(ld.h, 16)
            ld.n += 16
    ld_grp = ld.n

    @blk.vector
    def _(v):
        v.wait_ge(ld.h, ld_grp)
        for i in range(n):
            c = slots[i] * 16
            v.bn_stats(stat[:, c:c + 6], xst[:, slots[i] * D:slots[i] * D + 512])
            v.bn_stats(stat[:, c + 6:c + 12], xst[:, slots[i] * D + 512:(slots[i] + 1) * D])
        for i in range(n):
            c = slots[i] * 16
            v.bn_aggr(stat[:, c + 12:c + 14], stat[:, c:c + 12])
        ins = None
        for i in range(n):
            c = slots[i] * 16
            ins = v.tensor_scalar(stat[:, c + 15:c + 16], stat[:, c + 12:c + 13], -1.0, None, ALU.mult)
        sv.inc(ins)
    sv_stats = sv.n

    @blk.scalar
    def _(sc):
        sc.wait_ge(sv.h, sv_stats)
        for i in range(n):
            c = slots[i] * 16
            sc.activation(stat[:, c:c + 1], stat[:, c + 13:c + 14], AF.Ln, bias=eps_ap, scale=1.0)
        ins = None
        for i in range(n):
            c = slots[i] * 16
            ins = sc.activation(stat[:, c + 14:c + 15], stat[:, c:c + 1], AF.Exp, scale=-0.5)
        ss.inc(ins)
    ss_inv = ss.n

    xn_done = {}

    @blk.vector
    def _(v):
        v.wait_ge(ss.h, ss_inv)
        for (h, val) in wait_xn:
            if val > 0:
                v.wait_ge(h, val)
        for i in range(n):
            c = slots[i] * 16
            sl = slots[i]
            if want32:
                v.tensor_scalar(xn32[:, sl * D:(sl + 1) * D],
                                xst[:, sl * D:(sl + 1) * D],
                                stat[:, c + 15:c + 16], stat[:, c + 14:c + 15],
                                ALU.add, ALU.mult)
                ins = v.tensor_copy(xn_bf[:, sl * D:(sl + 1) * D],
                                    xn32[:, sl * D:(sl + 1) * D])
            else:
                ins = v.tensor_scalar(xn_bf[:, sl * D:(sl + 1) * D],
                                      xst[:, sl * D:(sl + 1) * D],
                                      stat[:, c + 15:c + 16], stat[:, c + 14:c + 15],
                                      ALU.add, ALU.mult)
            sv.inc(ins)
            xn_done[i] = sv.n
    return xn_done


def build_nc():
    nc = bass.Bass("TRN2", target_bir_lowering=False, enable_partition_id=True)
    _pid = nc.scalar.partition_id()
    kv_off = nc.scalar.scalar_reg_alu(ALU.mult, nc.scalar.scalar_reg_alu(ALU.mod, _pid, 2), S // 2)
    _ET = mybir.EngineType
    _par_regs = nc.alloc_registers("parreg", engines=[_ET.Activation])
    nc.regs_mov(_par_regs, _pid)
    nc.scalar.scalar_reg_alu(ALU.mod, _par_regs.handles[0], 2)
    _par = _par_regs.handles[0]
    _pid_sp = nc.sync.partition_id()
    _parsp_regs = nc.alloc_registers("parsp", engines=[_ET.SP])
    nc.regs_mov(_parsp_regs, _pid_sp)
    nc.sync.scalar_reg_alu(ALU.mod, _parsp_regs.handles[0], 2)
    _parsp = _parsp_regs.handles[0]

    x_own = nc.dram_tensor("x_own", [T, D], F32, kind="ExternalInput")
    x_kv = nc.dram_tensor("x_kv", [S // 2, D], F32, kind="ExternalInput")
    wq = nc.dram_tensor("wq", [D, D], F32, kind="ExternalInput")
    wk = nc.dram_tensor("wk", [D, D], F32, kind="ExternalInput")
    wv = nc.dram_tensor("wv", [D, D], F32, kind="ExternalInput")
    gate_w = nc.dram_tensor("gate_w", [D, E], F32, kind="ExternalInput")
    w1 = nc.dram_tensor("w1", [E, D, FF], F32, kind="ExternalInput")
    b1 = nc.dram_tensor("b1", [E, FF], F32, kind="ExternalInput")
    w2 = nc.dram_tensor("w2", [E, FF, D], F32, kind="ExternalInput")
    b2 = nc.dram_tensor("b2", [E, D], F32, kind="ExternalInput")
    out = nc.dram_tensor("out", [T, D], F32, kind="ExternalOutput")

    kT_sh = nc.dram_tensor("kT_sh", [DC, 128, S], BF16, addr_space="Shared")
    v_sh = nc.dram_tensor("v_sh", [S, D], BF16, addr_space="Shared")
    km_in = nc.dram_tensor("km_in", [1, 8], F32)
    km_out = nc.dram_tensor("km_out", [1, 8], F32)
    xnT_scr = nc.dram_tensor("xnT_scr", [DC, 128, T], BF16)
    xn_scr = nc.dram_tensor("xn_scr", [TT, 128, D], BF16)

    id_bf_c = nc.inline_tensor(np.eye(128, dtype=ml_dtypes.bfloat16), name="id_bf_c")
    id_32_c = nc.inline_tensor(np.eye(128, dtype=np.float32), name="id_32_c")

    es = ExitStack()
    with es:
        O_sb = es.enter_context(nc.sbuf_tensor("O_sb", [128, TT * 1026], F32))
        comb = es.enter_context(nc.sbuf_tensor("comb", [128, TT * E], F32))
        sel_full = es.enter_context(nc.sbuf_tensor("sel_full", [128, TT * E], F32))
        negB = es.enter_context(nc.sbuf_tensor("negB", [128, 1], F32))
        id_bf = es.enter_context(nc.sbuf_tensor("id_bf", [128, 128], BF16))
        id_32 = es.enter_context(nc.sbuf_tensor("id_32", [128, 128], F32))
        ones2_bf = es.enter_context(nc.sbuf_tensor("ones2_bf", [128, 2], BF16))
        onesk_bf = es.enter_context(nc.sbuf_tensor("onesk_bf", [128, 1], BF16))
        ones1_bf = es.enter_context(nc.sbuf_tensor("ones1_bf", [1, 128], BF16))
        kmax2 = es.enter_context(nc.sbuf_tensor("kmax2", [1, 2], F32))
        qmax2 = es.enter_context(nc.sbuf_tensor("qmax2", [1, 2], F32))
        bstat = es.enter_context(nc.sbuf_tensor("bstat", [1, 8], F32))
        nb_bf = es.enter_context(nc.sbuf_tensor("nb_bf", [1, 1], BF16))
        eps_sb = es.enter_context(nc.sbuf_tensor("eps_sb", [128, 1], F32))
        nl32_sb = es.enter_context(nc.sbuf_tensor("nl32_sb", [1, 1], F32))
        qt_guard = nc.sbuf_tensor("QT", [128, DC * T], BF16)
        QT = qt_guard.__enter__()

        ld = Ctr(es, nc, "ld")
        ldg = Ctr(es, nc, "ldg")
        wr = Ctr(es, nc, "wr")
        sv = Ctr(es, nc, "sv")
        ss = Ctr(es, nc, "ss")
        st = Ctr(es, nc, "st")
        so = Ctr(es, nc, "so")
        sg = Ctr(es, nc, "sg")
        cc = Ctr(es, nc, "cc")

        g1 = ExitStack()
        with g1:
            xst = g1.enter_context(nc.sbuf_tensor("xst", [128, 4 * D], F32))
            sq_scr = g1.enter_context(nc.sbuf_tensor("sq_scr", [128, D], F32))
            stat = g1.enter_context(nc.sbuf_tensor("stat", [128, 64], F32))
            xn32 = g1.enter_context(nc.sbuf_tensor("xn32", [128, 4 * D], F32))
            xn_bf = g1.enter_context(nc.sbuf_tensor("xn_bf", [128, 4 * D], BF16))
            wq_bf = g1.enter_context(nc.sbuf_tensor("wq_bf", [128, DC * D], BF16))
            gate32 = g1.enter_context(nc.sbuf_tensor("gate32", [128, DC * E], F32))
            xnT_own = g1.enter_context(nc.sbuf_tensor("xnT_own", [128, DC * T], BF16))
            xnT32_t = g1.enter_context(nc.sbuf_tensor("xnT32_t", [128, D], F32))
            glog = g1.enter_context(nc.sbuf_tensor("glog", [128, 4 * E], F32))
            mx2 = g1.enter_context(nc.sbuf_tensor("mx2", [128, 4 * E], F32))
            cstat = g1.enter_context(nc.sbuf_tensor("cstat", [128, 16], F32))
            expl = g1.enter_context(nc.sbuf_tensor("expl", [128, 4 * E], F32))
            maskt = g1.enter_context(nc.sbuf_tensor("maskt", [128, 4 * E], F32))
            qsq_bf = g1.enter_context(nc.sbuf_tensor("qsq_bf", [128, T], BF16))

            p1 = ExitStack()
            with p1:
                trbf_ps = [p1.enter_context(nc.psum_tensor(f"trbf{i}", [128, D], BF16)) for i in range(2)]
                tr32_ps = p1.enter_context(nc.psum_tensor("tr32", [128, D], F32))
                gate_ps = p1.enter_context(nc.psum_tensor("gate_ps", [128, E], F32))
                qt_ps = [p1.enter_context(nc.psum_tensor(f"qt{i}", [128, 512], F32)) for i in range(2)]

                blk = nc.Block("gen_own"); blk.__enter__()

                @blk.sync
                def _(sync):
                    sync.dma_start(out=id_bf[:], in_=id_bf_c[:]).then_inc(ld.h, 16)
                    sync.dma_start(out=id_32[:], in_=id_32_c[:]).then_inc(ld.h, 16)
                    sync.dma_start(
                        out=gate32[:].rearrange("p (j e) -> p j e", j=DC),
                        in_=gate_w[:].rearrange("(j p) e -> p j e", p=128),
                    ).then_inc(ld.h, 16)
                    ld.n += 48
                ld_const = ld.n

                @blk.vector
                def _(v):
                    v.memset(ones2_bf[:], 1.0)
                    v.memset(onesk_bf[:], 1.0)
                    v.memset(ones1_bf[:], 1.0)
                    v.memset(kmax2[:], 0.0)
                    v.memset(eps_sb[:], EPS)
                    v.memset(nl32_sb[:], NEG_LN32)
                    ins = None
                    for gt in range(TT):
                        ins = v.memset(O_sb[:, gt * 1026:(gt + 1) * 1026], 0.0)
                    sv.inc(ins)

                @blk.gpsimd
                def _(g):
                    g.dma_start(
                        out=wq_bf[:].rearrange("p (j d) -> p j d", j=DC),
                        in_=wq[:].rearrange("(j p) d -> p j d", p=128),
                    ).then_inc(ldg.h, 16)
                    ldg.n += 16
                ldg_wq = ldg.n

                grp_tr_done = {}
                grp_xn_done = {}
                grp_xnwr_done = {}
                evac_done = {}
                gate_done_d = {}
                glog_done = {}

                for grp in range(4):
                    tiles = [grp * 4 + i for i in range(4)]
                    xn_done = _ln_group(
                        blk,
                        [x_own[g * 128:(g + 1) * 128, :] for g in tiles],
                        xst, sq_scr, stat, xn32, xn_bf, True,
                        [i for i in range(4)],
                        ld, sv, ss, eps_sb[:],
                        wait_loads=[(sv.h, grp_xn_done.get(grp - 1, 0))],
                        wait_xn=[(st.h, grp_tr_done.get(grp - 1, 0)),
                                 (wr.h, grp_xnwr_done.get(grp - 1, 0))],
                    )
                    grp_xn_done[grp] = sv.n

                    @blk.scalar
                    def _(sc, tiles=tiles, xn_done=xn_done):
                        for i, g in enumerate(tiles):
                            sc.wait_ge(sv.h, xn_done[i])
                            ins = sc.dma_start(out=xn_scr[g, :, :],
                                               in_=xn_bf[:, (g % 4) * D:(g % 4 + 1) * D])
                            wr.inc(ins, 16)
                    grp_xnwr_done[grp] = wr.n

                    for i, g in enumerate(tiles):
                        @blk.tensor
                        def _(te, g=g, i=i, xn_done=xn_done):
                            te.wait_ge(sv.h, xn_done[i])
                            te.wait_ge(sv.h, evac_done.get(g - 2, 0))
                            te.wait_ge(sv.h, evac_done.get(g - 1, 0))
                            for j in range(DC):
                                te.transpose(trbf_ps[g % 2][:, j * 128:(j + 1) * 128],
                                             xn_bf[:, (g % 4) * D + j * 128:(g % 4) * D + (j + 1) * 128],
                                             id_bf[:])
                            ins = None
                            for j in range(DC):
                                ins = te.transpose(tr32_ps[:, j * 128:(j + 1) * 128],
                                                   xn32[:, (g % 4) * D + j * 128:(g % 4) * D + (j + 1) * 128],
                                                   id_32[:])
                            st.inc(ins)
                        tr_done = st.n

                        @blk.vector
                        def _(v, g=g, tr_done=tr_done):
                            v.wait_ge(st.h, tr_done)
                            v.wait_ge(st.h, gate_done_d.get(g - 1, 0))
                            dst = xnT_own[:].rearrange("p (j t) -> p j t", j=DC)[:, :, g * 128:(g + 1) * 128]
                            src = trbf_ps[g % 2][:].rearrange("p (j c) -> p j c", j=DC)
                            v.tensor_copy(dst, src)
                            ins = v.tensor_copy(xnT32_t[:], tr32_ps[:])
                            sv.inc(ins)
                            evac_done[g] = sv.n

                        @blk.tensor
                        def _(te, g=g):
                            te.wait_ge(sv.h, evac_done[g])
                            te.wait_ge(ld.h, ld_const)
                            te.wait_ge(sv.h, glog_done.get(g - 1, 0))
                            ins = None
                            for j in range(DC):
                                ins = te.matmul(gate_ps[:], xnT32_t[:, j * 128:(j + 1) * 128],
                                                gate32[:, j * E:(j + 1) * E],
                                                start=(j == 0), stop=(j == DC - 1))
                            st.inc(ins)
                            gate_done_d[g] = st.n

                        @blk.vector
                        def _(v, g=g):
                            v.wait_ge(st.h, gate_done_d[g])
                            ins = v.tensor_copy(glog[:, (g % 4) * E:(g % 4 + 1) * E], gate_ps[:])
                            sv.inc(ins)
                            glog_done[g] = sv.n
                    grp_tr_done[grp] = st.n

                    @blk.vector
                    def _(v, tiles=tiles):
                        v.wait_ge(sv.h, glog_done[tiles[-1]])
                        for g in tiles:
                            s4 = g % 4
                            v.max(out=mx2[:, s4 * E:(s4 + 1) * E], in_=glog[:, s4 * E:(s4 + 1) * E])
                        ins = None
                        for g in tiles:
                            s4 = g % 4
                            ins = v.tensor_scalar(cstat[:, s4 * 4:s4 * 4 + 1], mx2[:, s4 * E:s4 * E + 1],
                                                  -1.0, None, ALU.mult)
                        sv.inc(ins)
                    sv_negm1 = sv.n

                    @blk.scalar
                    def _(sc, tiles=tiles, sv_negm1=sv_negm1):
                        sc.wait_ge(sv.h, sv_negm1)
                        for g in tiles:
                            s4 = g % 4
                            sc.activation(cstat[:, s4 * 4 + 1:s4 * 4 + 2], mx2[:, s4 * E + 1:s4 * E + 2],
                                          AF.Exp, bias=cstat[:, s4 * 4:s4 * 4 + 1], scale=1.0)
                        ins = None
                        for g in tiles:
                            s4 = g % 4
                            ins = sc.activation(expl[:, s4 * E:(s4 + 1) * E], glog[:, s4 * E:(s4 + 1) * E],
                                                AF.Exp, bias=cstat[:, s4 * 4:s4 * 4 + 1], scale=1.0)
                        ss.inc(ins)
                    ss_cexp = ss.n

                    @blk.vector
                    def _(v, tiles=tiles, ss_cexp=ss_cexp):
                        v.wait_ge(ss.h, ss_cexp)
                        for g in tiles:
                            s4 = g % 4
                            v.tensor_scalar(cstat[:, s4 * 4 + 2:s4 * 4 + 3], cstat[:, s4 * 4 + 1:s4 * 4 + 2],
                                            1.0, None, ALU.add)
                        for g in tiles:
                            s4 = g % 4
                            v.reciprocal(cstat[:, s4 * 4 + 3:s4 * 4 + 4], cstat[:, s4 * 4 + 2:s4 * 4 + 3])
                        for g in tiles:
                            s4 = g % 4
                            v.tensor_scalar(maskt[:, s4 * E:(s4 + 1) * E], glog[:, s4 * E:(s4 + 1) * E],
                                            mx2[:, s4 * E + 1:s4 * E + 2], None, ALU.is_ge)
                        for g in tiles:
                            s4 = g % 4
                            v.tensor_copy(sel_full[:, g * E:(g + 1) * E], maskt[:, s4 * E:(s4 + 1) * E])
                        for g in tiles:
                            s4 = g % 4
                            v.tensor_tensor(comb[:, g * E:(g + 1) * E], expl[:, s4 * E:(s4 + 1) * E],
                                            maskt[:, s4 * E:(s4 + 1) * E], ALU.mult)
                        ins = None
                        for g in tiles:
                            s4 = g % 4
                            ins = v.tensor_scalar(comb[:, g * E:(g + 1) * E], comb[:, g * E:(g + 1) * E],
                                                  cstat[:, s4 * 4 + 3:s4 * 4 + 4], None, ALU.mult)
                        sv.inc(ins)

                qt_mm_done = {}
                NQT = T // 512

                @blk.tensor
                def _(te):
                    te.wait_ge(sv.h, evac_done[TT - 1])
                    te.wait_ge(ldg.h, ldg_wq)
                    ss_base = ss.n
                    for j in range(DC):
                        for n in range(NQT):
                            idx = j * NQT + n
                            if idx >= 2:
                                te.wait_ge(ss.h, ss_base + idx - 1)
                            ins = None
                            for dd in range(DC):
                                ins = te.matmul(qt_ps[idx % 2][:],
                                                wq_bf[:, dd * D + j * 128:dd * D + (j + 1) * 128],
                                                xnT_own[:, dd * T + n * 512:dd * T + (n + 1) * 512],
                                                start=(dd == 0), stop=(dd == DC - 1))
                            st.inc(ins)
                            qt_mm_done[idx] = st.n

                @blk.scalar
                def _(sc):
                    for j in range(DC):
                        for n in range(NQT):
                            idx = j * NQT + n
                            sc.wait_ge(st.h, qt_mm_done[idx])
                            ins = sc.activation(QT[:, j * T + n * 512:j * T + (n + 1) * 512],
                                                qt_ps[idx % 2][:], AF.Copy)
                            ss.inc(ins)
                ss_qt_all = ss.n

                @blk.scalar
                def _(sc):
                    for j in range(DC):
                        ins = sc.dma_start(out=xnT_scr[j, :, :], in_=xnT_own[:, j * T:(j + 1) * T])
                        wr.inc(ins, 16)
                wr_xnT = wr.n

            p2 = ExitStack()
            with p2:
                qn2_ps = p2.enter_context(nc.psum_tensor("qn2", [1, T], F32))
                qsq_done = {}
                qn_mm_done = {}
                for j in range(DC):
                    @blk.scalar
                    def _(sc, j=j):
                        if j == 0:
                            sc.wait_ge(ss.h, ss_qt_all)
                        else:
                            sc.wait_ge(st.h, qn_mm_done.get(j - 1, 0))
                        ins = sc.activation(qsq_bf[:], QT[:, j * T:(j + 1) * T], AF.Square)
                        ss.inc(ins)
                        qsq_done[j] = ss.n

                    @blk.tensor
                    def _(te, j=j):
                        te.wait_ge(ss.h, qsq_done[j])
                        ins = None
                        for n in range(T // 512):
                            ins = te.matmul(qn2_ps[:, n * 512:(n + 1) * 512],
                                            onesk_bf[:], qsq_bf[:, n * 512:(n + 1) * 512],
                                            start=(j == 0), stop=(j == DC - 1))
                        st.inc(ins)
                        qn_mm_done[j] = st.n

                @blk.vector
                def _(v):
                    v.wait_ge(st.h, qn_mm_done[DC - 1])
                    ins = v.reduce_max(qmax2[:, 0:1], qn2_ps[:], axis=AX.X)
                    sv.inc(ins)
            blk.__exit__(None, None, None)
            sv_qmax = sv.n

        g2 = ExitStack()
        with g2:
            xst = g2.enter_context(nc.sbuf_tensor("xst2", [128, 4 * D], F32))
            sq_scr = g2.enter_context(nc.sbuf_tensor("sq_scr2", [128, D], F32))
            stat = g2.enter_context(nc.sbuf_tensor("stat2", [128, 64], F32))
            xn_bf = g2.enter_context(nc.sbuf_tensor("xn_bf2", [128, 4 * D], BF16))
            wk_bf = g2.enter_context(nc.sbuf_tensor("wk_bf", [128, DC * D], BF16))
            wv_bf = g2.enter_context(nc.sbuf_tensor("wv_bf", [128, DC * D], BF16))
            xnT_kv = g2.enter_context(nc.sbuf_tensor("xnT_kv", [128, 2 * DC * 1024], BF16))
            ktb = g2.enter_context(nc.sbuf_tensor("ktb", [128, 2 * 1024], BF16))
            vtb = g2.enter_context(nc.sbuf_tensor("vtb", [128, 2 * 1024], BF16))
            ksq_bf = g2.enter_context(nc.sbuf_tensor("ksq_bf", [128, 2 * 1024], BF16))
            km_sb8 = g2.enter_context(nc.sbuf_tensor("km_sb8", [1, 8], F32))

            p3 = ExitStack()
            with p3:
                trbf_ps = [p3.enter_context(nc.psum_tensor(f"trbf2_{i}", [128, D], BF16)) for i in range(2)]
                kt_ps = [p3.enter_context(nc.psum_tensor(f"ktp{i}", [128, 512], F32)) for i in range(2)]
                v_ps = [p3.enter_context(nc.psum_tensor(f"vp{i}", [128, 512], F32)) for i in range(2)]
                kn2_ps = p3.enter_context(nc.psum_tensor("kn2", [1, 1024], F32))

                blk = nc.Block("gen_full"); blk.__enter__()

                @blk.gpsimd
                def _(g):
                    g.dma_start(
                        out=wk_bf[:].rearrange("p (j d) -> p j d", j=DC),
                        in_=wk[:].rearrange("(j p) d -> p j d", p=128),
                    ).then_inc(ldg.h, 16)
                    g.dma_start(
                        out=wv_bf[:].rearrange("p (j d) -> p j d", j=DC),
                        in_=wv[:].rearrange("(j p) d -> p j d", p=128),
                    ).then_inc(ldg.h, 16)
                    ldg.n += 32
                ldg_wkv = ldg.n

                ch_mm_done = {}
                ch_kmax_done = {}
                trevac_done = {}
                prev_xn_sv = [0]
                prev_tr_st = [0]
                NCHH = NCH // 2

                for ch in range(NCHH):
                    base = (ch % 2) * DC * 1024
                    for grp in range(2):
                        tiles = [ch * 8 + grp * 4 + i for i in range(4)]
                        xn_done = _ln_group(
                            blk,
                            [x_kv[g * 128:(g + 1) * 128, :] for g in tiles],
                            xst, sq_scr, stat, None, xn_bf, False,
                            [i for i in range(4)],
                            ld, sv, ss, eps_sb[:],
                            wait_loads=[(sv.h, prev_xn_sv[0])],
                            wait_xn=[(st.h, prev_tr_st[0])],
                        )
                        prev_xn_sv[0] = sv.n
                        for i in range(4):
                            tl = grp * 4 + i
                            @blk.tensor
                            def _(te, tl=tl, i=i, ch=ch, xn_done=xn_done, base=base):
                                te.wait_ge(sv.h, xn_done[i])
                                pk = (ch, tl - 2) if tl >= 2 else (ch - 1, 6 + tl)
                                if pk in trevac_done:
                                    h_, v_ = trevac_done[pk]
                                    te.wait_ge(h_, v_)
                                ins = None
                                for j in range(DC):
                                    ins = te.transpose(trbf_ps[tl % 2][:, j * 128:(j + 1) * 128],
                                                       xn_bf[:, i * D + j * 128:i * D + (j + 1) * 128],
                                                       id_bf[:])
                                st.inc(ins)
                            tr_done = st.n

                            if tl % 2 == 0:
                                @blk.vector
                                def _(v, tl=tl, tr_done=tr_done, ch=ch, base=base):
                                    v.wait_ge(st.h, tr_done)
                                    v.wait_ge(st.h, ch_mm_done.get(ch - 2, 0))
                                    dst = xnT_kv[:, base:base + DC * 1024].rearrange(
                                        "p (j t) -> p j t", j=DC)[:, :, tl * 128:(tl + 1) * 128]
                                    src = trbf_ps[tl % 2][:].rearrange("p (j c) -> p j c", j=DC)
                                    ins = v.tensor_copy(dst, src)
                                    sv.inc(ins)
                                    trevac_done[(ch, tl)] = (sv.h, sv.n)
                            else:
                                @blk.scalar
                                def _(sc, tl=tl, tr_done=tr_done, ch=ch, base=base):
                                    sc.wait_ge(st.h, tr_done)
                                    sc.wait_ge(st.h, ch_mm_done.get(ch - 2, 0))
                                    dst = xnT_kv[:, base:base + DC * 1024].rearrange(
                                        "p (j t) -> p j t", j=DC)[:, :, tl * 128:(tl + 1) * 128]
                                    src = trbf_ps[tl % 2][:].rearrange("p (j c) -> p j c", j=DC)
                                    ins = sc.activation(dst, src, AF.Copy)
                                    ss.inc(ins)
                                    trevac_done[(ch, tl)] = (ss.h, ss.n)
                        prev_tr_st[0] = st.n
                    sv_evac_ch = sv.n
                    ss_evac_ch = ss.n

                    kt_mm_done = {}
                    kt_evac_actual = {}
                    kt_wr_done = {}
                    ksq_done = {}

                    @blk.tensor
                    def _(te, ch=ch, base=base, sv_evac_ch=sv_evac_ch, ss_evac_ch=ss_evac_ch):
                        te.wait_ge(sv.h, sv_evac_ch)
                        te.wait_ge(ss.h, ss_evac_ch)
                        if ch == 0:
                            te.wait_ge(ldg.h, ldg_wkv)
                        ss_base = ss.n
                        for j in range(DC):
                            for n in range(2):
                                idx = j * 2 + n
                                if idx >= 2:
                                    te.wait_ge(ss.h, ss_base + idx - 1)
                                ins = None
                                for dd in range(DC):
                                    ins = te.matmul(kt_ps[idx % 2][:],
                                                    wk_bf[:, dd * D + j * 128:dd * D + (j + 1) * 128],
                                                    xnT_kv[:, base + dd * 1024 + n * 512:base + dd * 1024 + (n + 1) * 512],
                                                    start=(dd == 0), stop=(dd == DC - 1))
                                st.inc(ins)
                                kt_mm_done[idx] = st.n

                    @blk.scalar
                    def _(sc, ch=ch):
                        for j in range(DC):
                            for n in range(2):
                                idx = j * 2 + n
                                sc.wait_ge(st.h, kt_mm_done[idx])
                                if j >= 2:
                                    sc.wait_ge(wr.h, kt_wr_done.get(j - 2, 0))
                                    sc.wait_ge(sv.h, ksq_done.get(j - 2, 0))
                                ins = sc.activation(
                                    ktb[:, (j % 2) * 1024 + n * 512:(j % 2) * 1024 + (n + 1) * 512],
                                    kt_ps[idx % 2][:], AF.Copy)
                                ss.inc(ins)
                            kt_evac_actual[j] = ss.n
                            for p in range(2):
                                with sc.If_eq(_par, p):
                                    sc.dma_start(
                                        out=kT_sh[j, :, p * (S // 2) + ch * 1024:p * (S // 2) + (ch + 1) * 1024],
                                        in_=ktb[:, (j % 2) * 1024:(j % 2 + 1) * 1024],
                                    ).then_inc(wr.h, 16)
                            wr.n += 16
                            kt_wr_done[j] = wr.n

                    @blk.vector
                    def _(v, ch=ch):
                        for j in range(DC):
                            v.wait_ge(ss.h, kt_evac_actual[j])
                            ins = v.tensor_tensor(ksq_bf[:, (j % 2) * 1024:(j % 2 + 1) * 1024],
                                                  ktb[:, (j % 2) * 1024:(j % 2 + 1) * 1024],
                                                  ktb[:, (j % 2) * 1024:(j % 2 + 1) * 1024], ALU.mult)
                            sv.inc(ins)
                            ksq_done[j] = sv.n

                    @blk.tensor
                    def _(te, ch=ch):
                        te.wait_ge(sv.h, ch_kmax_done.get(ch - 1, 0))
                        ins = None
                        for j in range(DC):
                            te.wait_ge(sv.h, ksq_done[j])
                            for n in range(2):
                                ins = te.matmul(kn2_ps[:, n * 512:(n + 1) * 512],
                                                onesk_bf[:],
                                                ksq_bf[:, (j % 2) * 1024 + n * 512:(j % 2) * 1024 + (n + 1) * 512],
                                                start=(j == 0), stop=(j == DC - 1))
                        st.inc(ins)
                    kn_done = st.n

                    @blk.vector
                    def _(v, ch=ch, kn_done=kn_done):
                        v.wait_ge(st.h, kn_done)
                        ins = v.reduce_max(kmax2[:, 1:2], kn2_ps[:], axis=AX.X)
                        sv.inc(ins)
                        v.wait_ge(sv.h, sv.n)
                        ins = v.tensor_tensor(kmax2[:, 0:1], kmax2[:, 0:1], kmax2[:, 1:2], ALU.max)
                        sv.inc(ins)
                        ch_kmax_done[ch] = sv.n

                    v_mm_done = {}
                    v_evac_actual = {}
                    v_wr_done = {}

                    @blk.tensor
                    def _(te, ch=ch, base=base):
                        ss_base = ss.n
                        for tl in range(8):
                            for n in range(2):
                                idx = tl * 2 + n
                                if idx >= 2:
                                    te.wait_ge(ss.h, ss_base + idx - 1)
                                ins = None
                                for dd in range(DC):
                                    ins = te.matmul(v_ps[idx % 2][:],
                                                    xnT_kv[:, base + dd * 1024 + tl * 128:base + dd * 1024 + tl * 128 + 128],
                                                    wv_bf[:, dd * D + n * 512:dd * D + (n + 1) * 512],
                                                    start=(dd == 0), stop=(dd == DC - 1))
                                st.inc(ins)
                                v_mm_done[idx] = st.n

                    @blk.scalar
                    def _(sc, ch=ch):
                        for tl in range(8):
                            for n in range(2):
                                idx = tl * 2 + n
                                sc.wait_ge(st.h, v_mm_done[idx])
                                if tl >= 2:
                                    sc.wait_ge(wr.h, v_wr_done.get(tl - 2, 0))
                                ins = sc.activation(
                                    vtb[:, (tl % 2) * 1024 + n * 512:(tl % 2) * 1024 + (n + 1) * 512],
                                    v_ps[idx % 2][:], AF.Copy)
                                ss.inc(ins)
                            row = (ch * 8 + tl) * 128
                            for p in range(2):
                                with sc.If_eq(_par, p):
                                    sc.dma_start(
                                        out=v_sh[p * (S // 2) + row:p * (S // 2) + row + 128, :],
                                        in_=vtb[:, (tl % 2) * 1024:(tl % 2 + 1) * 1024],
                                    ).then_inc(wr.h, 16)
                            wr.n += 16
                            v_wr_done[tl] = wr.n
                    ch_mm_done[ch] = st.n
                wr_scr_all = wr.n

                @blk.vector
                def _(v):
                    v.wait_ge(sv.h, ch_kmax_done[NCHH - 1])
                    ins = v.memset(km_sb8[:], 0.0)
                    sv.inc(ins)
                    v.wait_ge(sv.h, sv.n)
                    ins = v.tensor_copy(km_sb8[:, 0:1], kmax2[:, 0:1])
                    sv.inc(ins)
                sv_km8 = sv.n

                @blk.scalar
                def _(sc):
                    sc.wait_ge(sv.h, sv_km8)
                    ins = sc.dma_start(out=km_in[:], in_=km_sb8[:])
                    wr.inc(ins, 16)
                wr_all_kv = wr.n

                @blk.gpsimd
                def _(g):
                    g.wait_ge(wr.h, wr_all_kv)
                    g.collective_compute(
                        "AllReduce", ALU.max,
                        replica_groups=[[2 * i, 2 * i + 1] for i in range(NCORES // 2)],
                        ins=[km_in.ap().opt()], outs=[km_out.ap().opt()],
                    ).then_inc(cc.h, 1)
                    cc.n += 1
                cc_bar = cc.n

                blk.__exit__(None, None, None)

        a1 = ExitStack()
        with a1:
            kt_tile = a1.enter_context(nc.sbuf_tensor("kt_tile", [128, 2 * DC * 1024], BF16))
            v_tile = a1.enter_context(nc.sbuf_tensor("v_tile", [128, 2 * 8 * 1024], BF16))
            expS = a1.enter_context(nc.sbuf_tensor("expS", [128, 2 * 8 * 512], BF16))
            x_res = a1.enter_context(nc.sbuf_tensor("x_res", [128, 2 * D], F32))
            rzc = a1.enter_context(nc.sbuf_tensor("rzc", [128, TT], F32))
            kmg8 = a1.enter_context(nc.sbuf_tensor("kmg8", [1, 8], F32))

            p5 = ExitStack()
            with p5:
                s_ps = [p5.enter_context(nc.psum_tensor(f"sps{i}", [128, 512], F32)) for i in range(2)]
                o_ps = [p5.enter_context(nc.psum_tensor(f"ops{i}", [128, 1026], F32)) for i in range(2)]

                blk = nc.Block("attn"); blk.__enter__()

                kv_loaded = {}
                kv_mm_done = {}

                def _kv_load_body(kvt):
                    @blk.sync
                    def _(sync, kvt=kvt):
                        sync.wait_ge(st.h, kv_mm_done.get(kvt - 2, 0))
                        if kvt == 0:
                            sync.wait_ge(wr.h, wr_all_kv)
                        half = kvt % 2
                        for p in range(2):
                            phys = (kvt + 8 * p) % NCH
                            with sync.If_eq(_parsp, p):
                                sync.dma_start(
                                    out=kt_tile[:, half * DC * 1024:(half + 1) * DC * 1024].rearrange("p (j k) -> p j k", j=DC),
                                    in_=kT_sh[:, :, phys * 1024:(phys + 1) * 1024].rearrange("j p k -> p j k"),
                                ).then_inc(ld.h, 16)
                                sync.dma_start(
                                    out=v_tile[:, half * 8 * 1024:(half + 1) * 8 * 1024].rearrange("p (g d) -> p g d", g=8),
                                    in_=v_sh[phys * 1024:(phys + 1) * 1024, :].rearrange("(g p) d -> p g d", p=128),
                                ).then_inc(ld.h, 16)
                        ld.n += 32
                    kv_loaded[kvt] = ld.n

                _kv_load_body(0)
                _kv_load_body(1)

                @blk.sync
                def _(sync):
                    sync.wait_ge(cc.h, cc_bar)
                    sync.dma_start(out=kmg8[:], in_=km_out[:]).then_inc(ld.h, 16)
                    ld.n += 16
                ld_kmg = ld.n

                @blk.scalar
                def _(sc):
                    sc.wait_ge(ld.h, ld_kmg)
                    sc.wait_ge(sv.h, sv_qmax)
                    ins = sc.activation(bstat[:, 0:1], qmax2[:, 0:1], AF.Ln)
                    ss.inc(ins)
                    sc.wait_ge(ss.h, ss.n)
                    ins = sc.activation(bstat[:, 1:2], kmg8[:, 0:1], AF.Ln)
                    ss.inc(ins)
                ss_lnk = ss.n

                @blk.vector
                def _(v):
                    v.wait_ge(ss.h, ss_lnk)
                    ins = v.tensor_tensor(bstat[:, 2:3], bstat[:, 0:1], bstat[:, 1:2], ALU.add)
                    sv.inc(ins)
                sv_lnsum = sv.n

                @blk.scalar
                def _(sc):
                    sc.wait_ge(sv.h, sv_lnsum)
                    ins = sc.activation(bstat[:, 3:4], bstat[:, 2:3], AF.Exp, bias=nl32_sb[:], scale=0.5)
                    ss.inc(ins)
                ss_B = ss.n

                @blk.vector
                def _(v):
                    v.wait_ge(ss.h, ss_B)
                    ins = v.tensor_scalar(bstat[:, 4:5], bstat[:, 3:4], -1.0, None, ALU.mult)
                    sv.inc(ins)
                    v.wait_ge(sv.h, sv.n)
                    ins = v.tensor_copy(nb_bf[:], bstat[0:1, 4:5])
                    sv.inc(ins)
                sv_nbbf = sv.n

                @blk.tensor
                def _(te):
                    te.wait_ge(sv.h, sv_nbbf)
                    ins = te.matmul(s_ps[0][:, 0:1], ones1_bf[:], nb_bf[:], start=True, stop=True)
                    st.inc(ins)
                st_nb = st.n

                @blk.vector
                def _(v):
                    v.wait_ge(st.h, st_nb)
                    ins = v.tensor_copy(negB[:], s_ps[0][:, 0:1])
                    sv.inc(ins)
                sv_negB = sv.n

                exp_done = {}
                pv_done = {}
                flush_done = {}

                for kvt in range(NCH):
                    if kvt >= 2:
                        _kv_load_body(kvt)

                    for qs in range(4):
                        ebase = (qs % 2) * 8 * 512
                        smm_done = {}

                        @blk.tensor
                        def _(te, kvt=kvt, qs=qs, smm_done=smm_done):
                            te.wait_ge(ld.h, kv_loaded[kvt])
                            half = (kvt % 2) * DC * 1024
                            ss_base = ss.n
                            for kg in range(8):
                                if kg >= 2:
                                    te.wait_ge(ss.h, ss_base + kg - 1)
                                else:
                                    key = (kvt, qs - 1, 6 + kg) if qs >= 1 else (kvt - 1, 3, 6 + kg)
                                    if key in exp_done:
                                        te.wait_ge(ss.h, exp_done[key])
                                ins = None
                                for dd in range(DC):
                                    ins = te.matmul(s_ps[kg % 2][:],
                                                    kt_tile[:, half + dd * 1024 + kg * 128:half + dd * 1024 + (kg + 1) * 128],
                                                    QT[:, dd * T + qs * 512:dd * T + (qs + 1) * 512],
                                                    start=(dd == 0), stop=(dd == DC - 1))
                                st.inc(ins)
                                smm_done[kg] = st.n

                        @blk.scalar
                        def _(sc, kvt=kvt, qs=qs, ebase=ebase, smm_done=smm_done):
                            if kvt == 0 and qs == 0:
                                sc.wait_ge(sv.h, sv_negB)
                            for kg in range(8):
                                sc.wait_ge(st.h, smm_done[kg])
                                ins = sc.activation(expS[:, ebase + kg * 512:ebase + (kg + 1) * 512],
                                                    s_ps[kg % 2][:], AF.Exp,
                                                    bias=negB[:], scale=SCALE)
                                ss.inc(ins)
                                exp_done[(kvt, qs, kg)] = ss.n

                        @blk.tensor
                        def _(te, kvt=kvt, qs=qs, ebase=ebase):
                            te.wait_ge(ss.h, exp_done[(kvt, qs, 7)])
                            half = (kvt % 2) * 8 * 1024
                            sv_base = sv.n
                            for qt in range(4):
                                if qt >= 2:
                                    te.wait_ge(sv.h, sv_base + qt - 1)
                                else:
                                    key = (kvt, qs - 1, 2 + qt) if qs >= 1 else (kvt - 1, 3, 2 + qt)
                                    if key in flush_done:
                                        te.wait_ge(sv.h, flush_done[key])
                                ins = None
                                for kg in range(8):
                                    lhs = expS[:, ebase + kg * 512 + qt * 128:ebase + kg * 512 + (qt + 1) * 128]
                                    te.matmul(o_ps[qt % 2][:, 0:512], lhs,
                                              v_tile[:, half + kg * 1024:half + kg * 1024 + 512],
                                              start=(kg == 0), stop=(kg == 7))
                                    te.matmul(o_ps[qt % 2][:, 512:1024], lhs,
                                              v_tile[:, half + kg * 1024 + 512:half + kg * 1024 + 1024],
                                              start=(kg == 0), stop=(kg == 7))
                                    ins = te.matmul(o_ps[qt % 2][:, 1024:1026], lhs,
                                                    ones2_bf[:],
                                                    start=(kg == 0), stop=(kg == 7))
                                st.inc(ins)
                                pv_done[(kvt, qs, qt)] = st.n

                        @blk.vector
                        def _(v, kvt=kvt, qs=qs):
                            for qt in range(4):
                                v.wait_ge(st.h, pv_done[(kvt, qs, qt)])
                                gt = qs * 4 + qt
                                ins = v.tensor_tensor(O_sb[:, gt * 1026:(gt + 1) * 1026],
                                                      O_sb[:, gt * 1026:(gt + 1) * 1026],
                                                      o_ps[qt % 2][:], ALU.add)
                                sv.inc(ins)
                                flush_done[(kvt, qs, qt)] = sv.n
                    kv_mm_done[kvt] = st.n

                last_flush = sv.n

                @blk.vector
                def _(v):
                    v.wait_ge(sv.h, last_flush)
                    ins = None
                    for gt in range(TT):
                        ins = v.reciprocal(rzc[:, gt:gt + 1], O_sb[:, gt * 1026 + 1024:gt * 1026 + 1025])
                    sv.inc(ins)
                sv_rz = sv.n

                xres_done = {}
                fin_done = {}
                for gt in range(TT):
                    @blk.sync
                    def _(sync, gt=gt):
                        sync.wait_ge(sv.h, fin_done.get(gt - 2, 0))
                        sync.dma_start(out=x_res[:, (gt % 2) * D:(gt % 2 + 1) * D],
                                       in_=x_own[gt * 128:(gt + 1) * 128, :]).then_inc(ld.h, 16)
                        ld.n += 16
                    xres_done[gt] = ld.n

                    @blk.vector
                    def _(v, gt=gt):
                        v.wait_ge(ld.h, xres_done[gt])
                        v.wait_ge(sv.h, sv_rz)
                        ins = v.scalar_tensor_tensor(O_sb[:, gt * 1026:gt * 1026 + 1024],
                                                     O_sb[:, gt * 1026:gt * 1026 + 1024],
                                                     rzc[:, gt:gt + 1],
                                                     x_res[:, (gt % 2) * D:(gt % 2 + 1) * D],
                                                     ALU.mult, ALU.add)
                        sv.inc(ins)
                        fin_done[gt] = sv.n
                blk.__exit__(None, None, None)

        qt_guard.__exit__(None, None, None)

        C = 640
        SC5 = C // 128
        FQ = 4
        f1 = ExitStack()
        with f1:
            iota_f = f1.enter_context(nc.sbuf_tensor("iota_f", [128, C], F32))
            comb_bf = f1.enter_context(nc.sbuf_tensor("comb_bf", [128, TT * E], BF16))
            posb = f1.enter_context(nc.sbuf_tensor("posb", [128, TT * E], F32))
            comb_sel = f1.enter_context(nc.sbuf_tensor("comb_sel", [128, 8], F32))
            b1_sb = f1.enter_context(nc.sbuf_tensor("b1_sb", [128, E * (FF // 128)], F32))
            b2e = f1.enter_context(nc.sbuf_tensor("b2e", [1, D], BF16))
            b1_rm = f1.enter_context(nc.sbuf_tensor("b1_rm", [128, 256], F32))

            def E_view(tt):
                return u20[:, tt * C:(tt + 1) * C]

            def hq_view(fc):
                return u20[:, fc * C:(fc + 1) * C]

            def ybf_view(st5):
                return u20[:, 8 * C + st5 * 1024:8 * C + (st5 + 1) * 1024]

            blk = nc.Block("ffn")
            blk.__enter__()

            ppre = ExitStack()
            with ppre:
                b1t_ps = ppre.enter_context(nc.psum_tensor("b1t_ps", [128, 256], F32))
                trp8 = ppre.enter_context(nc.psum_tensor("trp8", [8, 128], F32))
                trpb = ppre.enter_context(nc.psum_tensor("trpb", [128, 8], F32))
                selT = ppre.enter_context(nc.sbuf_tensor("selT", [8, T], F32))
                ones8 = ppre.enter_context(nc.sbuf_tensor("ones8", [8, T], F32))
                mask8 = ppre.enter_context(nc.sbuf_tensor("mask8", [8, T], F32))
                posT8 = ppre.enter_context(nc.sbuf_tensor("posT8", [8, T], F32))

                @blk.sync
                def _(sync):
                    sync.dma_start(
                        out=b1_rm[:].rearrange("q (m p) -> q m p", m=2),
                        in_=b1[:].rearrange("e f -> (e f)").rearrange("(m q p) -> q m p", m=2, p=128),
                    ).then_inc(ld.h, 16)
                    ld.n += 16
                ld_ffn_pre = ld.n

                @blk.gpsimd
                def _(g):
                    ins = g.iota(iota_f[:], [[1, C]], channel_multiplier=0,
                                 allow_small_or_imprecise_dtypes=True)
                    sg.inc(ins)
                sg_iota = sg.n

                @blk.tensor
                def _(te):
                    te.wait_ge(ld.h, ld_ffn_pre)
                    te.transpose(b1t_ps[:, 0:128], b1_rm[:, 0:128], id_32[:])
                    ins = te.transpose(b1t_ps[:, 128:256], b1_rm[:, 128:256], id_32[:])
                    st.inc(ins)
                st_b1t = st.n

                @blk.vector
                def _(v):
                    v.wait_ge(st.h, st_b1t)
                    v.tensor_copy(b1_sb[:], b1t_ps[:])
                    v.memset(ones8[:], 1.0)
                    ins = v.tensor_copy(comb_bf[:], comb[:])
                    sv.inc(ins)
                sv_sel = sv.n

                seltr_done = {}
                for tt in range(TT):
                    @blk.tensor
                    def _(te, tt=tt):
                        te.wait_ge(sv.h, sv_sel)
                        te.wait_ge(sv.h, seltr_done.get(tt - 1, 0))
                        ins = te.transpose(trp8[:], sel_full[:, tt * E:(tt + 1) * E], id_32[:])
                        st.inc(ins)
                    st_tr = st.n

                    @blk.vector
                    def _(v, tt=tt, st_tr=st_tr):
                        v.wait_ge(st.h, st_tr)
                        ins = v.tensor_copy(selT[:, tt * 128:(tt + 1) * 128], trp8[:])
                        sv.inc(ins)
                        seltr_done[tt] = sv.n

                @blk.vector
                def _(v):
                    v.wait_ge(sv.h, seltr_done[TT - 1])
                    v.tensor_tensor_scan(posT8[:], ones8[:], selT[:], 0.0, ALU.mult, ALU.add).then_inc(sv.h, 1)
                    sv.n += 1
                    v.wait_ge(sv.h, sv.n)
                    v.tensor_tensor(posT8[:], posT8[:], selT[:], ALU.subtract).then_inc(sv.h, 1)
                    sv.n += 1
                    v.wait_ge(sv.h, sv.n)
                    v.tensor_scalar(mask8[:], selT[:], -1.0e9, 1.0e9, ALU.mult, ALU.add).then_inc(sv.h, 1)
                    sv.n += 1
                    v.wait_ge(sv.h, sv.n)
                    ins = v.tensor_tensor(posT8[:], posT8[:], mask8[:], ALU.add)
                    sv.inc(ins)
                sv_posT = sv.n

                postr_done = {}
                for tt in range(TT):
                    @blk.tensor
                    def _(te, tt=tt):
                        te.wait_ge(sv.h, sv_posT)
                        te.wait_ge(sv.h, postr_done.get(tt - 1, 0))
                        ins = te.transpose(trpb[:], posT8[:, tt * 128:(tt + 1) * 128], id_32[0:8, 0:8])
                        st.inc(ins)
                    st_tr = st.n

                    @blk.vector
                    def _(v, tt=tt, st_tr=st_tr):
                        v.wait_ge(st.h, st_tr)
                        ins = v.tensor_copy(posb[:, tt * E:(tt + 1) * E], trpb[:])
                        sv.inc(ins)
                        postr_done[tt] = sv.n
            sv_ffn_pre = sv.n

            xn_nat = f1.enter_context(nc.sbuf_tensor("xn_nat", [128, TT * D], BF16))
            u20 = f1.enter_context(nc.sbuf_tensor("u20", [128, TT * C], BF16))
            ET_sb = f1.enter_context(nc.sbuf_tensor("ET_sb", [128, SC5 * T], BF16))
            xn_selT = f1.enter_context(nc.sbuf_tensor("xn_selT", [128, DC * C], BF16))
            y_f32 = f1.enter_context(nc.sbuf_tensor("y_f32", [128, SC5 * 1024], F32))
            w1c = f1.enter_context(nc.sbuf_tensor("w1c", [128, 4 * DC * 128], BF16))
            w2q = f1.enter_context(nc.sbuf_tensor("w2q", [128, 8 * 1024], BF16))

            @blk.sync
            def _(sync):
                sync.wait_ge(sv.h, sv_ffn_pre)
                sync.wait_ge(wr.h, wr_xnT)
                sync.dma_start(
                    out=xn_nat[:].rearrange("p (g d) -> p g d", g=TT),
                    in_=xn_scr[:].rearrange("g p d -> p g d"),
                ).then_inc(ld.h, 16)
                ld.n += 16
            ld_xn_nat = ld.n

            p6 = ExitStack()
            with p6:
                hy_ps = [p6.enter_context(nc.psum_tensor(f"hy{i}", [128, 1024], F32)) for i in range(2)]
                trE_ps = [p6.enter_context(nc.psum_tensor(f"trE_ps{i}", [128, C], BF16)) for i in range(2)]
                csel_ps = p6.enter_context(nc.psum_tensor("csel_ps", [128, 8], F32))

                scatter_done = {0: 0}
                oflush_all = {0: 0}
                last_y_done = [0]
                prev_h_done = [0]
                prev_scmm = [None]

                for e in range(E):
                    if prev_scmm[0] is not None:
                        @blk.vector
                        def _(v, e=e, scmm_prev=prev_scmm[0]):
                            for tt in range(TT):
                                v.wait_ge(st.h, scmm_prev[tt])
                                ins = v.tensor_tensor(O_sb[:, tt * 1026:tt * 1026 + 1024],
                                                      O_sb[:, tt * 1026:tt * 1026 + 1024],
                                                      hy_ps[tt % 2][:], ALU.add)
                                sv.inc(ins)
                    oflush_all[e] = sv.n

                    @blk.vector
                    def _(v, e=e):
                        if e == 0:
                            v.wait_ge(sv.h, sv_ffn_pre)
                            v.wait_ge(sg.h, sg_iota)
                            v.wait_ge(ld.h, ld_ffn_pre)
                        for tt in range(8):
                            v.tensor_scalar(E_view(tt), iota_f[:],
                                            posb[:, tt * E + e:tt * E + e + 1],
                                            None, ALU.is_equal)

                    @blk.vector
                    def _(v, e=e):
                        v.wait_ge(st.h, scatter_done[e])
                        ins = None
                        for tt in range(8, TT):
                            ins = v.tensor_scalar(E_view(tt), iota_f[:],
                                                  posb[:, tt * E + e:tt * E + e + 1],
                                                  None, ALU.is_equal)
                        sv.inc(ins)
                    E_done = sv.n

                    gth_mm = {}

                    @blk.tensor
                    def _(te, e=e, E_done=E_done, gth_mm=gth_mm):
                        te.wait_ge(sv.h, E_done)
                        te.wait_ge(sv.h, oflush_all[e])
                        if e == 0:
                            te.wait_ge(ld.h, ld_xn_nat)
                        ss_base = ss.n
                        for j in range(DC):
                            if j >= 2:
                                te.wait_ge(ss.h, ss_base + j - 1)
                            g_ps = hy_ps[j % 2]
                            ins = None
                            for tt in range(TT):
                                te.matmul(g_ps[:, 0:512],
                                          xn_nat[:, tt * D + j * 128:tt * D + (j + 1) * 128],
                                          u20[:, tt * C:tt * C + 512],
                                          start=(tt == 0), stop=(tt == TT - 1))
                                ins = te.matmul(g_ps[:, 512:C],
                                                xn_nat[:, tt * D + j * 128:tt * D + (j + 1) * 128],
                                                u20[:, tt * C + 512:(tt + 1) * C],
                                                start=(tt == 0), stop=(tt == TT - 1))
                            st.inc(ins)
                            gth_mm[j] = st.n
                        for st5 in range(SC5):
                            ins = None
                            for tt in range(TT):
                                ins = te.matmul(csel_ps[:, st5:st5 + 1],
                                                u20[:, tt * C + st5 * 128:tt * C + (st5 + 1) * 128],
                                                comb_bf[:, tt * E + e:tt * E + e + 1],
                                                start=(tt == 0), stop=(tt == TT - 1))
                            st.inc(ins)
                        gth_mm["csel"] = st.n
                        svb = sv.n
                        for tt in range(TT):
                            if tt >= 2:
                                te.wait_ge(sv.h, svb + tt)
                            ins = None
                            for sc5 in range(SC5):
                                ins = te.transpose(trE_ps[tt % 2][:, sc5 * 128:(sc5 + 1) * 128],
                                                   u20[:, tt * C + sc5 * 128:tt * C + (sc5 + 1) * 128],
                                                   id_bf[:])
                            st.inc(ins)
                            gth_mm[("et", tt)] = st.n

                    @blk.scalar
                    def _(sc, e=e, gth_mm=gth_mm):
                        if e == 0:
                            sc.wait_ge(sv.h, sv_sel)
                        for j in range(DC):
                            sc.wait_ge(st.h, gth_mm[j])
                            ins = sc.activation(xn_selT[:, j * C:(j + 1) * C],
                                                hy_ps[j % 2][:, 0:C], AF.Copy)
                            ss.inc(ins)
                    gth_evac = ss.n

                    @blk.vector
                    def _(v, e=e, gth_mm=gth_mm):
                        v.wait_ge(st.h, gth_mm["csel"])
                        ins = v.tensor_copy(comb_sel[:, 0:SC5], csel_ps[:, 0:SC5])
                        sv.inc(ins)
                        for tt in range(TT):
                            v.wait_ge(st.h, gth_mm[("et", tt)])
                            dst = ET_sb[:].rearrange("p (s t) -> p s t", s=SC5)[:, :, tt * 128:(tt + 1) * 128]
                            src = trE_ps[tt % 2][:].rearrange("p (s c) -> p s c", c=128)
                            ins = v.tensor_copy(dst, src)
                            sv.inc(ins)
                    et_evac = sv.n
                    csel_evac_sv = et_evac - TT

                    yflush = {}
                    for fb in range(FQ):
                        hmm_d = {}
                        w1_ld = {}

                        @blk.gpsimd
                        def _(g, e=e, fb=fb, w1_ld=w1_ld):
                            if e == 0 and fb == 0:
                                g.wait_ge(sv.h, sv_ffn_pre)
                            g.wait_ge(st.h, prev_h_done[0])
                            st_base_g = st.n
                            for fc in range(4):
                                ins = g.dma_start(
                                    out=w1c[:, (fc % 4) * DC * 128:((fc % 4) + 1) * DC * 128].rearrange("p (j f) -> p j f", j=DC),
                                    in_=w1[e, :, fb * 1024 + fc * 128:fb * 1024 + (fc + 1) * 128].rearrange("(j p) f -> p j f", p=128),
                                )
                                ldg.inc(ins, 16)
                                w1_ld[fc] = ldg.n
                            if fb == 0:
                                ins = g.dma_start(out=b2e[:], in_=b2[e][None, :])
                                ldg.inc(ins, 16)
                            g.wait_ge(st.h, last_y_done[0])
                            ins = g.dma_start(
                                out=w2q[:].rearrange("p (c d) -> p c d", c=8),
                                in_=w2[e, fb * 1024:(fb + 1) * 1024, :].rearrange("(c p) d -> p c d", p=128),
                            )
                            ldg.inc(ins, 16)
                            for fc in range(4, 8):
                                g.wait_ge(st.h, st_base_g + fc - 3)
                                ins = g.dma_start(
                                    out=w1c[:, (fc % 4) * DC * 128:((fc % 4) + 1) * DC * 128].rearrange("p (j f) -> p j f", j=DC),
                                    in_=w1[e, :, fb * 1024 + fc * 128:fb * 1024 + (fc + 1) * 128].rearrange("(j p) f -> p j f", p=128),
                                )
                                ldg.inc(ins, 16)
                                w1_ld[fc] = ldg.n
                        w2_ld = ldg.n

                        @blk.tensor
                        def _(te, e=e, fb=fb, hmm_d=hmm_d, w1_ld=w1_ld,
                              gth_evac=gth_evac, et_evac=et_evac, yflush=yflush):
                            ss_base = ss.n
                            for fc in range(8):
                                te.wait_ge(ldg.h, w1_ld[fc])
                                if fb == 0 and fc == 0:
                                    te.wait_ge(ss.h, gth_evac)
                                    te.wait_ge(sv.h, et_evac)
                                if fb > 0 and fc < 2:
                                    te.wait_ge(sv.h, yflush[(fb - 1, 4 - fc)])
                                if fc >= 2:
                                    te.wait_ge(ss.h, ss_base + fc - 1)
                                h_ps = hy_ps[fc % 2]
                                ins = None
                                for n2 in range(2):
                                    lo = n2 * 512
                                    hi = min(C, lo + 512)
                                    for dd in range(DC):
                                        ins = te.matmul(h_ps[:, lo:hi],
                                                        w1c[:, (fc % 4) * DC * 128 + dd * 128:(fc % 4) * DC * 128 + (dd + 1) * 128],
                                                        xn_selT[:, dd * C + lo:dd * C + hi],
                                                        start=(dd == 0), stop=(dd == DC - 1))
                                st.inc(ins)
                                hmm_d[fc] = st.n
                        prev_h_done[0] = st.n

                        @blk.scalar
                        def _(sc, e=e, fb=fb, hmm_d=hmm_d):
                            for fc in range(8):
                                sc.wait_ge(st.h, hmm_d[fc])
                                bcol = e * (FF // 128) + fb * 8 + fc
                                ins = sc.activation(hq_view(fc), hy_ps[fc % 2][:, 0:C],
                                                    AF.Gelu, bias=b1_sb[:, bcol:bcol + 1], scale=1.0)
                                ss.inc(ins)
                        gelu_all = ss.n

                        ymm_d = {}

                        @blk.tensor
                        def _(te, e=e, fb=fb, gelu_all=gelu_all, ymm_d=ymm_d,
                              w2_ld=w2_ld, yflush=yflush):
                            te.wait_ge(ss.h, gelu_all)
                            te.wait_ge(ldg.h, w2_ld)
                            sv_base = sv.n
                            for st5 in range(SC5):
                                if st5 >= 2:
                                    te.wait_ge(sv.h, sv_base + st5 - 1)
                                elif fb > 0:
                                    te.wait_ge(sv.h, yflush.get((fb - 1, 4 - st5), 0))
                                y_ps = hy_ps[st5 % 2]
                                first0 = first1 = True
                                if fb == 0:
                                    te.matmul(y_ps[:, 0:512], ones1_bf[:], b2e[:, 0:512],
                                              start=True, stop=False)
                                    te.matmul(y_ps[:, 512:1024], ones1_bf[:], b2e[:, 512:1024],
                                              start=True, stop=False)
                                    first0 = first1 = False
                                ins = None
                                for fc in range(8):
                                    lhs = u20[:, fc * C + st5 * 128:fc * C + (st5 + 1) * 128]
                                    te.matmul(y_ps[:, 0:512], lhs, w2q[:, fc * 1024:fc * 1024 + 512],
                                              start=first0, stop=(fc == 7))
                                    ins = te.matmul(y_ps[:, 512:1024], lhs,
                                                    w2q[:, fc * 1024 + 512:fc * 1024 + 1024],
                                                    start=first1, stop=(fc == 7))
                                    first0 = first1 = False
                                st.inc(ins)
                                ymm_d[st5] = st.n
                        last_y_done[0] = st.n

                        @blk.vector
                        def _(v, e=e, fb=fb, ymm_d=ymm_d, yflush=yflush):
                            for st5 in range(SC5):
                                v.wait_ge(st.h, ymm_d[st5])
                                if fb == 0:
                                    ins = v.tensor_copy(y_f32[:, st5 * 1024:(st5 + 1) * 1024],
                                                        hy_ps[st5 % 2][:])
                                else:
                                    ins = v.tensor_tensor(y_f32[:, st5 * 1024:(st5 + 1) * 1024],
                                                          y_f32[:, st5 * 1024:(st5 + 1) * 1024],
                                                          hy_ps[st5 % 2][:], ALU.add)
                                sv.inc(ins)
                                yflush[(fb, st5)] = sv.n

                    @blk.vector
                    def _(v, e=e, csel_evac_sv=csel_evac_sv):
                        v.wait_ge(sv.h, csel_evac_sv)
                        ins = None
                        for st5 in range(SC5):
                            ins = v.tensor_scalar(ybf_view(st5), y_f32[:, st5 * 1024:(st5 + 1) * 1024],
                                                  comb_sel[:, st5:st5 + 1], None, ALU.mult)
                        sv.inc(ins)
                    ybf_done = sv.n

                    scmm_d = {}

                    @blk.tensor
                    def _(te, e=e, ybf_done=ybf_done, scmm_d=scmm_d):
                        te.wait_ge(sv.h, ybf_done)
                        sv_base = sv.n
                        for tt in range(TT):
                            if tt >= 2:
                                te.wait_ge(sv.h, sv_base + tt - 1)
                            s_ps2 = hy_ps[tt % 2]
                            ins = None
                            for sc5 in range(SC5):
                                lhs = ET_sb[:, sc5 * T + tt * 128:sc5 * T + (tt + 1) * 128]
                                te.matmul(s_ps2[:, 0:512], lhs,
                                          u20[:, 8 * C + sc5 * 1024:8 * C + sc5 * 1024 + 512],
                                          start=(sc5 == 0), stop=(sc5 == SC5 - 1))
                                ins = te.matmul(s_ps2[:, 512:1024], lhs,
                                                u20[:, 8 * C + sc5 * 1024 + 512:8 * C + (sc5 + 1) * 1024],
                                                start=(sc5 == 0), stop=(sc5 == SC5 - 1))
                            st.inc(ins)
                            scmm_d[tt] = st.n
                        scatter_done[e + 1] = st.n

                    prev_scmm[0] = scmm_d

                @blk.vector
                def _(v, scmm_prev=prev_scmm[0]):
                    for tt in range(TT):
                        v.wait_ge(st.h, scmm_prev[tt])
                        ins = v.tensor_tensor(O_sb[:, tt * 1026:tt * 1026 + 1024],
                                              O_sb[:, tt * 1026:tt * 1026 + 1024],
                                              hy_ps[tt % 2][:], ALU.add)
                        sv.inc(ins)
                ffn_all = sv.n

                @blk.sync
                def _(sync):
                    sync.wait_ge(sv.h, ffn_all)
                    sync.dma_start(
                        out=out[:].rearrange("(t p) d -> p t d", p=128),
                        in_=O_sb[:].rearrange("p (t c) -> p t c", c=1026)[:, :, 0:1024],
                    ).then_inc(so.h, 16)
                    so.n += 16
                    sync.wait_ge(so.h, so.n)

                @blk.gpsimd
                def _(g):
                    g.wait_ge(so.h, so.n)

                blk.__exit__(None, None, None)

    return nc


_NC_CACHE = [None]


def kernel(L0_attn_x, L0_attn_W_q, L0_attn_W_k, L0_attn_W_v,
           L0_ffn_gate_w, L0_ffn_w1, L0_ffn_b1, L0_ffn_w2, L0_ffn_b2):
    x = np.ascontiguousarray(np.asarray(L0_attn_x, dtype=np.float32).reshape(S, D))
    wq_np = np.ascontiguousarray(np.asarray(L0_attn_W_q, dtype=np.float32))
    wk_np = np.ascontiguousarray(np.asarray(L0_attn_W_k, dtype=np.float32))
    wv_np = np.ascontiguousarray(np.asarray(L0_attn_W_v, dtype=np.float32))
    gw_np = np.ascontiguousarray(np.asarray(L0_ffn_gate_w, dtype=np.float32))
    w1_np = np.ascontiguousarray(np.asarray(L0_ffn_w1, dtype=np.float32))
    b1_np = np.ascontiguousarray(np.asarray(L0_ffn_b1, dtype=np.float32))
    w2_np = np.ascontiguousarray(np.asarray(L0_ffn_w2, dtype=np.float32))
    b2_np = np.ascontiguousarray(np.asarray(L0_ffn_b2, dtype=np.float32))

    if _NC_CACHE[0] is None:
        _NC_CACHE[0] = build_nc()
    nc = _NC_CACHE[0]

    in_maps = []
    for c in range(NCORES):
        in_maps.append({
            "x_own": np.ascontiguousarray(x[c * T:(c + 1) * T]),
            "x_kv": np.ascontiguousarray(x[(c % 2) * (S // 2):((c % 2) + 1) * (S // 2)]),
            "wq": wq_np, "wk": wk_np, "wv": wv_np,
            "gate_w": gw_np,
            "w1": w1_np, "b1": b1_np, "w2": w2_np, "b2": b2_np,
        })
    res = run_bass_kernel_spmd(nc, in_maps, core_ids=list(range(NCORES)))
    shards = [res.results[c]["out"] for c in range(NCORES)]
    return np.concatenate(shards, axis=0).reshape(1, S, D).astype(np.float32, copy=False)


# version 17
# speedup vs baseline: 1.1284x; 1.0074x over previous
import numpy as np
from contextlib import ExitStack

import ml_dtypes
import concourse.bass as bass
import concourse.mybir as mybir
from concourse.bass_utils import run_bass_kernel_spmd

F32 = mybir.dt.float32
BF16 = mybir.dt.bfloat16
AF = mybir.ActivationFunctionType
ALU = mybir.AluOpType
AX = mybir.AxisListType

NCORES = 8
S = 16384
D = 1024
T = S // NCORES
E = 8
FF = 4096
DC = D // 128
TT = T // 128
NCH = S // 1024
EPS = 1e-5
SCALE = 1.0 / 32.0
NEG_LN32 = float(-np.log(32.0))


class Ctr:
    def __init__(self, es, nc, name):
        self.h = es.enter_context(nc.semaphore(name))
        self.n = 0

    def inc(self, ins, k=1):
        ins.then_inc(self.h, k)
        self.n += k
        return self.n


def _ln_group(blk, srcs, xst, sq_scr, stat, xn32, xn_bf, want32,
              slots, ld, sv, ss, eps_ap, wait_loads=(), wait_xn=(), xn_slots=None,
              stat_slots=None):
    n = len(srcs)
    st_sl = stat_slots if stat_slots is not None else slots

    @blk.sync
    def _(sync):
        for (h, v) in wait_loads:
            if v > 0:
                sync.wait_ge(h, v)
        for i in range(n):
            sync.dma_start(out=xst[:, slots[i] * D:(slots[i] + 1) * D],
                           in_=srcs[i]).then_inc(ld.h, 16)
            ld.n += 16
    ld_grp = ld.n

    @blk.vector
    def _(v):
        v.wait_ge(ld.h, ld_grp)
        for i in range(n):
            c = st_sl[i] * 16
            v.bn_stats(stat[:, c:c + 6], xst[:, slots[i] * D:slots[i] * D + 512])
            v.bn_stats(stat[:, c + 6:c + 12], xst[:, slots[i] * D + 512:(slots[i] + 1) * D])
        for i in range(n):
            c = st_sl[i] * 16
            v.bn_aggr(stat[:, c + 12:c + 14], stat[:, c:c + 12])
        ins = None
        for i in range(n):
            c = st_sl[i] * 16
            ins = v.tensor_scalar(stat[:, c + 15:c + 16], stat[:, c + 12:c + 13], -1.0, None, ALU.mult)
        sv.inc(ins)
    sv_stats = sv.n

    @blk.scalar
    def _(sc):
        sc.wait_ge(sv.h, sv_stats)
        for i in range(n):
            c = st_sl[i] * 16
            sc.activation(stat[:, c:c + 1], stat[:, c + 13:c + 14], AF.Ln, bias=eps_ap, scale=1.0)
        ins = None
        for i in range(n):
            c = st_sl[i] * 16
            ins = sc.activation(stat[:, c + 14:c + 15], stat[:, c:c + 1], AF.Exp, scale=-0.5)
        ss.inc(ins)
    ss_inv = ss.n

    xn_done = {}

    @blk.vector
    def _(v):
        v.wait_ge(ss.h, ss_inv)
        for (h, val) in wait_xn:
            if val > 0:
                v.wait_ge(h, val)
        xs = xn_slots if xn_slots is not None else slots
        for i in range(n):
            c = st_sl[i] * 16
            sl = slots[i]
            xo = xs[i]
            if want32:
                v.tensor_scalar(xn32[:, xo * D:(xo + 1) * D],
                                xst[:, sl * D:(sl + 1) * D],
                                stat[:, c + 15:c + 16], stat[:, c + 14:c + 15],
                                ALU.add, ALU.mult)
                ins = v.tensor_copy(xn_bf[:, xo * D:(xo + 1) * D],
                                    xn32[:, xo * D:(xo + 1) * D])
            else:
                ins = v.tensor_scalar(xn_bf[:, xo * D:(xo + 1) * D],
                                      xst[:, sl * D:(sl + 1) * D],
                                      stat[:, c + 15:c + 16], stat[:, c + 14:c + 15],
                                      ALU.add, ALU.mult)
            sv.inc(ins)
            xn_done[i] = sv.n
    return xn_done


def build_nc():
    nc = bass.Bass("TRN2", target_bir_lowering=False, enable_partition_id=True)
    _pid = nc.scalar.partition_id()
    kv_off = nc.scalar.scalar_reg_alu(ALU.mult, nc.scalar.scalar_reg_alu(ALU.mod, _pid, 2), S // 2)
    _ET = mybir.EngineType
    _par_regs = nc.alloc_registers("parreg", engines=[_ET.Activation])
    nc.regs_mov(_par_regs, _pid)
    nc.scalar.scalar_reg_alu(ALU.mod, _par_regs.handles[0], 2)
    _par = _par_regs.handles[0]
    _pid_sp = nc.sync.partition_id()
    _parsp_regs = nc.alloc_registers("parsp", engines=[_ET.SP])
    nc.regs_mov(_parsp_regs, _pid_sp)
    nc.sync.scalar_reg_alu(ALU.mod, _parsp_regs.handles[0], 2)
    _parsp = _parsp_regs.handles[0]

    x_own = nc.dram_tensor("x_own", [T, D], F32, kind="ExternalInput")
    x_kv = nc.dram_tensor("x_kv", [S // 2, D], F32, kind="ExternalInput")
    wq = nc.dram_tensor("wq", [D, D], F32, kind="ExternalInput")
    wk = nc.dram_tensor("wk", [D, D], F32, kind="ExternalInput")
    wv = nc.dram_tensor("wv", [D, D], F32, kind="ExternalInput")
    gate_w = nc.dram_tensor("gate_w", [D, E], F32, kind="ExternalInput")
    w1 = nc.dram_tensor("w1", [E, D, FF], F32, kind="ExternalInput")
    b1 = nc.dram_tensor("b1", [E, FF], F32, kind="ExternalInput")
    w2 = nc.dram_tensor("w2", [E, FF, D], F32, kind="ExternalInput")
    b2 = nc.dram_tensor("b2", [E, D], F32, kind="ExternalInput")
    out = nc.dram_tensor("out", [T, D], F32, kind="ExternalOutput")

    kT_sh = nc.dram_tensor("kT_sh", [DC, 128, S], BF16, addr_space="Shared")
    v_sh = nc.dram_tensor("v_sh", [S, D], BF16, addr_space="Shared")
    km_in = nc.dram_tensor("km_in", [1, 8], F32)
    km_out = nc.dram_tensor("km_out", [1, 8], F32)
    xnT_scr = nc.dram_tensor("xnT_scr", [DC, 128, T], BF16)
    xn_scr = nc.dram_tensor("xn_scr", [TT, 128, D], BF16)

    id_bf_c = nc.inline_tensor(np.eye(128, dtype=ml_dtypes.bfloat16), name="id_bf_c")
    id_32_c = nc.inline_tensor(np.eye(128, dtype=np.float32), name="id_32_c")

    es = ExitStack()
    with es:
        O_sb = es.enter_context(nc.sbuf_tensor("O_sb", [128, TT * 1026], F32))
        comb = es.enter_context(nc.sbuf_tensor("comb", [128, TT * E], F32))
        sel_full = es.enter_context(nc.sbuf_tensor("sel_full", [128, TT * E], F32))
        negB = es.enter_context(nc.sbuf_tensor("negB", [128, 1], F32))
        id_bf = es.enter_context(nc.sbuf_tensor("id_bf", [128, 128], BF16))
        id_32 = es.enter_context(nc.sbuf_tensor("id_32", [128, 128], F32))
        ones2_bf = es.enter_context(nc.sbuf_tensor("ones2_bf", [128, 2], BF16))
        onesk_bf = es.enter_context(nc.sbuf_tensor("onesk_bf", [128, 1], BF16))
        ones1_bf = es.enter_context(nc.sbuf_tensor("ones1_bf", [1, 128], BF16))
        kmax2 = es.enter_context(nc.sbuf_tensor("kmax2", [1, 2], F32))
        qmax2 = es.enter_context(nc.sbuf_tensor("qmax2", [1, 2], F32))
        bstat = es.enter_context(nc.sbuf_tensor("bstat", [1, 8], F32))
        nb_bf = es.enter_context(nc.sbuf_tensor("nb_bf", [1, 1], BF16))
        eps_sb = es.enter_context(nc.sbuf_tensor("eps_sb", [128, 1], F32))
        nl32_sb = es.enter_context(nc.sbuf_tensor("nl32_sb", [1, 1], F32))
        qt_guard = nc.sbuf_tensor("QT", [128, DC * T], BF16)
        QT = qt_guard.__enter__()

        ld = Ctr(es, nc, "ld")
        ldg = Ctr(es, nc, "ldg")
        wr = Ctr(es, nc, "wr")
        sv = Ctr(es, nc, "sv")
        ss = Ctr(es, nc, "ss")
        st = Ctr(es, nc, "st")
        so = Ctr(es, nc, "so")
        sg = Ctr(es, nc, "sg")
        cc = Ctr(es, nc, "cc")

        g1 = ExitStack()
        with g1:
            xst = g1.enter_context(nc.sbuf_tensor("xst", [128, 4 * D], F32))
            sq_scr = g1.enter_context(nc.sbuf_tensor("sq_scr", [128, D], F32))
            stat = g1.enter_context(nc.sbuf_tensor("stat", [128, 64], F32))
            xn32 = g1.enter_context(nc.sbuf_tensor("xn32", [128, 4 * D], F32))
            xn_bf = g1.enter_context(nc.sbuf_tensor("xn_bf", [128, 4 * D], BF16))
            wq_bf = g1.enter_context(nc.sbuf_tensor("wq_bf", [128, DC * D], BF16))
            gate32 = g1.enter_context(nc.sbuf_tensor("gate32", [128, DC * E], F32))
            xnT_own = g1.enter_context(nc.sbuf_tensor("xnT_own", [128, DC * T], BF16))
            xnT32_t = g1.enter_context(nc.sbuf_tensor("xnT32_t", [128, D], F32))
            glog = g1.enter_context(nc.sbuf_tensor("glog", [128, 4 * E], F32))
            mx2 = g1.enter_context(nc.sbuf_tensor("mx2", [128, 4 * E], F32))
            cstat = g1.enter_context(nc.sbuf_tensor("cstat", [128, 16], F32))
            expl = g1.enter_context(nc.sbuf_tensor("expl", [128, 4 * E], F32))
            maskt = g1.enter_context(nc.sbuf_tensor("maskt", [128, 4 * E], F32))
            qsq_bf = g1.enter_context(nc.sbuf_tensor("qsq_bf", [128, T], BF16))

            p1 = ExitStack()
            with p1:
                trbf_ps = [p1.enter_context(nc.psum_tensor(f"trbf{i}", [128, D], BF16)) for i in range(2)]
                tr32_ps = p1.enter_context(nc.psum_tensor("tr32", [128, D], F32))
                gate_ps = p1.enter_context(nc.psum_tensor("gate_ps", [128, E], F32))
                qt_ps = [p1.enter_context(nc.psum_tensor(f"qt{i}", [128, 512], F32)) for i in range(2)]

                blk = nc.Block("gen_own"); blk.__enter__()

                @blk.sync
                def _(sync):
                    sync.dma_start(out=id_bf[:], in_=id_bf_c[:]).then_inc(ld.h, 16)
                    sync.dma_start(out=id_32[:], in_=id_32_c[:]).then_inc(ld.h, 16)
                    sync.dma_start(
                        out=gate32[:].rearrange("p (j e) -> p j e", j=DC),
                        in_=gate_w[:].rearrange("(j p) e -> p j e", p=128),
                    ).then_inc(ld.h, 16)
                    ld.n += 48
                ld_const = ld.n

                @blk.vector
                def _(v):
                    v.memset(ones2_bf[:], 1.0)
                    v.memset(onesk_bf[:], 1.0)
                    v.memset(ones1_bf[:], 1.0)
                    v.memset(kmax2[:], 0.0)
                    v.memset(eps_sb[:], EPS)
                    ins = v.memset(nl32_sb[:], NEG_LN32)
                    sv.inc(ins)

                @blk.gpsimd
                def _(g):
                    g.dma_start(
                        out=wq_bf[:].rearrange("p (j d) -> p j d", j=DC),
                        in_=wq[:].rearrange("(j p) d -> p j d", p=128),
                    ).then_inc(ldg.h, 16)
                    ldg.n += 16
                ldg_wq = ldg.n

                grp_tr_done = {}
                grp_xn_done = {}
                grp_xnwr_done = {}
                evac_done = {}
                gate_done_d = {}
                glog_done = {}

                for grp in range(4):
                    tiles = [grp * 4 + i for i in range(4)]
                    xn_done = _ln_group(
                        blk,
                        [x_own[g * 128:(g + 1) * 128, :] for g in tiles],
                        xst, sq_scr, stat, xn32, xn_bf, True,
                        [i for i in range(4)],
                        ld, sv, ss, eps_sb[:],
                        wait_loads=[(sv.h, grp_xn_done.get(grp - 1, 0))],
                        wait_xn=[(st.h, grp_tr_done.get(grp - 1, 0)),
                                 (wr.h, grp_xnwr_done.get(grp - 1, 0))],
                    )
                    grp_xn_done[grp] = sv.n

                    @blk.scalar
                    def _(sc, tiles=tiles, xn_done=xn_done):
                        for i, g in enumerate(tiles):
                            sc.wait_ge(sv.h, xn_done[i])
                            ins = sc.dma_start(out=xn_scr[g, :, :],
                                               in_=xn_bf[:, (g % 4) * D:(g % 4 + 1) * D])
                            wr.inc(ins, 16)
                    grp_xnwr_done[grp] = wr.n

                    for i, g in enumerate(tiles):
                        @blk.tensor
                        def _(te, g=g, i=i, xn_done=xn_done):
                            te.wait_ge(sv.h, xn_done[i])
                            te.wait_ge(sv.h, evac_done.get(g - 2, 0))
                            te.wait_ge(sv.h, evac_done.get(g - 1, 0))
                            for j in range(DC):
                                te.transpose(trbf_ps[g % 2][:, j * 128:(j + 1) * 128],
                                             xn_bf[:, (g % 4) * D + j * 128:(g % 4) * D + (j + 1) * 128],
                                             id_bf[:])
                            ins = None
                            for j in range(DC):
                                ins = te.transpose(tr32_ps[:, j * 128:(j + 1) * 128],
                                                   xn32[:, (g % 4) * D + j * 128:(g % 4) * D + (j + 1) * 128],
                                                   id_32[:])
                            st.inc(ins)
                        tr_done = st.n

                        @blk.vector
                        def _(v, g=g, tr_done=tr_done):
                            v.wait_ge(st.h, tr_done)
                            v.wait_ge(st.h, gate_done_d.get(g - 1, 0))
                            dst = xnT_own[:].rearrange("p (j t) -> p j t", j=DC)[:, :, g * 128:(g + 1) * 128]
                            src = trbf_ps[g % 2][:].rearrange("p (j c) -> p j c", j=DC)
                            v.tensor_copy(dst, src)
                            ins = v.tensor_copy(xnT32_t[:], tr32_ps[:])
                            sv.inc(ins)
                            evac_done[g] = sv.n

                        @blk.tensor
                        def _(te, g=g):
                            te.wait_ge(sv.h, evac_done[g])
                            te.wait_ge(ld.h, ld_const)
                            te.wait_ge(sv.h, glog_done.get(g - 1, 0))
                            ins = None
                            for j in range(DC):
                                ins = te.matmul(gate_ps[:], xnT32_t[:, j * 128:(j + 1) * 128],
                                                gate32[:, j * E:(j + 1) * E],
                                                start=(j == 0), stop=(j == DC - 1))
                            st.inc(ins)
                            gate_done_d[g] = st.n

                        @blk.vector
                        def _(v, g=g):
                            v.wait_ge(st.h, gate_done_d[g])
                            ins = v.tensor_copy(glog[:, (g % 4) * E:(g % 4 + 1) * E], gate_ps[:])
                            sv.inc(ins)
                            glog_done[g] = sv.n
                    grp_tr_done[grp] = st.n

                    @blk.vector
                    def _(v, tiles=tiles):
                        v.wait_ge(sv.h, glog_done[tiles[-1]])
                        for g in tiles:
                            s4 = g % 4
                            v.max(out=mx2[:, s4 * E:(s4 + 1) * E], in_=glog[:, s4 * E:(s4 + 1) * E])
                        ins = None
                        for g in tiles:
                            s4 = g % 4
                            ins = v.tensor_scalar(cstat[:, s4 * 4:s4 * 4 + 1], mx2[:, s4 * E:s4 * E + 1],
                                                  -1.0, None, ALU.mult)
                        sv.inc(ins)
                    sv_negm1 = sv.n

                    @blk.scalar
                    def _(sc, tiles=tiles, sv_negm1=sv_negm1):
                        sc.wait_ge(sv.h, sv_negm1)
                        for g in tiles:
                            s4 = g % 4
                            sc.activation(cstat[:, s4 * 4 + 1:s4 * 4 + 2], mx2[:, s4 * E + 1:s4 * E + 2],
                                          AF.Exp, bias=cstat[:, s4 * 4:s4 * 4 + 1], scale=1.0)
                        ins = None
                        for g in tiles:
                            s4 = g % 4
                            ins = sc.activation(expl[:, s4 * E:(s4 + 1) * E], glog[:, s4 * E:(s4 + 1) * E],
                                                AF.Exp, bias=cstat[:, s4 * 4:s4 * 4 + 1], scale=1.0)
                        ss.inc(ins)
                    ss_cexp = ss.n

                    @blk.vector
                    def _(v, tiles=tiles, ss_cexp=ss_cexp):
                        v.wait_ge(ss.h, ss_cexp)
                        for g in tiles:
                            s4 = g % 4
                            v.tensor_scalar(cstat[:, s4 * 4 + 2:s4 * 4 + 3], cstat[:, s4 * 4 + 1:s4 * 4 + 2],
                                            1.0, None, ALU.add)
                        for g in tiles:
                            s4 = g % 4
                            v.reciprocal(cstat[:, s4 * 4 + 3:s4 * 4 + 4], cstat[:, s4 * 4 + 2:s4 * 4 + 3])
                        for g in tiles:
                            s4 = g % 4
                            v.tensor_scalar(maskt[:, s4 * E:(s4 + 1) * E], glog[:, s4 * E:(s4 + 1) * E],
                                            mx2[:, s4 * E + 1:s4 * E + 2], None, ALU.is_ge)
                        for g in tiles:
                            s4 = g % 4
                            v.tensor_copy(sel_full[:, g * E:(g + 1) * E], maskt[:, s4 * E:(s4 + 1) * E])
                        for g in tiles:
                            s4 = g % 4
                            v.tensor_tensor(comb[:, g * E:(g + 1) * E], expl[:, s4 * E:(s4 + 1) * E],
                                            maskt[:, s4 * E:(s4 + 1) * E], ALU.mult)
                        ins = None
                        for g in tiles:
                            s4 = g % 4
                            ins = v.tensor_scalar(comb[:, g * E:(g + 1) * E], comb[:, g * E:(g + 1) * E],
                                                  cstat[:, s4 * 4 + 3:s4 * 4 + 4], None, ALU.mult)
                        sv.inc(ins)

                qt_mm_done = {}
                NQT = T // 512

                @blk.tensor
                def _(te):
                    te.wait_ge(sv.h, evac_done[TT - 1])
                    te.wait_ge(ldg.h, ldg_wq)
                    ss_base = ss.n
                    for j in range(DC):
                        for n in range(NQT):
                            idx = j * NQT + n
                            if idx >= 2:
                                te.wait_ge(ss.h, ss_base + idx - 1)
                            ins = None
                            for dd in range(DC):
                                ins = te.matmul(qt_ps[idx % 2][:],
                                                wq_bf[:, dd * D + j * 128:dd * D + (j + 1) * 128],
                                                xnT_own[:, dd * T + n * 512:dd * T + (n + 1) * 512],
                                                start=(dd == 0), stop=(dd == DC - 1))
                            st.inc(ins)
                            qt_mm_done[idx] = st.n

                @blk.scalar
                def _(sc):
                    for j in range(DC):
                        for n in range(NQT):
                            idx = j * NQT + n
                            sc.wait_ge(st.h, qt_mm_done[idx])
                            ins = sc.activation(QT[:, j * T + n * 512:j * T + (n + 1) * 512],
                                                qt_ps[idx % 2][:], AF.Copy)
                            ss.inc(ins)
                ss_qt_all = ss.n

                @blk.scalar
                def _(sc):
                    for j in range(DC):
                        ins = sc.dma_start(out=xnT_scr[j, :, :], in_=xnT_own[:, j * T:(j + 1) * T])
                        wr.inc(ins, 16)
                wr_xnT = wr.n

            p2 = ExitStack()
            with p2:
                qn2_ps = p2.enter_context(nc.psum_tensor("qn2", [1, T], F32))
                qsq_done = {}
                qn_mm_done = {}
                for j in range(DC):
                    @blk.scalar
                    def _(sc, j=j):
                        if j == 0:
                            sc.wait_ge(ss.h, ss_qt_all)
                        else:
                            sc.wait_ge(st.h, qn_mm_done.get(j - 1, 0))
                        ins = sc.activation(qsq_bf[:], QT[:, j * T:(j + 1) * T], AF.Square)
                        ss.inc(ins)
                        qsq_done[j] = ss.n

                    @blk.tensor
                    def _(te, j=j):
                        te.wait_ge(ss.h, qsq_done[j])
                        ins = None
                        for n in range(T // 512):
                            ins = te.matmul(qn2_ps[:, n * 512:(n + 1) * 512],
                                            onesk_bf[:], qsq_bf[:, n * 512:(n + 1) * 512],
                                            start=(j == 0), stop=(j == DC - 1))
                        st.inc(ins)
                        qn_mm_done[j] = st.n

                @blk.vector
                def _(v):
                    v.wait_ge(st.h, qn_mm_done[DC - 1])
                    ins = v.reduce_max(qmax2[:, 0:1], qn2_ps[:], axis=AX.X)
                    sv.inc(ins)
            blk.__exit__(None, None, None)
            sv_qmax = sv.n

        g2 = ExitStack()
        with g2:
            xst = g2.enter_context(nc.sbuf_tensor("xst2", [128, 4 * D], F32))
            stat = g2.enter_context(nc.sbuf_tensor("stat2", [128, 128], F32))
            xn_bf = g2.enter_context(nc.sbuf_tensor("xn_bf2", [128, 4 * D], BF16))
            wk_bf = g2.enter_context(nc.sbuf_tensor("wk_bf", [128, DC * D], BF16))
            wv_bf = g2.enter_context(nc.sbuf_tensor("wv_bf", [128, DC * D], BF16))
            xnT_kv = g2.enter_context(nc.sbuf_tensor("xnT_kv", [128, 2 * DC * 1024], BF16))
            ktb = g2.enter_context(nc.sbuf_tensor("ktb", [128, 2 * 1024], BF16))
            vtb = g2.enter_context(nc.sbuf_tensor("vtb", [128, 2 * 1024], BF16))
            ksq_bf = g2.enter_context(nc.sbuf_tensor("ksq_bf", [128, 2 * 1024], BF16))
            km_sb8 = g2.enter_context(nc.sbuf_tensor("km_sb8", [1, 8], F32))

            p3 = ExitStack()
            with p3:
                trbf_ps = [p3.enter_context(nc.psum_tensor(f"trbf2_{i}", [128, D], BF16)) for i in range(2)]
                kt_ps = [p3.enter_context(nc.psum_tensor(f"ktp{i}", [128, 512], F32)) for i in range(2)]
                v_ps = [p3.enter_context(nc.psum_tensor(f"vp{i}", [128, 512], F32)) for i in range(2)]
                kn2_ps = p3.enter_context(nc.psum_tensor("kn2", [1, 1024], F32))

                blk = nc.Block("gen_full"); blk.__enter__()

                @blk.gpsimd
                def _(g):
                    g.dma_start(
                        out=wk_bf[:].rearrange("p (j d) -> p j d", j=DC),
                        in_=wk[:].rearrange("(j p) d -> p j d", p=128),
                    ).then_inc(ldg.h, 16)
                    g.dma_start(
                        out=wv_bf[:].rearrange("p (j d) -> p j d", j=DC),
                        in_=wv[:].rearrange("(j p) d -> p j d", p=128),
                    ).then_inc(ldg.h, 16)
                    ldg.n += 32
                ldg_wkv = ldg.n

                ch_mm_done = {}
                ch_kmax_done = {}
                trevac_done = {}
                prev_xn_sv = {0: 0, 1: 0}
                prev_tr_st = [0]
                NCHH = NCH // 2

                for ch in range(NCHH):
                    base = (ch % 2) * DC * 1024
                    for grp in range(2):
                        tiles = [ch * 8 + grp * 4 + i for i in range(4)]
                        xn_done = _ln_group(
                            blk,
                            [x_kv[g * 128:(g + 1) * 128, :] for g in tiles],
                            xst if grp % 2 == 0 else O_sb, None, stat, None, xn_bf, False,
                            [i for i in range(4)],
                            ld, sv, ss, eps_sb[:],
                            wait_loads=[(sv.h, prev_xn_sv[grp % 2])],
                            wait_xn=[(st.h, prev_tr_st[0])],
                            xn_slots=[i for i in range(4)],
                            stat_slots=[(grp % 2) * 4 + i for i in range(4)],
                        )
                        prev_xn_sv[grp % 2] = sv.n
                        for i in range(4):
                            tl = grp * 4 + i
                            @blk.tensor
                            def _(te, tl=tl, i=i, ch=ch, xn_done=xn_done, base=base):
                                te.wait_ge(sv.h, xn_done[i])
                                pk = (ch, tl - 2) if tl >= 2 else (ch - 1, 6 + tl)
                                if pk in trevac_done:
                                    h_, v_ = trevac_done[pk]
                                    te.wait_ge(h_, v_)
                                ins = None
                                for j in range(DC):
                                    ins = te.transpose(trbf_ps[tl % 2][:, j * 128:(j + 1) * 128],
                                                       xn_bf[:, i * D + j * 128:i * D + (j + 1) * 128],
                                                       id_bf[:])
                                st.inc(ins)
                            tr_done = st.n

                            if tl % 2 == 0:
                                @blk.vector
                                def _(v, tl=tl, tr_done=tr_done, ch=ch, base=base):
                                    v.wait_ge(st.h, tr_done)
                                    v.wait_ge(st.h, ch_mm_done.get(ch - 2, 0))
                                    dst = xnT_kv[:, base:base + DC * 1024].rearrange(
                                        "p (j t) -> p j t", j=DC)[:, :, tl * 128:(tl + 1) * 128]
                                    src = trbf_ps[tl % 2][:].rearrange("p (j c) -> p j c", j=DC)
                                    ins = v.tensor_copy(dst, src)
                                    sv.inc(ins)
                                    trevac_done[(ch, tl)] = (sv.h, sv.n)
                            else:
                                @blk.scalar
                                def _(sc, tl=tl, tr_done=tr_done, ch=ch, base=base):
                                    sc.wait_ge(st.h, tr_done)
                                    sc.wait_ge(st.h, ch_mm_done.get(ch - 2, 0))
                                    dst = xnT_kv[:, base:base + DC * 1024].rearrange(
                                        "p (j t) -> p j t", j=DC)[:, :, tl * 128:(tl + 1) * 128]
                                    src = trbf_ps[tl % 2][:].rearrange("p (j c) -> p j c", j=DC)
                                    ins = sc.activation(dst, src, AF.Copy)
                                    ss.inc(ins)
                                    trevac_done[(ch, tl)] = (ss.h, ss.n)
                        prev_tr_st[0] = st.n
                    sv_evac_ch = sv.n
                    ss_evac_ch = ss.n

                    kt_mm_done = {}
                    kt_evac_actual = {}
                    kt_wr_done = {}
                    ksq_done = {}

                    @blk.tensor
                    def _(te, ch=ch, base=base, sv_evac_ch=sv_evac_ch, ss_evac_ch=ss_evac_ch):
                        te.wait_ge(sv.h, sv_evac_ch)
                        te.wait_ge(ss.h, ss_evac_ch)
                        if ch == 0:
                            te.wait_ge(ldg.h, ldg_wkv)
                        ss_base = ss.n
                        for j in range(DC):
                            for n in range(2):
                                idx = j * 2 + n
                                if idx >= 2:
                                    te.wait_ge(ss.h, ss_base + idx - 1)
                                ins = None
                                for dd in range(DC):
                                    ins = te.matmul(kt_ps[idx % 2][:],
                                                    wk_bf[:, dd * D + j * 128:dd * D + (j + 1) * 128],
                                                    xnT_kv[:, base + dd * 1024 + n * 512:base + dd * 1024 + (n + 1) * 512],
                                                    start=(dd == 0), stop=(dd == DC - 1))
                                st.inc(ins)
                                kt_mm_done[idx] = st.n

                    @blk.scalar
                    def _(sc, ch=ch):
                        for j in range(DC):
                            for n in range(2):
                                idx = j * 2 + n
                                sc.wait_ge(st.h, kt_mm_done[idx])
                                if j >= 2:
                                    sc.wait_ge(wr.h, kt_wr_done.get(j - 2, 0))
                                    sc.wait_ge(sv.h, ksq_done.get(j - 2, 0))
                                ins = sc.activation(
                                    ktb[:, (j % 2) * 1024 + n * 512:(j % 2) * 1024 + (n + 1) * 512],
                                    kt_ps[idx % 2][:], AF.Copy)
                                ss.inc(ins)
                            kt_evac_actual[j] = ss.n
                            for p in range(2):
                                with sc.If_eq(_par, p):
                                    sc.dma_start(
                                        out=kT_sh[j, :, p * (S // 2) + ch * 1024:p * (S // 2) + (ch + 1) * 1024],
                                        in_=ktb[:, (j % 2) * 1024:(j % 2 + 1) * 1024],
                                    ).then_inc(wr.h, 16)
                            wr.n += 16
                            kt_wr_done[j] = wr.n

                    @blk.vector
                    def _(v, ch=ch):
                        for j in range(DC):
                            v.wait_ge(ss.h, kt_evac_actual[j])
                            ins = v.tensor_tensor(ksq_bf[:, (j % 2) * 1024:(j % 2 + 1) * 1024],
                                                  ktb[:, (j % 2) * 1024:(j % 2 + 1) * 1024],
                                                  ktb[:, (j % 2) * 1024:(j % 2 + 1) * 1024], ALU.mult)
                            sv.inc(ins)
                            ksq_done[j] = sv.n

                    @blk.tensor
                    def _(te, ch=ch):
                        te.wait_ge(sv.h, ch_kmax_done.get(ch - 1, 0))
                        ins = None
                        for j in range(DC):
                            te.wait_ge(sv.h, ksq_done[j])
                            for n in range(2):
                                ins = te.matmul(kn2_ps[:, n * 512:(n + 1) * 512],
                                                onesk_bf[:],
                                                ksq_bf[:, (j % 2) * 1024 + n * 512:(j % 2) * 1024 + (n + 1) * 512],
                                                start=(j == 0), stop=(j == DC - 1))
                        st.inc(ins)
                    kn_done = st.n

                    @blk.vector
                    def _(v, ch=ch, kn_done=kn_done):
                        v.wait_ge(st.h, kn_done)
                        ins = v.reduce_max(kmax2[:, 1:2], kn2_ps[:], axis=AX.X)
                        sv.inc(ins)
                        v.wait_ge(sv.h, sv.n)
                        ins = v.tensor_tensor(kmax2[:, 0:1], kmax2[:, 0:1], kmax2[:, 1:2], ALU.max)
                        sv.inc(ins)
                        ch_kmax_done[ch] = sv.n

                    v_mm_done = {}
                    v_evac_actual = {}
                    v_wr_done = {}

                    @blk.tensor
                    def _(te, ch=ch, base=base):
                        ss_base = ss.n
                        for tl in range(8):
                            for n in range(2):
                                idx = tl * 2 + n
                                if idx >= 2:
                                    te.wait_ge(ss.h, ss_base + idx - 1)
                                ins = None
                                for dd in range(DC):
                                    ins = te.matmul(v_ps[idx % 2][:],
                                                    xnT_kv[:, base + dd * 1024 + tl * 128:base + dd * 1024 + tl * 128 + 128],
                                                    wv_bf[:, dd * D + n * 512:dd * D + (n + 1) * 512],
                                                    start=(dd == 0), stop=(dd == DC - 1))
                                st.inc(ins)
                                v_mm_done[idx] = st.n

                    @blk.scalar
                    def _(sc, ch=ch):
                        for tl in range(8):
                            for n in range(2):
                                idx = tl * 2 + n
                                sc.wait_ge(st.h, v_mm_done[idx])
                                if tl >= 2:
                                    sc.wait_ge(wr.h, v_wr_done.get(tl - 2, 0))
                                ins = sc.activation(
                                    vtb[:, (tl % 2) * 1024 + n * 512:(tl % 2) * 1024 + (n + 1) * 512],
                                    v_ps[idx % 2][:], AF.Copy)
                                ss.inc(ins)
                            row = (ch * 8 + tl) * 128
                            for p in range(2):
                                with sc.If_eq(_par, p):
                                    sc.dma_start(
                                        out=v_sh[p * (S // 2) + row:p * (S // 2) + row + 128, :],
                                        in_=vtb[:, (tl % 2) * 1024:(tl % 2 + 1) * 1024],
                                    ).then_inc(wr.h, 16)
                            wr.n += 16
                            v_wr_done[tl] = wr.n
                    ch_mm_done[ch] = st.n
                wr_scr_all = wr.n
                gen_xn_final = max(prev_xn_sv[0], prev_xn_sv[1])

                @blk.vector
                def _(v):
                    v.wait_ge(sv.h, ch_kmax_done[NCHH - 1])
                    ins = v.memset(km_sb8[:], 0.0)
                    sv.inc(ins)
                    v.wait_ge(sv.h, sv.n)
                    ins = v.tensor_copy(km_sb8[:, 0:1], kmax2[:, 0:1])
                    sv.inc(ins)
                sv_km8 = sv.n

                @blk.scalar
                def _(sc):
                    sc.wait_ge(sv.h, sv_km8)
                    ins = sc.dma_start(out=km_in[:], in_=km_sb8[:])
                    wr.inc(ins, 16)
                wr_all_kv = wr.n

                @blk.gpsimd
                def _(g):
                    g.wait_ge(wr.h, wr_all_kv)
                    g.collective_compute(
                        "AllReduce", ALU.max,
                        replica_groups=[[2 * i, 2 * i + 1] for i in range(NCORES // 2)],
                        ins=[km_in.ap().opt()], outs=[km_out.ap().opt()],
                    ).then_inc(cc.h, 1)
                    cc.n += 1
                cc_bar = cc.n

                blk.__exit__(None, None, None)

        a1 = ExitStack()
        with a1:
            kt_tile = a1.enter_context(nc.sbuf_tensor("kt_tile", [128, 2 * DC * 1024], BF16))
            v_tile = a1.enter_context(nc.sbuf_tensor("v_tile", [128, 2 * 8 * 1024], BF16))
            expS = a1.enter_context(nc.sbuf_tensor("expS", [128, 2 * 8 * 512], BF16))
            x_res = a1.enter_context(nc.sbuf_tensor("x_res", [128, 2 * D], F32))
            rzc = a1.enter_context(nc.sbuf_tensor("rzc", [128, TT], F32))
            kmg8 = a1.enter_context(nc.sbuf_tensor("kmg8", [1, 8], F32))

            p5 = ExitStack()
            with p5:
                s_ps = [p5.enter_context(nc.psum_tensor(f"sps{i}", [128, 512], F32)) for i in range(2)]
                o_ps = [p5.enter_context(nc.psum_tensor(f"ops{i}", [128, 1026], F32)) for i in range(2)]

                blk = nc.Block("attn"); blk.__enter__()

                @blk.vector
                def _(v):
                    v.wait_ge(sv.h, gen_xn_final)
                    ins = None
                    for gt in range(TT):
                        ins = v.memset(O_sb[:, gt * 1026:(gt + 1) * 1026], 0.0)
                    sv.inc(ins)
                sv_omemset = sv.n

                kv_loaded = {}
                kv_mm_done = {}

                def _kv_load_body(kvt):
                    @blk.sync
                    def _(sync, kvt=kvt):
                        sync.wait_ge(st.h, kv_mm_done.get(kvt - 2, 0))
                        if kvt == 0:
                            sync.wait_ge(wr.h, wr_all_kv)
                        half = kvt % 2
                        for p in range(2):
                            phys = (kvt + 8 * p) % NCH
                            with sync.If_eq(_parsp, p):
                                sync.dma_start(
                                    out=kt_tile[:, half * DC * 1024:(half + 1) * DC * 1024].rearrange("p (j k) -> p j k", j=DC),
                                    in_=kT_sh[:, :, phys * 1024:(phys + 1) * 1024].rearrange("j p k -> p j k"),
                                ).then_inc(ld.h, 16)
                                sync.dma_start(
                                    out=v_tile[:, half * 8 * 1024:(half + 1) * 8 * 1024].rearrange("p (g d) -> p g d", g=8),
                                    in_=v_sh[phys * 1024:(phys + 1) * 1024, :].rearrange("(g p) d -> p g d", p=128),
                                ).then_inc(ld.h, 16)
                        ld.n += 32
                    kv_loaded[kvt] = ld.n

                _kv_load_body(0)
                _kv_load_body(1)

                @blk.sync
                def _(sync):
                    sync.wait_ge(cc.h, cc_bar)
                    sync.dma_start(out=kmg8[:], in_=km_out[:]).then_inc(ld.h, 16)
                    ld.n += 16
                ld_kmg = ld.n

                @blk.scalar
                def _(sc):
                    sc.wait_ge(ld.h, ld_kmg)
                    sc.wait_ge(sv.h, sv_qmax)
                    ins = sc.activation(bstat[:, 0:1], qmax2[:, 0:1], AF.Ln)
                    ss.inc(ins)
                    sc.wait_ge(ss.h, ss.n)
                    ins = sc.activation(bstat[:, 1:2], kmg8[:, 0:1], AF.Ln)
                    ss.inc(ins)
                ss_lnk = ss.n

                @blk.vector
                def _(v):
                    v.wait_ge(ss.h, ss_lnk)
                    ins = v.tensor_tensor(bstat[:, 2:3], bstat[:, 0:1], bstat[:, 1:2], ALU.add)
                    sv.inc(ins)
                sv_lnsum = sv.n

                @blk.scalar
                def _(sc):
                    sc.wait_ge(sv.h, sv_lnsum)
                    ins = sc.activation(bstat[:, 3:4], bstat[:, 2:3], AF.Exp, bias=nl32_sb[:], scale=0.5)
                    ss.inc(ins)
                ss_B = ss.n

                @blk.vector
                def _(v):
                    v.wait_ge(ss.h, ss_B)
                    ins = v.tensor_scalar(bstat[:, 4:5], bstat[:, 3:4], -1.0, None, ALU.mult)
                    sv.inc(ins)
                    v.wait_ge(sv.h, sv.n)
                    ins = v.tensor_copy(nb_bf[:], bstat[0:1, 4:5])
                    sv.inc(ins)
                sv_nbbf = sv.n

                @blk.tensor
                def _(te):
                    te.wait_ge(sv.h, sv_nbbf)
                    ins = te.matmul(s_ps[0][:, 0:1], ones1_bf[:], nb_bf[:], start=True, stop=True)
                    st.inc(ins)
                st_nb = st.n

                @blk.vector
                def _(v):
                    v.wait_ge(st.h, st_nb)
                    ins = v.tensor_copy(negB[:], s_ps[0][:, 0:1])
                    sv.inc(ins)
                sv_negB = sv.n

                exp_done = {}
                pv_done = {}
                flush_done = {}

                for kvt in range(NCH):
                    if kvt >= 2:
                        _kv_load_body(kvt)

                    for qs in range(4):
                        ebase = (qs % 2) * 8 * 512
                        smm_done = {}

                        @blk.tensor
                        def _(te, kvt=kvt, qs=qs, smm_done=smm_done):
                            te.wait_ge(ld.h, kv_loaded[kvt])
                            half = (kvt % 2) * DC * 1024
                            ss_base = ss.n
                            for kg in range(8):
                                if kg >= 2:
                                    te.wait_ge(ss.h, ss_base + kg - 1)
                                else:
                                    key = (kvt, qs - 1, 6 + kg) if qs >= 1 else (kvt - 1, 3, 6 + kg)
                                    if key in exp_done:
                                        te.wait_ge(ss.h, exp_done[key])
                                ins = None
                                for dd in range(DC):
                                    ins = te.matmul(s_ps[kg % 2][:],
                                                    kt_tile[:, half + dd * 1024 + kg * 128:half + dd * 1024 + (kg + 1) * 128],
                                                    QT[:, dd * T + qs * 512:dd * T + (qs + 1) * 512],
                                                    start=(dd == 0), stop=(dd == DC - 1))
                                st.inc(ins)
                                smm_done[kg] = st.n

                        @blk.scalar
                        def _(sc, kvt=kvt, qs=qs, ebase=ebase, smm_done=smm_done):
                            if kvt == 0 and qs == 0:
                                sc.wait_ge(sv.h, sv_negB)
                            for kg in range(8):
                                sc.wait_ge(st.h, smm_done[kg])
                                ins = sc.activation(expS[:, ebase + kg * 512:ebase + (kg + 1) * 512],
                                                    s_ps[kg % 2][:], AF.Exp,
                                                    bias=negB[:], scale=SCALE)
                                ss.inc(ins)
                                exp_done[(kvt, qs, kg)] = ss.n

                        @blk.tensor
                        def _(te, kvt=kvt, qs=qs, ebase=ebase):
                            te.wait_ge(ss.h, exp_done[(kvt, qs, 7)])
                            half = (kvt % 2) * 8 * 1024
                            sv_base = sv.n
                            for qt in range(4):
                                if qt >= 2:
                                    te.wait_ge(sv.h, sv_base + qt - 1)
                                else:
                                    key = (kvt, qs - 1, 2 + qt) if qs >= 1 else (kvt - 1, 3, 2 + qt)
                                    if key in flush_done:
                                        te.wait_ge(sv.h, flush_done[key])
                                ins = None
                                for kg in range(8):
                                    lhs = expS[:, ebase + kg * 512 + qt * 128:ebase + kg * 512 + (qt + 1) * 128]
                                    te.matmul(o_ps[qt % 2][:, 0:512], lhs,
                                              v_tile[:, half + kg * 1024:half + kg * 1024 + 512],
                                              start=(kg == 0), stop=(kg == 7))
                                    te.matmul(o_ps[qt % 2][:, 512:1024], lhs,
                                              v_tile[:, half + kg * 1024 + 512:half + kg * 1024 + 1024],
                                              start=(kg == 0), stop=(kg == 7))
                                    ins = te.matmul(o_ps[qt % 2][:, 1024:1026], lhs,
                                                    ones2_bf[:],
                                                    start=(kg == 0), stop=(kg == 7))
                                st.inc(ins)
                                pv_done[(kvt, qs, qt)] = st.n

                        @blk.vector
                        def _(v, kvt=kvt, qs=qs):
                            if kvt == 0:
                                v.wait_ge(sv.h, sv_omemset)
                            for qt in range(4):
                                v.wait_ge(st.h, pv_done[(kvt, qs, qt)])
                                gt = qs * 4 + qt
                                ins = v.tensor_tensor(O_sb[:, gt * 1026:(gt + 1) * 1026],
                                                      O_sb[:, gt * 1026:(gt + 1) * 1026],
                                                      o_ps[qt % 2][:], ALU.add)
                                sv.inc(ins)
                                flush_done[(kvt, qs, qt)] = sv.n
                    kv_mm_done[kvt] = st.n

                last_flush = sv.n

                @blk.vector
                def _(v):
                    v.wait_ge(sv.h, last_flush)
                    ins = None
                    for gt in range(TT):
                        ins = v.reciprocal(rzc[:, gt:gt + 1], O_sb[:, gt * 1026 + 1024:gt * 1026 + 1025])
                    sv.inc(ins)
                sv_rz = sv.n

                xres_done = {}
                fin_done = {}
                for gt in range(TT):
                    @blk.sync
                    def _(sync, gt=gt):
                        sync.wait_ge(sv.h, fin_done.get(gt - 2, 0))
                        sync.dma_start(out=x_res[:, (gt % 2) * D:(gt % 2 + 1) * D],
                                       in_=x_own[gt * 128:(gt + 1) * 128, :]).then_inc(ld.h, 16)
                        ld.n += 16
                    xres_done[gt] = ld.n

                    @blk.vector
                    def _(v, gt=gt):
                        v.wait_ge(ld.h, xres_done[gt])
                        v.wait_ge(sv.h, sv_rz)
                        ins = v.scalar_tensor_tensor(O_sb[:, gt * 1026:gt * 1026 + 1024],
                                                     O_sb[:, gt * 1026:gt * 1026 + 1024],
                                                     rzc[:, gt:gt + 1],
                                                     x_res[:, (gt % 2) * D:(gt % 2 + 1) * D],
                                                     ALU.mult, ALU.add)
                        sv.inc(ins)
                        fin_done[gt] = sv.n
                blk.__exit__(None, None, None)

        qt_guard.__exit__(None, None, None)

        C = 640
        SC5 = C // 128
        FQ = 4
        f1 = ExitStack()
        with f1:
            iota_f = f1.enter_context(nc.sbuf_tensor("iota_f", [128, C], F32))
            comb_bf = f1.enter_context(nc.sbuf_tensor("comb_bf", [128, TT * E], BF16))
            posb = f1.enter_context(nc.sbuf_tensor("posb", [128, TT * E], F32))
            comb_sel = f1.enter_context(nc.sbuf_tensor("comb_sel", [128, 8], F32))
            b1_sb = f1.enter_context(nc.sbuf_tensor("b1_sb", [128, E * (FF // 128)], F32))
            b2e = f1.enter_context(nc.sbuf_tensor("b2e", [1, D], BF16))
            b1_rm = f1.enter_context(nc.sbuf_tensor("b1_rm", [128, 256], F32))

            def E_view(tt):
                return u20[:, tt * C:(tt + 1) * C]

            def hq_view(fc):
                return u20[:, fc * C:(fc + 1) * C]

            def ybf_view(st5):
                return u20[:, 8 * C + st5 * 1024:8 * C + (st5 + 1) * 1024]

            blk = nc.Block("ffn")
            blk.__enter__()

            ppre = ExitStack()
            with ppre:
                b1t_ps = ppre.enter_context(nc.psum_tensor("b1t_ps", [128, 256], F32))
                trp8 = ppre.enter_context(nc.psum_tensor("trp8", [8, 128], F32))
                trpb = ppre.enter_context(nc.psum_tensor("trpb", [128, 8], F32))
                selT = ppre.enter_context(nc.sbuf_tensor("selT", [8, T], F32))
                ones8 = ppre.enter_context(nc.sbuf_tensor("ones8", [8, T], F32))
                mask8 = ppre.enter_context(nc.sbuf_tensor("mask8", [8, T], F32))
                posT8 = ppre.enter_context(nc.sbuf_tensor("posT8", [8, T], F32))

                @blk.sync
                def _(sync):
                    sync.dma_start(
                        out=b1_rm[:].rearrange("q (m p) -> q m p", m=2),
                        in_=b1[:].rearrange("e f -> (e f)").rearrange("(m q p) -> q m p", m=2, p=128),
                    ).then_inc(ld.h, 16)
                    ld.n += 16
                ld_ffn_pre = ld.n

                @blk.gpsimd
                def _(g):
                    ins = g.iota(iota_f[:], [[1, C]], channel_multiplier=0,
                                 allow_small_or_imprecise_dtypes=True)
                    sg.inc(ins)
                sg_iota = sg.n

                @blk.tensor
                def _(te):
                    te.wait_ge(ld.h, ld_ffn_pre)
                    te.transpose(b1t_ps[:, 0:128], b1_rm[:, 0:128], id_32[:])
                    ins = te.transpose(b1t_ps[:, 128:256], b1_rm[:, 128:256], id_32[:])
                    st.inc(ins)
                st_b1t = st.n

                @blk.vector
                def _(v):
                    v.wait_ge(st.h, st_b1t)
                    v.tensor_copy(b1_sb[:], b1t_ps[:])
                    v.memset(ones8[:], 1.0)
                    ins = v.tensor_copy(comb_bf[:], comb[:])
                    sv.inc(ins)
                sv_sel = sv.n

                seltr_done = {}
                for tt in range(TT):
                    @blk.tensor
                    def _(te, tt=tt):
                        te.wait_ge(sv.h, sv_sel)
                        te.wait_ge(sv.h, seltr_done.get(tt - 1, 0))
                        ins = te.transpose(trp8[:], sel_full[:, tt * E:(tt + 1) * E], id_32[:])
                        st.inc(ins)
                    st_tr = st.n

                    @blk.vector
                    def _(v, tt=tt, st_tr=st_tr):
                        v.wait_ge(st.h, st_tr)
                        ins = v.tensor_copy(selT[:, tt * 128:(tt + 1) * 128], trp8[:])
                        sv.inc(ins)
                        seltr_done[tt] = sv.n

                @blk.vector
                def _(v):
                    v.wait_ge(sv.h, seltr_done[TT - 1])
                    v.tensor_tensor_scan(posT8[:], ones8[:], selT[:], 0.0, ALU.mult, ALU.add).then_inc(sv.h, 1)
                    sv.n += 1
                    v.wait_ge(sv.h, sv.n)
                    v.tensor_tensor(posT8[:], posT8[:], selT[:], ALU.subtract).then_inc(sv.h, 1)
                    sv.n += 1
                    v.wait_ge(sv.h, sv.n)
                    v.tensor_scalar(mask8[:], selT[:], -1.0e9, 1.0e9, ALU.mult, ALU.add).then_inc(sv.h, 1)
                    sv.n += 1
                    v.wait_ge(sv.h, sv.n)
                    ins = v.tensor_tensor(posT8[:], posT8[:], mask8[:], ALU.add)
                    sv.inc(ins)
                sv_posT = sv.n

                postr_done = {}
                for tt in range(TT):
                    @blk.tensor
                    def _(te, tt=tt):
                        te.wait_ge(sv.h, sv_posT)
                        te.wait_ge(sv.h, postr_done.get(tt - 1, 0))
                        ins = te.transpose(trpb[:], posT8[:, tt * 128:(tt + 1) * 128], id_32[0:8, 0:8])
                        st.inc(ins)
                    st_tr = st.n

                    @blk.vector
                    def _(v, tt=tt, st_tr=st_tr):
                        v.wait_ge(st.h, st_tr)
                        ins = v.tensor_copy(posb[:, tt * E:(tt + 1) * E], trpb[:])
                        sv.inc(ins)
                        postr_done[tt] = sv.n
            sv_ffn_pre = sv.n

            xn_nat = f1.enter_context(nc.sbuf_tensor("xn_nat", [128, TT * D], BF16))
            u20 = f1.enter_context(nc.sbuf_tensor("u20", [128, TT * C], BF16))
            ET_sb = f1.enter_context(nc.sbuf_tensor("ET_sb", [128, SC5 * T], BF16))
            xn_selT = f1.enter_context(nc.sbuf_tensor("xn_selT", [128, DC * C], BF16))
            y_f32 = f1.enter_context(nc.sbuf_tensor("y_f32", [128, SC5 * 1024], F32))
            w1c = f1.enter_context(nc.sbuf_tensor("w1c", [128, 4 * DC * 128], BF16))
            w2q = f1.enter_context(nc.sbuf_tensor("w2q", [128, 8 * 1024], BF16))

            @blk.sync
            def _(sync):
                sync.wait_ge(sv.h, sv_ffn_pre)
                sync.wait_ge(wr.h, wr_xnT)
                sync.dma_start(
                    out=xn_nat[:].rearrange("p (g d) -> p g d", g=TT),
                    in_=xn_scr[:].rearrange("g p d -> p g d"),
                ).then_inc(ld.h, 16)
                ld.n += 16
            ld_xn_nat = ld.n

            p6 = ExitStack()
            with p6:
                hy_ps = [p6.enter_context(nc.psum_tensor(f"hy{i}", [128, 1024], F32)) for i in range(2)]
                trE_ps = [p6.enter_context(nc.psum_tensor(f"trE_ps{i}", [128, C], BF16)) for i in range(2)]
                csel_ps = p6.enter_context(nc.psum_tensor("csel_ps", [128, 8], F32))

                scatter_done = {0: 0}
                oflush_all = {0: 0}
                last_y_done = [0]
                prev_h_done = [0]
                prev_scmm = [None]

                for e in range(E):
                    if prev_scmm[0] is not None:
                        @blk.vector
                        def _(v, e=e, scmm_prev=prev_scmm[0]):
                            for tt in range(TT):
                                v.wait_ge(st.h, scmm_prev[tt])
                                ins = v.tensor_tensor(O_sb[:, tt * 1026:tt * 1026 + 1024],
                                                      O_sb[:, tt * 1026:tt * 1026 + 1024],
                                                      hy_ps[tt % 2][:], ALU.add)
                                sv.inc(ins)
                    oflush_all[e] = sv.n

                    @blk.vector
                    def _(v, e=e):
                        if e == 0:
                            v.wait_ge(sv.h, sv_ffn_pre)
                            v.wait_ge(sg.h, sg_iota)
                            v.wait_ge(ld.h, ld_ffn_pre)
                        for tt in range(8):
                            v.tensor_scalar(E_view(tt), iota_f[:],
                                            posb[:, tt * E + e:tt * E + e + 1],
                                            None, ALU.is_equal)

                    @blk.vector
                    def _(v, e=e):
                        v.wait_ge(st.h, scatter_done[e])
                        ins = None
                        for tt in range(8, TT):
                            ins = v.tensor_scalar(E_view(tt), iota_f[:],
                                                  posb[:, tt * E + e:tt * E + e + 1],
                                                  None, ALU.is_equal)
                        sv.inc(ins)
                    E_done = sv.n

                    gth_mm = {}

                    @blk.tensor
                    def _(te, e=e, E_done=E_done, gth_mm=gth_mm):
                        te.wait_ge(sv.h, E_done)
                        te.wait_ge(sv.h, oflush_all[e])
                        if e == 0:
                            te.wait_ge(ld.h, ld_xn_nat)
                        ss_base = ss.n
                        for j in range(DC):
                            if j >= 2:
                                te.wait_ge(ss.h, ss_base + j - 1)
                            g_ps = hy_ps[j % 2]
                            ins = None
                            for tt in range(TT):
                                te.matmul(g_ps[:, 0:512],
                                          xn_nat[:, tt * D + j * 128:tt * D + (j + 1) * 128],
                                          u20[:, tt * C:tt * C + 512],
                                          start=(tt == 0), stop=(tt == TT - 1))
                                ins = te.matmul(g_ps[:, 512:C],
                                                xn_nat[:, tt * D + j * 128:tt * D + (j + 1) * 128],
                                                u20[:, tt * C + 512:(tt + 1) * C],
                                                start=(tt == 0), stop=(tt == TT - 1))
                            st.inc(ins)
                            gth_mm[j] = st.n
                        for st5 in range(SC5):
                            ins = None
                            for tt in range(TT):
                                ins = te.matmul(csel_ps[:, st5:st5 + 1],
                                                u20[:, tt * C + st5 * 128:tt * C + (st5 + 1) * 128],
                                                comb_bf[:, tt * E + e:tt * E + e + 1],
                                                start=(tt == 0), stop=(tt == TT - 1))
                            st.inc(ins)
                        gth_mm["csel"] = st.n
                        svb = sv.n
                        for tt in range(TT):
                            if tt >= 2:
                                te.wait_ge(sv.h, svb + tt)
                            ins = None
                            for sc5 in range(SC5):
                                ins = te.transpose(trE_ps[tt % 2][:, sc5 * 128:(sc5 + 1) * 128],
                                                   u20[:, tt * C + sc5 * 128:tt * C + (sc5 + 1) * 128],
                                                   id_bf[:])
                            st.inc(ins)
                            gth_mm[("et", tt)] = st.n

                    @blk.scalar
                    def _(sc, e=e, gth_mm=gth_mm):
                        if e == 0:
                            sc.wait_ge(sv.h, sv_sel)
                        for j in range(DC):
                            sc.wait_ge(st.h, gth_mm[j])
                            ins = sc.activation(xn_selT[:, j * C:(j + 1) * C],
                                                hy_ps[j % 2][:, 0:C], AF.Copy)
                            ss.inc(ins)
                    gth_evac = ss.n

                    @blk.vector
                    def _(v, e=e, gth_mm=gth_mm):
                        v.wait_ge(st.h, gth_mm["csel"])
                        ins = v.tensor_copy(comb_sel[:, 0:SC5], csel_ps[:, 0:SC5])
                        sv.inc(ins)
                        for tt in range(TT):
                            v.wait_ge(st.h, gth_mm[("et", tt)])
                            dst = ET_sb[:].rearrange("p (s t) -> p s t", s=SC5)[:, :, tt * 128:(tt + 1) * 128]
                            src = trE_ps[tt % 2][:].rearrange("p (s c) -> p s c", c=128)
                            ins = v.tensor_copy(dst, src)
                            sv.inc(ins)
                    et_evac = sv.n
                    csel_evac_sv = et_evac - TT

                    yflush = {}
                    for fb in range(FQ):
                        hmm_d = {}
                        w1_ld = {}

                        @blk.gpsimd
                        def _(g, e=e, fb=fb, w1_ld=w1_ld):
                            if e == 0 and fb == 0:
                                g.wait_ge(sv.h, sv_ffn_pre)
                            g.wait_ge(st.h, prev_h_done[0])
                            st_base_g = st.n
                            for fc in range(4):
                                ins = g.dma_start(
                                    out=w1c[:, (fc % 4) * DC * 128:((fc % 4) + 1) * DC * 128].rearrange("p (j f) -> p j f", j=DC),
                                    in_=w1[e, :, fb * 1024 + fc * 128:fb * 1024 + (fc + 1) * 128].rearrange("(j p) f -> p j f", p=128),
                                )
                                ldg.inc(ins, 16)
                                w1_ld[fc] = ldg.n
                            if fb == 0:
                                ins = g.dma_start(out=b2e[:], in_=b2[e][None, :])
                                ldg.inc(ins, 16)
                            g.wait_ge(st.h, last_y_done[0])
                            ins = g.dma_start(
                                out=w2q[:].rearrange("p (c d) -> p c d", c=8),
                                in_=w2[e, fb * 1024:(fb + 1) * 1024, :].rearrange("(c p) d -> p c d", p=128),
                            )
                            ldg.inc(ins, 16)
                            for fc in range(4, 8):
                                g.wait_ge(st.h, st_base_g + fc - 3)
                                ins = g.dma_start(
                                    out=w1c[:, (fc % 4) * DC * 128:((fc % 4) + 1) * DC * 128].rearrange("p (j f) -> p j f", j=DC),
                                    in_=w1[e, :, fb * 1024 + fc * 128:fb * 1024 + (fc + 1) * 128].rearrange("(j p) f -> p j f", p=128),
                                )
                                ldg.inc(ins, 16)
                                w1_ld[fc] = ldg.n
                        w2_ld = ldg.n

                        @blk.tensor
                        def _(te, e=e, fb=fb, hmm_d=hmm_d, w1_ld=w1_ld,
                              gth_evac=gth_evac, et_evac=et_evac, yflush=yflush):
                            ss_base = ss.n
                            for fc in range(8):
                                te.wait_ge(ldg.h, w1_ld[fc])
                                if fb == 0 and fc == 0:
                                    te.wait_ge(ss.h, gth_evac)
                                    te.wait_ge(sv.h, et_evac)
                                if fb > 0 and fc < 2:
                                    te.wait_ge(sv.h, yflush[(fb - 1, 4 - fc)])
                                if fc >= 2:
                                    te.wait_ge(ss.h, ss_base + fc - 1)
                                h_ps = hy_ps[fc % 2]
                                ins = None
                                for n2 in range(2):
                                    lo = n2 * 512
                                    hi = min(C, lo + 512)
                                    for dd in range(DC):
                                        ins = te.matmul(h_ps[:, lo:hi],
                                                        w1c[:, (fc % 4) * DC * 128 + dd * 128:(fc % 4) * DC * 128 + (dd + 1) * 128],
                                                        xn_selT[:, dd * C + lo:dd * C + hi],
                                                        start=(dd == 0), stop=(dd == DC - 1))
                                st.inc(ins)
                                hmm_d[fc] = st.n
                        prev_h_done[0] = st.n

                        @blk.scalar
                        def _(sc, e=e, fb=fb, hmm_d=hmm_d):
                            for fc in range(8):
                                sc.wait_ge(st.h, hmm_d[fc])
                                bcol = e * (FF // 128) + fb * 8 + fc
                                ins = sc.activation(hq_view(fc), hy_ps[fc % 2][:, 0:C],
                                                    AF.Gelu, bias=b1_sb[:, bcol:bcol + 1], scale=1.0)
                                ss.inc(ins)
                        gelu_all = ss.n

                        ymm_d = {}

                        @blk.tensor
                        def _(te, e=e, fb=fb, gelu_all=gelu_all, ymm_d=ymm_d,
                              w2_ld=w2_ld, yflush=yflush):
                            te.wait_ge(ss.h, gelu_all)
                            te.wait_ge(ldg.h, w2_ld)
                            sv_base = sv.n
                            for st5 in range(SC5):
                                if st5 >= 2:
                                    te.wait_ge(sv.h, sv_base + st5 - 1)
                                elif fb > 0:
                                    te.wait_ge(sv.h, yflush.get((fb - 1, 4 - st5), 0))
                                y_ps = hy_ps[st5 % 2]
                                first0 = first1 = True
                                if fb == 0:
                                    te.matmul(y_ps[:, 0:512], ones1_bf[:], b2e[:, 0:512],
                                              start=True, stop=False)
                                    te.matmul(y_ps[:, 512:1024], ones1_bf[:], b2e[:, 512:1024],
                                              start=True, stop=False)
                                    first0 = first1 = False
                                ins = None
                                for fc in range(8):
                                    lhs = u20[:, fc * C + st5 * 128:fc * C + (st5 + 1) * 128]
                                    te.matmul(y_ps[:, 0:512], lhs, w2q[:, fc * 1024:fc * 1024 + 512],
                                              start=first0, stop=(fc == 7))
                                    ins = te.matmul(y_ps[:, 512:1024], lhs,
                                                    w2q[:, fc * 1024 + 512:fc * 1024 + 1024],
                                                    start=first1, stop=(fc == 7))
                                    first0 = first1 = False
                                st.inc(ins)
                                ymm_d[st5] = st.n
                        last_y_done[0] = st.n

                        @blk.vector
                        def _(v, e=e, fb=fb, ymm_d=ymm_d, yflush=yflush):
                            for st5 in range(SC5):
                                v.wait_ge(st.h, ymm_d[st5])
                                if fb == 0:
                                    ins = v.tensor_copy(y_f32[:, st5 * 1024:(st5 + 1) * 1024],
                                                        hy_ps[st5 % 2][:])
                                else:
                                    ins = v.tensor_tensor(y_f32[:, st5 * 1024:(st5 + 1) * 1024],
                                                          y_f32[:, st5 * 1024:(st5 + 1) * 1024],
                                                          hy_ps[st5 % 2][:], ALU.add)
                                sv.inc(ins)
                                yflush[(fb, st5)] = sv.n

                    @blk.vector
                    def _(v, e=e, csel_evac_sv=csel_evac_sv):
                        v.wait_ge(sv.h, csel_evac_sv)
                        ins = None
                        for st5 in range(SC5):
                            ins = v.tensor_scalar(ybf_view(st5), y_f32[:, st5 * 1024:(st5 + 1) * 1024],
                                                  comb_sel[:, st5:st5 + 1], None, ALU.mult)
                        sv.inc(ins)
                    ybf_done = sv.n

                    scmm_d = {}

                    @blk.tensor
                    def _(te, e=e, ybf_done=ybf_done, scmm_d=scmm_d):
                        te.wait_ge(sv.h, ybf_done)
                        sv_base = sv.n
                        for tt in range(TT):
                            if tt >= 2:
                                te.wait_ge(sv.h, sv_base + tt - 1)
                            s_ps2 = hy_ps[tt % 2]
                            ins = None
                            for sc5 in range(SC5):
                                lhs = ET_sb[:, sc5 * T + tt * 128:sc5 * T + (tt + 1) * 128]
                                te.matmul(s_ps2[:, 0:512], lhs,
                                          u20[:, 8 * C + sc5 * 1024:8 * C + sc5 * 1024 + 512],
                                          start=(sc5 == 0), stop=(sc5 == SC5 - 1))
                                ins = te.matmul(s_ps2[:, 512:1024], lhs,
                                                u20[:, 8 * C + sc5 * 1024 + 512:8 * C + (sc5 + 1) * 1024],
                                                start=(sc5 == 0), stop=(sc5 == SC5 - 1))
                            st.inc(ins)
                            scmm_d[tt] = st.n
                        scatter_done[e + 1] = st.n

                    prev_scmm[0] = scmm_d

                @blk.vector
                def _(v, scmm_prev=prev_scmm[0]):
                    for tt in range(TT):
                        v.wait_ge(st.h, scmm_prev[tt])
                        ins = v.tensor_tensor(O_sb[:, tt * 1026:tt * 1026 + 1024],
                                              O_sb[:, tt * 1026:tt * 1026 + 1024],
                                              hy_ps[tt % 2][:], ALU.add)
                        sv.inc(ins)
                ffn_all = sv.n

                @blk.sync
                def _(sync):
                    sync.wait_ge(sv.h, ffn_all)
                    sync.dma_start(
                        out=out[:].rearrange("(t p) d -> p t d", p=128),
                        in_=O_sb[:].rearrange("p (t c) -> p t c", c=1026)[:, :, 0:1024],
                    ).then_inc(so.h, 16)
                    so.n += 16
                    sync.wait_ge(so.h, so.n)

                @blk.gpsimd
                def _(g):
                    g.wait_ge(so.h, so.n)

                blk.__exit__(None, None, None)

    return nc


_NC_CACHE = [None]


def kernel(L0_attn_x, L0_attn_W_q, L0_attn_W_k, L0_attn_W_v,
           L0_ffn_gate_w, L0_ffn_w1, L0_ffn_b1, L0_ffn_w2, L0_ffn_b2):
    x = np.ascontiguousarray(np.asarray(L0_attn_x, dtype=np.float32).reshape(S, D))
    wq_np = np.ascontiguousarray(np.asarray(L0_attn_W_q, dtype=np.float32))
    wk_np = np.ascontiguousarray(np.asarray(L0_attn_W_k, dtype=np.float32))
    wv_np = np.ascontiguousarray(np.asarray(L0_attn_W_v, dtype=np.float32))
    gw_np = np.ascontiguousarray(np.asarray(L0_ffn_gate_w, dtype=np.float32))
    w1_np = np.ascontiguousarray(np.asarray(L0_ffn_w1, dtype=np.float32))
    b1_np = np.ascontiguousarray(np.asarray(L0_ffn_b1, dtype=np.float32))
    w2_np = np.ascontiguousarray(np.asarray(L0_ffn_w2, dtype=np.float32))
    b2_np = np.ascontiguousarray(np.asarray(L0_ffn_b2, dtype=np.float32))

    if _NC_CACHE[0] is None:
        _NC_CACHE[0] = build_nc()
    nc = _NC_CACHE[0]

    in_maps = []
    for c in range(NCORES):
        in_maps.append({
            "x_own": np.ascontiguousarray(x[c * T:(c + 1) * T]),
            "x_kv": np.ascontiguousarray(x[(c % 2) * (S // 2):((c % 2) + 1) * (S // 2)]),
            "wq": wq_np, "wk": wk_np, "wv": wv_np,
            "gate_w": gw_np,
            "w1": w1_np, "b1": b1_np, "w2": w2_np, "b2": b2_np,
        })
    res = run_bass_kernel_spmd(nc, in_maps, core_ids=list(range(NCORES)))
    shards = [res.results[c]["out"] for c in range(NCORES)]
    return np.concatenate(shards, axis=0).reshape(1, S, D).astype(np.float32, copy=False)
